# version 08
# baseline (speedup 1.0000x reference)
import numpy as np
import ml_dtypes
from contextlib import ExitStack

import concourse.bass as bass
import concourse.mybir as mybir
from concourse.bass_utils import run_bass_kernel_spmd

F32 = mybir.dt.float32
BF16 = mybir.dt.bfloat16
I32 = mybir.dt.int32
AF = mybir.ActivationFunctionType
ALU = mybir.AluOpType
AX = mybir.AxisListType

T, D = 4096, 2048
H8, DK, DV, M = 8, 64, 128, 64
C = 128
CH = 512
NCH = T // CH
NA = CH // C
TOWN = 2048
EPS = 1e-5
NEXP = 8
HID = 1024
PHASES = 6
DBG_DUMP = False
P0_SERIAL = False


class Sem:
    def __init__(self, ctx, nc, name):
        self.h = ctx.enter_context(nc.semaphore(name))
        self.n = 0

    def inc(self, inst, k=1):
        inst.then_inc(self.h, k)
        self.n += k
        return self.n

    @property
    def mark(self):
        return (self, self.n)


class Waiter:

    def __init__(self):
        self.seen = {}

    def __call__(self, eng, mark):
        if mark is None:
            return
        sem, val = mark
        if val <= 0:
            return
        if self.seen.get(id(sem), -1) >= val:
            return
        eng.wait_ge(sem.h, val)
        self.seen[id(sem)] = val


def build_nc():
    nc = bass.Bass("TRN2", target_bir_lowering=False, debug=False,
                   detect_race_conditions=False)

    def din(name, shape, dtype=F32):
        return nc.dram_tensor(name, list(shape), dtype, kind="ExternalInput").ap()

    x_ap = din("x", (T, D))
    xown_ap = din("x_own", (TOWN, D))
    wq_ap = din("wq", (D, 512))
    wk_ap = din("wk", (D, 512))
    ws_ap = din("ws", (D, 512))
    wv_ap = din("wv", (D, 1024))
    wg_ap = din("wg", (D, 1024))
    wo_ap = din("wo", (D, D))
    gnw_ap = din("gnw", (128, 1))
    gwT_ap = din("gatewT", (D, 8))
    w1T_ap = din("w1T", (NEXP, D, HID))
    w3T_ap = din("w3T", (NEXP, D, HID))
    w2T_ap = din("w2T", (NEXP, HID, D))
    ws1T_ap = din("ws1T", (D, HID))
    ws3T_ap = din("ws3T", (D, HID))
    ws2T_ap = din("ws2T", (HID, D))
    mut_ap = din("mask_ut", (128, 128), BF16)
    ident_ap = din("ident", (128, 128), BF16)
    id32_ap = din("ident32", (128, 128))
    onec_ap = din("ones_col", (128, 1), BF16)
    oner_ap = din("ones_row", (1, 128))
    cst_ap = din("constv", (128, 2))
    hgv_ap = din("hgv", (1, 1), I32)

    out_ap = nc.dram_tensor("out_half", [TOWN, D], F32, kind="ExternalOutput").ap()

    dbk = "ExternalOutput" if DBG_DUMP else "Internal"
    xlnT_d = nc.dram_tensor("xlnT_d", [128, 16, T], BF16, kind=dbk).ap()
    sgg_d = nc.dram_tensor("sgg_d", [T, 1024], BF16, kind=dbk).ap()
    o3s_d = nc.dram_tensor("o3s_d", [4, 1024, 1024], BF16, kind="Internal").ap()
    o3m_d = nc.dram_tensor("o3m_d", [4, 2048, 1024], BF16, kind="Internal").ap()
    x1_d = nc.dram_tensor("x1_d", [TOWN, D], F32, kind=dbk).ap()
    x1lnT_d = nc.dram_tensor("x1lnT_d", [128, 16, TOWN], BF16, kind=dbk).ap()
    wsel_d = nc.dram_tensor("wsel_d", [8, TOWN], F32, kind=dbk).ap()

    ctx = ExitStack()
    with ctx:
        s_pe = Sem(ctx, nc, "s_pe")
        s_act = Sem(ctx, nc, "s_act")
        s_dve = Sem(ctx, nc, "s_dve")
        s_pre = Sem(ctx, nc, "s_pre")
        sx = [Sem(ctx, nc, f"sx{i}") for i in range(2)]
        sxt = [Sem(ctx, nc, f"sxt{i}") for i in range(2)]
        sxl1 = [Sem(ctx, nc, f"sxl1_{i}") for i in range(2)]
        ssg = [Sem(ctx, nc, f"ssg{i}") for i in range(2)]
        sxl2 = [Sem(ctx, nc, f"sxl2_{i}") for i in range(2)]
        ssgi = Sem(ctx, nc, "ssgi")
        so3a = [Sem(ctx, nc, f"so3a{i}") for i in range(NA)]
        scc = [Sem(ctx, nc, f"scc{i}") for i in range(4)]
        so3f = [Sem(ctx, nc, f"so3f{i}") for i in range(2)]
        sxo = [Sem(ctx, nc, f"sxo{i}") for i in range(2)]
        sx1 = [Sem(ctx, nc, f"sx1_{i}") for i in range(2)]
        sxt1 = [Sem(ctx, nc, f"sxt1_{i}") for i in range(2)]
        sxlb = [Sem(ctx, nc, f"sxlb{i}") for i in range(2)]
        swse = Sem(ctx, nc, "swse")
        sxw = Sem(ctx, nc, "sxw")
        scro = Sem(ctx, nc, "scro")
        sxf = Sem(ctx, nc, "sxf")
        sout = Sem(ctx, nc, "sout")
        s_wat = Sem(ctx, nc, "s_wat")
        s_wg = Sem(ctx, nc, "s_wg")
        s_wo = Sem(ctx, nc, "s_wo")
        s_wgt = Sem(ctx, nc, "s_wgt")
        s_w13 = Sem(ctx, nc, "s_w13")
        s_w2f = Sem(ctx, nc, "s_w2f")

        ALL_SEMS = [s_pe, s_act, s_dve, s_pre, *sx, *sxt, *sxl1, *ssg, *sxl2,
                    ssgi, *so3a, *scc, *so3f, *sxo, *sx1, *sxt1, *sxlb, swse,
                    sxw, scro, sxf, sout, s_wat, s_wg, s_wo, s_wgt, s_w13, s_w2f]

        Wpe, Wact, Wdve, Wsp, Wpl = Waiter(), Waiter(), Waiter(), Waiter(), Waiter()

        mut = ctx.enter_context(nc.sbuf_tensor("mut_s", [128, 128], BF16))
        ident = ctx.enter_context(nc.sbuf_tensor("ident_s", [128, 128], BF16))
        id32 = ctx.enter_context(nc.sbuf_tensor("id32_s", [128, 128], F32))
        onec = ctx.enter_context(nc.sbuf_tensor("onec_s", [128, 1], BF16))
        oner = ctx.enter_context(nc.sbuf_tensor("oner_s", [1, 128], F32))
        gnw = ctx.enter_context(nc.sbuf_tensor("gnw_s", [128, 1], F32))
        cst = ctx.enter_context(nc.sbuf_tensor("cst_s", [128, 2], F32))
        hgv = ctx.enter_context(nc.sbuf_tensor("hgv_s", [1, 1], I32))

        block = ctx.enter_context(nc.Block())

        def on(engname, fn):
            getattr(block, engname)(fn)

        mk = {}

        def phase_barrier():
            cur = [s.mark for s in ALL_SEMS]

            def mkf(wtr):
                def f(eng):
                    for m in cur:
                        wtr(eng, m)
                return f
            on("tensor", mkf(Wpe))
            on("scalar", mkf(Wact))
            on("vector", mkf(Wdve))
            on("sync", mkf(Wsp))
            on("gpsimd", mkf(Wpl))

        def _pre(sp):
            for dst, src in ((mut, mut_ap), (ident, ident_ap), (id32, id32_ap),
                             (onec, onec_ap), (oner, oner_ap), (gnw, gnw_ap),
                             (cst, cst_ap), (hgv, hgv_ap)):
                s_pre.inc(sp.dma_start(out=dst[:], in_=src), 16)
        on("sync", _pre)
        pre_mark = s_pre.mark

        for _en, _W in (("tensor", Wpe), ("scalar", Wact), ("vector", Wdve),
                        ("gpsimd", Wpl)):
            def _pw(eng, _W=_W):
                _W(eng, pre_mark)
            on(_en, _pw)

        aw = ExitStack()
        wqs = aw.enter_context(nc.sbuf_tensor("wqs", [128, 16, 512], BF16))
        wks = aw.enter_context(nc.sbuf_tensor("wks", [128, 16, 512], BF16))
        wss = aw.enter_context(nc.sbuf_tensor("wss", [128, 16, 512], BF16))
        wvs = aw.enter_context(nc.sbuf_tensor("wvs", [128, 16, 1024], BF16))

        def p2_w(g):
            s_wat.inc(g.dma_start(out=wqs[:], in_=wq_ap.rearrange("(dc p) n -> p dc n", p=128)), 16)
            s_wat.inc(g.dma_start(out=wks[:], in_=wk_ap.rearrange("(dc p) n -> p dc n", p=128)), 16)
            s_wat.inc(g.dma_start(out=wss[:], in_=ws_ap.rearrange("(dc p) n -> p dc n", p=128)), 16)
            s_wat.inc(g.dma_start(out=wvs[:], in_=wv_ap.rearrange("(dc p) n -> p dc n", p=128)), 16)
        on("gpsimd", p2_w)
        wat_mark = s_wat.mark

        with ExitStack() as p0:
            xb = [p0.enter_context(nc.sbuf_tensor(f"p0x{i}", [128, NA, D], F32)) for i in range(2)]
            xl = [p0.enter_context(nc.sbuf_tensor("p0l0", [128, NA, D], BF16))] * 2
            xtT = [p0.enter_context(nc.sbuf_tensor(f"p0t{i}", [128, 16, CH], BF16)) for i in range(2)]
            st = p0.enter_context(nc.sbuf_tensor("p0st", [128, 8], F32))
            sqs = p0.enter_context(nc.sbuf_tensor("p0sq", [128, D], F32))
            pt = [p0.enter_context(nc.psum_tensor(f"p0p{i}", [128, 512], BF16)) for i in range(2)]

            for ch in range(NCH):
                if P0_SERIAL and ch >= 1:
                    cur = [s.mark for s in (s_pe, s_act, s_dve)]

                    def bar(eng, cur=cur, W=None):
                        for m in cur:
                            W(eng, m)
                    on("sync", lambda sp, cur=cur: [Wsp(sp, m) for m in cur])
                    on("vector", lambda v, cur=cur: [Wdve(v, m) for m in cur])
                    on("scalar", lambda s, cur=cur: [Wact(s, m) for m in cur])
                    on("tensor", lambda t, cur=cur: [Wpe(t, m) for m in cur])

                def sp_load(sp, ch=ch):
                    Wsp(sp, pre_mark)
                    if ch >= 2:
                        Wsp(sp, mk[("p0c", (ch - 2) * NA + NA - 1)])
                    s = sx[ch % 2]
                    s.inc(sp.dma_start(
                        out=xb[ch % 2][:],
                        in_=x_ap[ch * CH:(ch + 1) * CH, :].rearrange("(a p) d -> p a d", p=128)), 16)
                    mk[("p0ld", ch)] = s.mark
                on("sync", sp_load)

                for a in range(NA):
                    i = ch * NA + a

                    def dve_s1(v, ch=ch, a=a, i=i):
                        if i == 0:
                            Wdve(v, pre_mark)
                        Wdve(v, mk[("p0ld", ch)])
                        if ch >= 1:
                            Wdve(v, mk[("p0tr", (ch - 1) * NA + NA - 1, 3)])
                        xa = xb[ch % 2][:, a, :]
                        s_dve.inc(v.reduce_sum(st[:, 0:1], xa, axis=AX.X))
                        v.wait_ge(s_dve.h, s_dve.n)
                        s_dve.inc(v.tensor_scalar_mul(st[:, 1:2], st[:, 0:1], 1.0 / D))
                        v.wait_ge(s_dve.h, s_dve.n)
                        s_dve.inc(v.tensor_mul(st[:, 2:3], st[:, 1:2], st[:, 1:2]))
                        mk[("p0a", i)] = s_dve.mark
                    on("vector", dve_s1)

                    def dve_s2(v, ch=ch, a=a, i=i):
                        s_dve.inc(v.tensor_mul(sqs[:], xb[ch % 2][:, a, :], xb[ch % 2][:, a, :]))
                        v.wait_ge(s_dve.h, s_dve.n)
                        s_dve.inc(v.reduce_sum(st[:, 0:1], sqs[:], axis=AX.X))
                        v.wait_ge(s_dve.h, s_dve.n)
                        s_dve.inc(v.scalar_tensor_tensor(st[:, 3:4], st[:, 0:1], 1.0 / D,
                                                         st[:, 2:3], ALU.mult, ALU.subtract))
                        mk[("p0b", i)] = s_dve.mark
                    on("vector", dve_s2)

                    def act_s2(s, i=i):
                        Wact(s, mk[("p0b", i)])
                        s_act.inc(s.activation(st[:, 5:6], st[:, 3:4], AF.Ln, bias=cst[:, 0:1]))
                        s.wait_ge(s_act.h, s_act.n)
                        s_act.inc(s.activation(st[:, 6:7], st[:, 5:6], AF.Exp, scale=-0.5))
                        mk[("p0r", i)] = s_act.mark
                    on("scalar", act_s2)

                    def dve_s3(v, ch=ch, a=a, i=i):
                        Wdve(v, mk[("p0r", i)])
                        s_dve.inc(v.tensor_scalar(xl[ch % 2][:, a, :], xb[ch % 2][:, a, :],
                                                  st[:, 1:2], st[:, 6:7],
                                                  ALU.subtract, ALU.mult))
                        mk[("p0c", i)] = s_dve.mark
                    on("vector", dve_s3)

                    for k in range(4):
                        gidx = i * 4 + k

                        def pe_tr(t, ch=ch, a=a, i=i, k=k, gidx=gidx):
                            if gidx == 0:
                                Wpe(t, pre_mark)
                            Wpe(t, mk[("p0c", i)])
                            if gidx >= 2:
                                Wpe(t, mk[("p0ev", gidx - 2)])
                            p = pt[gidx % 2]
                            for j in range(4):
                                dc = k * 4 + j
                                inst = t.transpose(p[:, j * 128:(j + 1) * 128],
                                                   xl[ch % 2][:, a, dc * 128:(dc + 1) * 128],
                                                   ident[:])
                            s_pe.inc(inst)
                            mk[("p0tr", i, k)] = s_pe.mark
                        on("tensor", pe_tr)

                        def act_ev(s, ch=ch, a=a, i=i, k=k, gidx=gidx):
                            Wact(s, mk[("p0tr", i, k)])
                            if ch >= 2:
                                Wact(s, mk[("p0st", ch - 2)])
                            s_act.inc(s.copy(
                                xtT[ch % 2][:, k * 4:(k + 1) * 4, a * 128:(a + 1) * 128],
                                pt[gidx % 2][:].rearrange("p (j t) -> p j t", j=4)))
                            mk[("p0ev", gidx)] = s_act.mark
                        on("scalar", act_ev)

                def sp_store(sp, ch=ch):
                    Wsp(sp, mk[("p0ev", (ch * NA + NA - 1) * 4 + 3)])
                    s = sxt[ch % 2]
                    s.inc(sp.dma_start(
                        out=xlnT_d[:, :, ch * CH:(ch + 1) * CH], in_=xtT[ch % 2][:]), 16)
                    mk[("p0st", ch)] = s.mark
                on("sync", sp_store)

        if PHASES < 2:
            def sp_end0(sp):
                for s in ALL_SEMS[3:]:
                    Wsp(sp, s.mark)
            on("sync", sp_end0)
            aw.close()
            return nc

        phase_barrier()
        with ExitStack() as p1:
            wgt = p1.enter_context(nc.sbuf_tensor("p1wg", [128, 16, 1024], BF16))
            xlt = [p1.enter_context(nc.sbuf_tensor(f"p1x{i}", [128, 16, CH], BF16)) for i in range(2)]
            sgo = [p1.enter_context(nc.sbuf_tensor(f"p1o{i}", [128, NA, 1024], BF16)) for i in range(2)]
            sgt = p1.enter_context(nc.sbuf_tensor("p1s", [128, 512], F32))
            pg = [p1.enter_context(nc.psum_tensor(f"p1p{i}", [128, 512], F32)) for i in range(2)]

            def p1_w(g):
                s_wg.inc(g.dma_start(out=wgt[:],
                                     in_=wg_ap.rearrange("(dc p) n -> p dc n", p=128)), 16)
            on("gpsimd", p1_w)
            wg_mark = s_wg.mark

            for ch in range(NCH):
                def sp_l(sp, ch=ch):
                    Wsp(sp, mk[("p0st", ch)])
                    if ch >= 2:
                        Wsp(sp, mk[("p1mm", (ch - 2) * NA * 2 + NA * 2 - 1)])
                    s = sxl1[ch % 2]
                    s.inc(sp.dma_start(out=xlt[ch % 2][:],
                                       in_=xlnT_d[:, :, ch * CH:(ch + 1) * CH]), 16)
                    mk[("p1ld", ch)] = s.mark
                on("sync", sp_l)

                for a in range(NA):
                    for vb in range(2):
                        gi = (ch * NA + a) * 2 + vb

                        def pe_g(t, ch=ch, a=a, vb=vb, gi=gi):
                            Wpe(t, mk[("p1ld", ch)])
                            Wpe(t, wg_mark)
                            if gi >= 2:
                                Wpe(t, mk[("p1ev", gi - 2)])
                            p = pg[gi % 2]
                            for dc in range(16):
                                inst = t.matmul(p[:],
                                                xlt[ch % 2][:, dc, a * 128:(a + 1) * 128],
                                                wgt[:, dc, vb * 512:(vb + 1) * 512],
                                                start=(dc == 0), stop=(dc == 15))
                            s_pe.inc(inst)
                            mk[("p1mm", gi)] = s_pe.mark
                        on("tensor", pe_g)

                        def act_g(s, gi=gi):
                            Wact(s, mk[("p1mm", gi)])
                            if gi >= 1:
                                Wact(s, mk[("p1ev", gi - 1)])
                            s_act.inc(s.activation(sgt[:], pg[gi % 2][:], AF.Sigmoid))
                            mk[("p1sg", gi)] = s_act.mark
                        on("scalar", act_g)

                        def dve_g(v, ch=ch, a=a, vb=vb, gi=gi):
                            Wdve(v, mk[("p1sg", gi)])
                            if ch >= 2:
                                Wdve(v, mk[("p1st", ch - 2)])
                            s_dve.inc(v.tensor_mul(
                                sgo[ch % 2][:, a, vb * 512:(vb + 1) * 512],
                                pg[gi % 2][:], sgt[:]))
                            mk[("p1ev", gi)] = s_dve.mark
                        on("vector", dve_g)

                def sp_s(sp, ch=ch):
                    Wsp(sp, mk[("p1ev", (ch * NA + NA - 1) * 2 + 1)])
                    s = ssg[ch % 2]
                    s.inc(sp.dma_start(
                        out=sgg_d[ch * CH:(ch + 1) * CH, :].rearrange("(a p) n -> p a n", p=128),
                        in_=sgo[ch % 2][:]), 16)
                    mk[("p1st", ch)] = s.mark
                on("sync", sp_s)

        if PHASES < 3:
            def sp_end1(sp):
                for s in ALL_SEMS[3:]:
                    Wsp(sp, s.mark)
            on("sync", sp_end1)
            aw.close()
            return nc

        phase_barrier()
        with ExitStack() as p2:
            xlt2 = [p2.enter_context(nc.sbuf_tensor(f"p2x{i}", [128, 16, CH], BF16)) for i in range(2)]
            sgi = [p2.enter_context(nc.sbuf_tensor("p2g0", [128, NA, 1024], BF16))] * 2
            qsT = [p2.enter_context(nc.sbuf_tensor(f"qsT{h}", [64, CH], BF16)) for h in range(H8)]
            kT = [p2.enter_context(nc.sbuf_tensor(f"kT{h}", [64, CH], BF16)) for h in range(H8)]
            sT = [p2.enter_context(nc.sbuf_tensor(f"sT{h}", [64, CH], BF16)) for h in range(H8)]
            k_t = p2.enter_context(nc.sbuf_tensor("k_t", [128, NA, 512], BF16))
            v_t = p2.enter_context(nc.sbuf_tensor("v_t", [128, NA, 1024], BF16))
            hkT = [p2.enter_context(nc.sbuf_tensor(f"hkT{h}", [64, 64], F32)) for h in range(H8)]
            hv = [p2.enter_context(nc.sbuf_tensor(f"hv{h}", [64, 128], F32)) for h in range(H8)]
            nz = p2.enter_context(nc.sbuf_tensor("nz", [64, H8], F32))
            cs = p2.enter_context(nc.sbuf_tensor("cs", [64, H8], F32))
            lnt = p2.enter_context(nc.sbuf_tensor("lnt", [64, 2 * H8], F32))
            hkTb = [[p2.enter_context(nc.sbuf_tensor(f"hkTb{h}_{p}", [64, 64], BF16)) for p in range(2)] for h in range(H8)]
            hvb = [[p2.enter_context(nc.sbuf_tensor(f"hvb{h}_{p}", [64, 128], BF16)) for p in range(2)] for h in range(H8)]
            hkb = [p2.enter_context(nc.sbuf_tensor(f"hkb{h}", [64, 64], BF16)) for h in range(H8)]
            Emt = [p2.enter_context(nc.sbuf_tensor(f"Emt{h}", [64, C], BF16)) for h in range(H8)]
            Etm = [p2.enter_context(nc.sbuf_tensor(f"Etm{h}", [C, 64], BF16)) for h in range(H8)]
            ATm = [p2.enter_context(nc.sbuf_tensor(f"ATm{h}", [C, C], BF16)) for h in range(H8)]
            BTm = [p2.enter_context(nc.sbuf_tensor(f"BTm{h}", [C, C], BF16)) for h in range(H8)]
            rD1 = [p2.enter_context(nc.sbuf_tensor(f"rD1{h}", [C, 64], F32)) for h in range(H8)]
            okb = [p2.enter_context(nc.sbuf_tensor(f"ok{h}", [C, 64], F32)) for h in range(H8)]
            eob = [p2.enter_context(nc.sbuf_tensor(f"eo{h}", [C, 64], F32)) for h in range(H8)]
            pqb = [p2.enter_context(nc.sbuf_tensor(f"pq{h}", [C, 64], BF16)) for h in range(H8)]
            pqT = [p2.enter_context(nc.sbuf_tensor(f"pqT{h}", [64, C], BF16)) for h in range(H8)]
            ovs = [p2.enter_context(nc.sbuf_tensor(f"ovs{h}", [C, 128], F32)) for h in range(H8)]
            sq2 = [p2.enter_context(nc.sbuf_tensor(f"sq2{h}", [C, 128], F32)) for h in range(H8)]
            scl = p2.enter_context(nc.sbuf_tensor("scl", [128, 4 * H8], F32))
            o3T = [p2.enter_context(nc.sbuf_tensor(f"o3T{i}", [128, H8, C], BF16)) for i in range(NA)]
            bP = [p2.enter_context(nc.psum_tensor(f"bP{i}", [128, 512], F32)) for i in range(2)]
            bA = [p2.enter_context(nc.psum_tensor(f"bA{i}", [128, 512], F32)) for i in range(2)]
            bU = p2.enter_context(nc.psum_tensor("bU", [128, 512], F32))
            bV = p2.enter_context(nc.psum_tensor("bV", [128, 512], F32))
            bT1 = p2.enter_context(nc.psum_tensor("bT1", [128, 1024], BF16))
            bT2 = p2.enter_context(nc.psum_tensor("bT2", [128, 1024], BF16))

            prj = [None] * NCH
            prjv = [None] * NCH

            def emit_p2_sp(ch):
                def f(sp, ch=ch):
                    Wsp(sp, mk[("p0st", ch)])
                    Wsp(sp, mk[("p1st", ch)])
                    if ch >= 1:
                        Wsp(sp, mk[((ch - 1) * NA + NA - 1, "dve7")])
                    if ch >= 2:
                        Wsp(sp, prj[ch - 2])
                    s = sxl2[ch % 2]
                    s.inc(sp.dma_start(out=xlt2[ch % 2][:],
                                       in_=xlnT_d[:, :, ch * CH:(ch + 1) * CH]), 16)
                    mk[("p2ld", ch)] = s.mark
                    ssgi.inc(sp.dma_start(
                        out=sgi[ch % 2][:],
                        in_=sgg_d[ch * CH:(ch + 1) * CH, :].rearrange("(a p) n -> p a n", p=128)), 16)
                    mk[("p2sg", ch)] = ssgi.mark
                on("sync", f)

            def emit_proj(ch):
                blocks = ([("q", bi) for bi in range(4)] + [("k", bi) for bi in range(4)]
                          + [("s", bi) for bi in range(4)] + [("kt", a) for a in range(NA)]
                          + [("vt", (a, vb)) for a in range(NA) for vb in range(2)])

                for i, (kind, arg) in enumerate(blocks):
                    def pe_f(t, ch=ch, i=i, kind=kind, arg=arg):
                        if i == 0:
                            Wpe(t, mk[("p2ld", ch)])
                            Wpe(t, wat_mark)
                            if ch >= 1:
                                Wpe(t, mk[((ch - 1) * NA + NA - 1, "act1")])
                                Wpe(t, mk[((ch - 1) * NA + NA - 1, "dve7")])
                        p = bP[i % 2]
                        if i >= 2:
                            Wpe(t, mk[("prjev", ch, i - 2)])
                        if kind in ("q", "k", "s"):
                            wsb = {"q": wqs, "k": wks, "s": wss}[kind]
                            for dc in range(16):
                                inst = t.matmul(p[:], wsb[:, dc, arg * 128:(arg + 1) * 128],
                                                xlt2[ch % 2][:, dc, :],
                                                start=(dc == 0), stop=(dc == 15))
                        elif kind == "kt":
                            for dc in range(16):
                                inst = t.matmul(p[:], xlt2[ch % 2][:, dc, arg * 128:(arg + 1) * 128],
                                                wks[:, dc, :], start=(dc == 0), stop=(dc == 15))
                        else:
                            a, vb = arg
                            for dc in range(16):
                                inst = t.matmul(p[:], xlt2[ch % 2][:, dc, a * 128:(a + 1) * 128],
                                                wvs[:, dc, vb * 512:(vb + 1) * 512],
                                                start=(dc == 0), stop=(dc == 15))
                        s_pe.inc(inst)
                        mk[("prjmm", ch, i)] = s_pe.mark
                        if i == len(blocks) - 1:
                            prj[ch] = s_pe.mark
                    on("tensor", pe_f)

                    if kind in ("q", "k"):
                        def act_f(s, ch=ch, i=i, kind=kind, arg=arg):
                            Wact(s, mk[("prjmm", ch, i)])
                            p = bP[i % 2]
                            dsts = qsT if kind == "q" else kT
                            sc = (DK ** -0.5) if kind == "q" else 1.0
                            h0 = arg * 2
                            s.activation(dsts[h0][:], p[0:64, :], AF.Copy, scale=sc)
                            s_act.inc(s.activation(dsts[h0 + 1][:], p[64:128, :], AF.Copy, scale=sc))
                            mk[("prjev", ch, i)] = s_act.mark
                        on("scalar", act_f)
                    else:
                        def dve_f(v, ch=ch, i=i, kind=kind, arg=arg):
                            Wdve(v, mk[("prjmm", ch, i)])
                            p = bP[i % 2]
                            if kind == "s":
                                h0 = arg * 2
                                v.tensor_scalar(sT[h0][:], p[0:64, :], -32.0, 32.0, ALU.max, ALU.min)
                                inst = v.tensor_scalar(sT[h0 + 1][:], p[64:128, :], -32.0, 32.0,
                                                       ALU.max, ALU.min)
                            elif kind == "kt":
                                inst = v.tensor_copy(k_t[:, arg, :], p[:])
                            else:
                                a, vb = arg
                                inst = v.tensor_copy(v_t[:, a, vb * 512:(vb + 1) * 512], p[:])
                            s_dve.inc(inst)
                            mk[("prjev", ch, i)] = s_dve.mark
                            if i == len(blocks) - 1:
                                prjv[ch] = s_dve.mark
                        on("vector", dve_f)

            def scan_chunk(ch, a):
                g = ch * NA + a
                par = g % 2
                first = (g == 0)
                tsl = slice(a * C, (a + 1) * C)

                def act1(s):
                    Wact(s, prjv[ch])
                    Wact(s, mk.get((g - 1, "dve2B")))
                    Wact(s, mk.get((g - 1, "pe5")))
                    for h in range(H8):
                        if first:
                            inst = s.activation(Emt[h][:], sT[h][:, tsl], AF.Exp)
                        else:
                            inst = s.activation(Emt[h][:], sT[h][:, tsl], AF.Exp,
                                                bias=nz[:, h:h + 1])
                        s_act.inc(inst)
                    mk[(g, "act1")] = s_act.mark
                on("scalar", act1)

                def pe1(t):
                    Wpe(t, mk[("prjev", ch, 7)])
                    Wpe(t, mk.get((g - 1, "act6")))
                    for h in range(H8):
                        p = bA[h // 4]
                        inst = t.matmul(p[:, (h % 4) * 128:(h % 4) * 128 + 128],
                                        kT[h][:, tsl], qsT[h][:, tsl], start=True, stop=True)
                        s_pe.inc(inst)
                    mk[(g, "pe1a")] = s_pe.mark
                    Wpe(t, mk[(g, "act1")])
                    Wpe(t, mk.get((g - 1, "act2")))
                    if not first:
                        Wpe(t, mk.get((g - 1, "dve2B")))
                    for h in range(H8):
                        inst = t.transpose(bT1[:, h * 64:(h + 1) * 64], Emt[h][:],
                                           ident[0:64, 0:64])
                        if not first:
                            inst = t.transpose(bT1[0:64, 512 + h * 64:512 + (h + 1) * 64],
                                               hkTb[h][(g - 1) % 2][:], ident[0:64, 0:64])
                        s_pe.inc(inst)
                    mk[(g, "pe1b")] = s_pe.mark
                on("tensor", pe1)

                def act2(s):
                    Wact(s, mk[(g, "pe1b")])
                    Wact(s, mk.get((g - 1, "pe3")))
                    for h in range(H8):
                        inst = s.copy(Etm[h][:], bT1[:, h * 64:(h + 1) * 64])
                        if not first:
                            inst = s.copy(hkb[h][:], bT1[0:64, 512 + h * 64:512 + (h + 1) * 64])
                        s_act.inc(inst)
                    mk[(g, "act2")] = s_act.mark
                on("scalar", act2)

                def dve_cs(v):
                    Wdve(v, mk[(g, "act1")])
                    for h in range(H8):
                        inst = v.reduce_sum(cs[:, h:h + 1], Emt[h][:], axis=AX.X)
                        s_dve.inc(inst)
                    mk[(g, "dvecs")] = s_dve.mark
                on("vector", dve_cs)

                def dve1(v):
                    Wdve(v, mk[(g, "pe1a")])
                    for h in range(H8):
                        p = bA[h // 4]
                        inst = v.tensor_mul(ATm[h][:], p[:, (h % 4) * 128:(h % 4) * 128 + 128],
                                            mut[:])
                        s_dve.inc(inst)
                    mk[(g, "dve1")] = s_dve.mark
                on("vector", dve1)

                for half in range(2):
                    hs = range(half * 4, half * 4 + 4)
                    tag = "A" if half == 0 else "B"

                    def pe2(t, hs=hs, half=half, tag=tag):
                        Wpe(t, mk[(g, "act2")])
                        if half == 0:
                            Wpe(t, mk.get((g - 1, "dve3")))
                            Wpe(t, mk.get((g - 1, "dve2B")))
                        else:
                            Wpe(t, mk[(g, "dve2A")])
                            Wpe(t, mk[(g, "act3A")])
                        for h in hs:
                            r = h % 4
                            t.matmul(bU[0:64, r * 64:(r + 1) * 64], Etm[h][:],
                                     k_t[:, a, h * 64:(h + 1) * 64], start=True, stop=True)
                            t.matmul(bV[0:64, r * 128:(r + 1) * 128], Etm[h][:],
                                     v_t[:, a, h * 128:(h + 1) * 128], start=True, stop=True)
                            inst = t.matmul(bU[:, 256 + r * 64:256 + (r + 1) * 64], mut[:],
                                            Etm[h][:], start=True, stop=True)
                            s_pe.inc(inst)
                        mk[(g, "pe2" + tag)] = s_pe.mark
                    on("tensor", pe2)

                    def act3(s, hs=hs, tag=tag):
                        Wact(s, mk[(g, "pe2" + tag)])
                        Wact(s, mk[(g, "dvecs")])
                        Wact(s, mk.get((g - 1, "dve4")))
                        for h in hs:
                            r = h % 4
                            if first:
                                s.activation(rD1[h][:], bU[:, 256 + r * 64:256 + (r + 1) * 64],
                                             AF.Ln)
                            else:
                                s.activation(rD1[h][:], bU[:, 256 + r * 64:256 + (r + 1) * 64],
                                             AF.Ln, bias=cst[:, 1:2])
                        for h in hs:
                            if first:
                                inst = s.activation(lnt[:, 2 * h:2 * h + 1], cs[:, h:h + 1], AF.Ln)
                            else:
                                inst = s.activation(lnt[:, 2 * h:2 * h + 1], cs[:, h:h + 1], AF.Ln,
                                                    bias=cst[0:64, 1:2])
                        s_act.inc(inst)
                        s.wait_ge(s_act.h, s_act.n)
                        for h in hs:
                            s.activation(rD1[h][:], rD1[h][:], AF.Exp, scale=-1.0)
                        for h in hs:
                            inst = s.activation(lnt[:, 2 * h + 1:2 * h + 2],
                                                lnt[:, 2 * h:2 * h + 1], AF.Exp, scale=-1.0)
                        s_act.inc(inst)
                        mk[(g, "act3" + tag)] = s_act.mark
                    on("scalar", act3)

                    def dve2(v, hs=hs, tag=tag, first=first):
                        Wdve(v, mk[(g, "act3" + tag)])
                        if first:
                            for h in hs:
                                r = h % 4
                                gam = lnt[:, 2 * h + 1:2 * h + 2]
                                v.tensor_scalar_mul(nz[:, h:h + 1], lnt[:, 2 * h:2 * h + 1], -1.0)
                                v.tensor_scalar_mul(hkT[h][:], bU[0:64, r * 64:(r + 1) * 64], gam)
                                inst = v.tensor_scalar_mul(hv[h][:], bV[0:64, r * 128:(r + 1) * 128], gam)
                            s_dve.inc(inst)
                        else:
                            for h in hs:
                                r = h % 4
                                v.tensor_sub(nz[:, h:h + 1], nz[:, h:h + 1], lnt[:, 2 * h:2 * h + 1])
                                v.tensor_add(hkT[h][:], hkT[h][:], bU[0:64, r * 64:(r + 1) * 64])
                                inst = v.tensor_add(hv[h][:], hv[h][:], bV[0:64, r * 128:(r + 1) * 128])
                            s_dve.inc(inst)
                            v.wait_ge(s_dve.h, s_dve.n)
                            for h in hs:
                                gam = lnt[:, 2 * h + 1:2 * h + 2]
                                v.tensor_scalar_mul(hkT[h][:], hkT[h][:], gam)
                                inst = v.tensor_scalar_mul(hv[h][:], hv[h][:], gam)
                            s_dve.inc(inst)
                        v.wait_ge(s_dve.h, s_dve.n)
                        for h in hs:
                            v.tensor_copy(hkTb[h][par][:], hkT[h][:])
                            inst = v.tensor_copy(hvb[h][par][:], hv[h][:])
                        s_dve.inc(inst)
                        mk[(g, "dve2" + tag)] = s_dve.mark
                    on("vector", dve2)

                def pe3(t):
                    Wpe(t, mk[(g, "dve1")])
                    Wpe(t, mk[(g, "dve2B")])
                    Wpe(t, mk[(g, "act3B")])
                    for h in range(H8):
                        if first:
                            inst = t.matmul(bU[:, h * 64:(h + 1) * 64], ATm[h][:], Etm[h][:],
                                            start=True, stop=True)
                        else:
                            t.matmul(bU[:, h * 64:(h + 1) * 64], qsT[h][:, tsl], hkb[h][:],
                                     start=True, stop=False)
                            inst = t.matmul(bU[:, h * 64:(h + 1) * 64], ATm[h][:], Etm[h][:],
                                            start=False, stop=True)
                        s_pe.inc(inst)
                    mk[(g, "pe3")] = s_pe.mark
                on("tensor", pe3)

                def dve3(v):
                    Wdve(v, mk[(g, "pe3")])
                    Wdve(v, mk[(g, "act3B")])
                    for h in range(H8):
                        inst = v.tensor_mul(okb[h][:], bU[:, h * 64:(h + 1) * 64], rD1[h][:])
                        s_dve.inc(inst)
                    mk[(g, "dve3")] = s_dve.mark
                on("vector", dve3)

                def act4(s):
                    Wact(s, mk[(g, "dve3")])
                    for h in range(H8):
                        s_act.inc(s.activation(eob[h][:], okb[h][:], AF.Exp))
                    mk[(g, "act4")] = s_act.mark
                on("scalar", act4)

                def dve4(v):
                    Wdve(v, mk[(g, "act4")])
                    for h in range(H8):
                        s_dve.inc(v.reduce_sum(scl[:, 4 * h:4 * h + 1], eob[h][:], axis=AX.X))
                        v.wait_ge(s_dve.h, s_dve.n)
                        s_dve.inc(v.reciprocal(scl[:, 4 * h + 1:4 * h + 2], scl[:, 4 * h:4 * h + 1]))
                        v.wait_ge(s_dve.h, s_dve.n)
                        inst = v.scalar_tensor_tensor(pqb[h][:], eob[h][:],
                                                      scl[:, 4 * h + 1:4 * h + 2],
                                                      rD1[h][:], ALU.mult, ALU.mult)
                        s_dve.inc(inst)
                    mk[(g, "dve4")] = s_dve.mark
                on("vector", dve4)

                def pe4(t):
                    Wpe(t, mk[(g, "dve4")])
                    Wpe(t, mk.get((g - 1, "act8")))
                    for h in range(H8):
                        inst = t.transpose(bT2[0:64, h * 128:(h + 1) * 128], pqb[h][:], ident[:])
                        s_pe.inc(inst)
                    mk[(g, "pe4")] = s_pe.mark
                on("tensor", pe4)

                def act5(s):
                    Wact(s, mk[(g, "pe4")])
                    for h in range(H8):
                        s_act.inc(s.copy(pqT[h][:], bT2[0:64, h * 128:(h + 1) * 128]))
                    mk[(g, "act5")] = s_act.mark
                on("scalar", act5)

                def pe5(t):
                    Wpe(t, mk[(g, "act5")])
                    Wpe(t, mk[(g, "dve1")])
                    for h in range(H8):
                        p = bA[h // 4]
                        inst = t.matmul(p[:, (h % 4) * 128:(h % 4) * 128 + 128],
                                        Emt[h][:], pqT[h][:], start=True, stop=True)
                        s_pe.inc(inst)
                    mk[(g, "pe5")] = s_pe.mark
                on("tensor", pe5)

                def dve5(v):
                    Wdve(v, mk[(g, "pe5")])
                    for h in range(H8):
                        p = bA[h // 4]
                        inst = v.tensor_mul(BTm[h][:], p[:, (h % 4) * 128:(h % 4) * 128 + 128],
                                            mut[:])
                        s_dve.inc(inst)
                    mk[(g, "dve5")] = s_dve.mark
                on("vector", dve5)

                def pe6(t):
                    Wpe(t, mk[(g, "dve5")])
                    Wpe(t, mk[(g, "dve2B")])
                    for h in range(H8):
                        p = bA[h // 4]
                        reg = p[:, (h % 4) * 128:(h % 4) * 128 + 128]
                        if first:
                            inst = t.matmul(reg, BTm[h][:], v_t[:, a, h * 128:(h + 1) * 128],
                                            start=True, stop=True)
                        else:
                            t.matmul(reg, BTm[h][:], v_t[:, a, h * 128:(h + 1) * 128],
                                     start=True, stop=False)
                            inst = t.matmul(reg, pqT[h][:], hvb[h][(g - 1) % 2][:],
                                            start=False, stop=True)
                        s_pe.inc(inst)
                    mk[(g, "pe6")] = s_pe.mark
                on("tensor", pe6)

                def act6(s):
                    Wact(s, mk[(g, "pe6")])
                    for h in range(H8):
                        p = bA[h // 4]
                        s_act.inc(s.copy(ovs[h][:], p[:, (h % 4) * 128:(h % 4) * 128 + 128]))
                    mk[(g, "act6")] = s_act.mark
                on("scalar", act6)

                def dve6(v):
                    Wdve(v, mk[(g, "act6")])
                    for h in range(H8):
                        inst = v.tensor_mul(sq2[h][:], ovs[h][:], ovs[h][:])
                    s_dve.inc(inst)
                    v.wait_ge(s_dve.h, s_dve.n)
                    for h in range(H8):
                        inst = v.reduce_sum(scl[:, 4 * h + 2:4 * h + 3], sq2[h][:], axis=AX.X)
                    s_dve.inc(inst)
                    mk[(g, "dve6")] = s_dve.mark
                on("vector", dve6)

                def act7(s):
                    Wact(s, mk[(g, "dve6")])
                    for h in range(H8):
                        inst = s.activation(scl[:, 4 * h + 3:4 * h + 4], scl[:, 4 * h + 2:4 * h + 3],
                                            AF.Ln, bias=cst[:, 0:1], scale=1.0 / DV)
                    s_act.inc(inst)
                    s.wait_ge(s_act.h, s_act.n)
                    for h in range(H8):
                        inst = s.activation(scl[:, 4 * h + 3:4 * h + 4],
                                            scl[:, 4 * h + 3:4 * h + 4], AF.Exp, scale=-0.5)
                    s_act.inc(inst)
                    mk[(g, "act7")] = s_act.mark
                on("scalar", act7)

                def dve7(v):
                    Wdve(v, mk[(g, "act7")])
                    Wdve(v, mk[("p2sg", ch)])
                    for h in range(H8):
                        inst = v.scalar_tensor_tensor(sq2[h][:], ovs[h][:],
                                                      scl[:, 4 * h + 3:4 * h + 4],
                                                      sgi[ch % 2][:, a, h * 128:(h + 1) * 128],
                                                      ALU.mult, ALU.mult)
                    s_dve.inc(inst)
                    v.wait_ge(s_dve.h, s_dve.n)
                    for h in range(H8):
                        inst = v.tensor_copy(BTm[h][:], sq2[h][:])
                    s_dve.inc(inst)
                    mk[(g, "dve7")] = s_dve.mark
                on("vector", dve7)

                def pe7(t):
                    Wpe(t, mk[(g, "dve7")])
                    Wpe(t, mk[(g, "act5")])
                    for h in range(H8):
                        inst = t.transpose(bT2[:, h * 128:(h + 1) * 128], BTm[h][:], ident[:])
                        s_pe.inc(inst)
                    mk[(g, "pe7")] = s_pe.mark
                on("tensor", pe7)

                def act8(s):
                    Wact(s, mk[(g, "pe7")])
                    if g >= 4:
                        Wact(s, mk[("p2o3", g - 4)])
                    for h in range(H8):
                        s_act.inc(s.copy(o3T[a][:, h, :], bT2[:, h * 128:(h + 1) * 128]))
                    mk[(g, "act8")] = s_act.mark
                on("scalar", act8)

            for ch in range(NCH):
                emit_p2_sp(ch)
                emit_proj(ch)
                for a in range(NA):
                    scan_chunk(ch, a)

                def sp_o3(sp, ch=ch):
                    half, j = ch // 4, ch % 4
                    for a in range(NA):
                        Wsp(sp, mk[(ch * NA + a, "act8")])
                        s = so3a[a]
                        s.inc(sp.dma_start(
                            out=o3s_d[j, :, half * 512 + a * 128:half * 512 + (a + 1) * 128]
                                .rearrange("(h p) t -> p h t", p=128),
                            in_=o3T[a][:]), 16)
                        mk[("p2o3", ch * NA + a)] = s.mark
                on("sync", sp_o3)

                if ch >= 4:
                    def pl_cc(g_, ch=ch):
                        j = ch % 4
                        for a in range(NA):
                            Wpl(g_, mk[("p2o3", ch * NA + a)])
                            Wpl(g_, mk[("p2o3", j * NA + a)])
                        s = scc[j]
                        s.inc(g_.collective_compute(
                            "AllGather", ALU.bypass,
                            replica_groups=[[0, 1], [2, 3], [4, 5], [6, 7]],
                            ins=[o3s_d[j].opt()],
                            outs=[o3m_d[j].opt()]))
                        mk[("a2a", j)] = s.mark
                    on("gpsimd", pl_cc)

        if PHASES < 4:
            def sp_end2(sp):
                for s in ALL_SEMS[3:]:
                    Wsp(sp, s.mark)
            on("sync", sp_end2)
            aw.close()
            return nc

        aw.close()
        phase_barrier()
        with ExitStack() as pc1:
            wos = pc1.enter_context(nc.sbuf_tensor("wos", [128, 16, 2048], BF16))
            o3f = [pc1.enter_context(nc.sbuf_tensor(f"o3f{i}", [128, 16, 512], BF16)) for i in range(2)]
            xo = [pc1.enter_context(nc.sbuf_tensor(f"xo{i}", [128, D], F32)) for i in range(2)]
            x1b = [pc1.enter_context(nc.sbuf_tensor(f"x1b{i}", [128, D], F32)) for i in range(2)]
            x1l = [pc1.enter_context(nc.sbuf_tensor(f"x1l{i}", [128, D], BF16)) for i in range(2)]
            xtT1 = [pc1.enter_context(nc.sbuf_tensor(f"xtT1{i}", [128, 16, 128], BF16)) for i in range(2)]
            st2 = pc1.enter_context(nc.sbuf_tensor("pc1st", [128, 8], F32))
            sq1 = pc1.enter_context(nc.sbuf_tensor("pc1sq", [128, D], F32))
            pw = [pc1.enter_context(nc.psum_tensor(f"pw{i}", [128, 512], F32)) for i in range(4)]
            ptr = [pc1.enter_context(nc.psum_tensor(f"ptr{i}", [128, 512], BF16)) for i in range(2)]

            def pc1_w(g):
                s_wo.inc(g.dma_start(out=wos[:], in_=wo_ap.rearrange("(hc p) n -> p hc n", p=128)), 16)
            on("gpsimd", pc1_w)
            wo_mark = s_wo.mark

            def pc1_gnw(v):
                Wdve(v, wo_mark)
                Wdve(v, pre_mark)
                for hc in range(16):
                    s_dve.inc(v.tensor_scalar_mul(wos[:, hc, :], wos[:, hc, :], gnw[:]))
            on("vector", pc1_gnw)
            wognw_mark = s_dve.mark

            for j in range(4):
                def sp_o3f(sp, j=j):
                    Wsp(sp, mk[("a2a", j)])
                    Wsp(sp, pre_mark)
                    if j >= 2:
                        Wsp(sp, mk[("pc1mm", (j - 2) * 4 + 3, 3)])
                    with sp.register(f"hgo{j}") as hgr:
                        sp.reg_load(hgr, hgv[0:1, 0:1])
                        off = sp.snap(hgr, min_val=0, max_val=1)
                        s = so3f[j % 2]
                        s.inc(sp.dma_start(
                            out=o3f[j % 2][:],
                            in_=o3m_d[j].rearrange("(hc p) t -> p hc t", p=128)[:, :, bass.ts(off, 512)]), 16)
                    mk[("pc1o3", j)] = so3f[j % 2].mark
                on("sync", sp_o3f)

                for tc4 in range(4):
                    tc = j * 4 + tc4

                    def sp_x(sp, tc=tc):
                        if tc >= 2:
                            Wsp(sp, mk[("pc1x1", tc - 2)])
                        s = sxo[tc % 2]
                        s.inc(sp.dma_start(out=xo[tc % 2][:],
                                           in_=xown_ap[tc * 128:(tc + 1) * 128, :]), 16)
                        mk[("pc1ldx", tc)] = s.mark
                    on("sync", sp_x)

                    def pe_wo(t, tc=tc, j=j, tc4=tc4):
                        Wpe(t, mk[("pc1o3", j)])
                        Wpe(t, wognw_mark)
                        for dg in range(4):
                            p = pw[dg]
                            if tc >= 1:
                                Wpe(t, mk[("pc1ev", tc - 1, dg)])
                            for hc in range(16):
                                inst = t.matmul(p[:], o3f[j % 2][:, hc, tc4 * 128:(tc4 + 1) * 128],
                                                wos[:, hc, dg * 512:(dg + 1) * 512],
                                                start=(hc == 0), stop=(hc == 15))
                            s_pe.inc(inst)
                            mk[("pc1mm", tc, dg)] = s_pe.mark
                    on("tensor", pe_wo)

                    def dve_x1(v, tc=tc):
                        Wdve(v, mk[("pc1ldx", tc)])
                        if tc >= 2:
                            Wdve(v, mk[("pc1x1st", tc - 2)])
                        for dg in range(4):
                            Wdve(v, mk[("pc1mm", tc, dg)])
                            inst = v.tensor_add(x1b[tc % 2][:, dg * 512:(dg + 1) * 512],
                                                xo[tc % 2][:, dg * 512:(dg + 1) * 512],
                                                pw[dg][:])
                            s_dve.inc(inst)
                            mk[("pc1ev", tc, dg)] = s_dve.mark
                        mk[("pc1x1", tc)] = s_dve.mark
                    on("vector", dve_x1)

                    def sp_x1(sp, tc=tc):
                        Wsp(sp, mk[("pc1x1", tc)])
                        s = sx1[tc % 2]
                        s.inc(sp.dma_start(out=x1_d[tc * 128:(tc + 1) * 128, :],
                                           in_=x1b[tc % 2][:]), 16)
                        mk[("pc1x1st", tc)] = s.mark
                    on("sync", sp_x1)

                    def dve_lna(v, tc=tc):
                        s_dve.inc(v.reduce_sum(st2[:, 0:1], x1b[tc % 2][:], axis=AX.X))
                        v.wait_ge(s_dve.h, s_dve.n)
                        s_dve.inc(v.tensor_scalar_mul(st2[:, 1:2], st2[:, 0:1], 1.0 / D))
                        v.wait_ge(s_dve.h, s_dve.n)
                        s_dve.inc(v.tensor_mul(st2[:, 2:3], st2[:, 1:2], st2[:, 1:2]))
                        mk[("pc1lna", tc)] = s_dve.mark
                    on("vector", dve_lna)

                    def dve_lnb(v, tc=tc):
                        s_dve.inc(v.tensor_mul(sq1[:], x1b[tc % 2][:], x1b[tc % 2][:]))
                        v.wait_ge(s_dve.h, s_dve.n)
                        s_dve.inc(v.reduce_sum(st2[:, 0:1], sq1[:], axis=AX.X))
                        v.wait_ge(s_dve.h, s_dve.n)
                        s_dve.inc(v.scalar_tensor_tensor(st2[:, 3:4], st2[:, 0:1], 1.0 / D,
                                                         st2[:, 2:3], ALU.mult, ALU.subtract))
                        mk[("pc1lnb", tc)] = s_dve.mark
                    on("vector", dve_lnb)

                    def act_lnb(s, tc=tc):
                        Wact(s, mk[("pc1lnb", tc)])
                        s_act.inc(s.activation(st2[:, 5:6], st2[:, 3:4], AF.Ln, bias=cst[:, 0:1]))
                        s.wait_ge(s_act.h, s_act.n)
                        s_act.inc(s.activation(st2[:, 6:7], st2[:, 5:6], AF.Exp, scale=-0.5))
                        mk[("pc1rstd", tc)] = s_act.mark
                    on("scalar", act_lnb)

                    def dve_lnc(v, tc=tc):
                        Wdve(v, mk[("pc1rstd", tc)])
                        if tc >= 2:
                            Wdve(v, mk[("pc1tr", tc - 2, 3)])
                        s_dve.inc(v.tensor_scalar(x1l[tc % 2][:], x1b[tc % 2][:],
                                                  st2[:, 1:2], st2[:, 6:7],
                                                  ALU.subtract, ALU.mult))
                        mk[("pc1ln", tc)] = s_dve.mark
                    on("vector", dve_lnc)

                    for k in range(4):
                        def pe_t1(t, tc=tc, k=k):
                            Wpe(t, mk[("pc1ln", tc)])
                            gi = tc * 4 + k
                            if gi >= 2:
                                Wpe(t, mk[("pc1te", gi - 2)])
                            p = ptr[gi % 2]
                            for jj in range(4):
                                dc = k * 4 + jj
                                inst = t.transpose(p[:, jj * 128:(jj + 1) * 128],
                                                   x1l[tc % 2][:, dc * 128:(dc + 1) * 128],
                                                   ident[:])
                            s_pe.inc(inst)
                            mk[("pc1tr", tc, k)] = s_pe.mark
                        on("tensor", pe_t1)

                        def act_e1(s, tc=tc, k=k):
                            gi = tc * 4 + k
                            Wact(s, mk[("pc1tr", tc, k)])
                            if tc >= 2:
                                Wact(s, mk[("pc1xst", tc - 2)])
                            s_act.inc(s.copy(xtT1[tc % 2][:, k * 4:(k + 1) * 4, :],
                                             ptr[gi % 2][:].rearrange("p (j t) -> p j t", j=4)))
                            mk[("pc1te", gi)] = s_act.mark
                        on("scalar", act_e1)

                    def sp_xt1(sp, tc=tc):
                        Wsp(sp, mk[("pc1te", tc * 4 + 3)])
                        s = sxt1[tc % 2]
                        s.inc(sp.dma_start(out=x1lnT_d[:, :, tc * 128:(tc + 1) * 128],
                                           in_=xtT1[tc % 2][:]), 16)
                        mk[("pc1xst", tc)] = s.mark
                    on("sync", sp_xt1)

        if PHASES < 5:
            def sp_end3(sp):
                for s in ALL_SEMS[3:]:
                    Wsp(sp, s.mark)
            on("sync", sp_end3)
            return nc

        phase_barrier()
        with ExitStack() as pcb:
            gw = pcb.enter_context(nc.sbuf_tensor("gw", [128, 16, 8], BF16))
            xl2 = [pcb.enter_context(nc.sbuf_tensor(f"xl2{i}", [128, 16, 128], BF16)) for i in range(2)]
            sc2 = pcb.enter_context(nc.sbuf_tensor("sc2", [128, 40], F32))
            wrow = pcb.enter_context(nc.sbuf_tensor("wrow", [8, 128], F32))
            pl2 = pcb.enter_context(nc.psum_tensor("pl2", [128, 8], F32))
            pt2 = pcb.enter_context(nc.psum_tensor("pt2", [8, 128], F32))

            def pcb_w(g):
                s_wgt.inc(g.dma_start(out=gw[:], in_=gwT_ap.rearrange("(dc p) e -> p dc e", p=128)), 16)
            on("gpsimd", pcb_w)
            gw_mark = s_wgt.mark

            for tc in range(16):
                def sp_l2(sp, tc=tc):
                    Wsp(sp, mk[("pc1xst", tc)])
                    if tc >= 2:
                        Wsp(sp, mk[("pcb_mm", tc - 2)])
                    s = sxlb[tc % 2]
                    s.inc(sp.dma_start(out=xl2[tc % 2][:],
                                       in_=x1lnT_d[:, :, tc * 128:(tc + 1) * 128]), 16)
                    mk[("pcb_ld", tc)] = s.mark
                on("sync", sp_l2)

                def pe_l2(t, tc=tc):
                    Wpe(t, mk[("pcb_ld", tc)])
                    Wpe(t, gw_mark)
                    if tc >= 1:
                        Wpe(t, mk[("pcb_sig", tc - 1)])
                    for dc in range(16):
                        inst = t.matmul(pl2[:], xl2[tc % 2][:, dc, :], gw[:, dc, :],
                                        start=(dc == 0), stop=(dc == 15))
                    s_pe.inc(inst)
                    mk[("pcb_mm", tc)] = s_pe.mark
                on("tensor", pe_l2)

                def act_l2(s, tc=tc):
                    Wact(s, mk[("pcb_mm", tc)])
                    if tc >= 1:
                        Wact(s, mk[("pcb_ws", tc - 1)])
                    s_act.inc(s.activation(sc2[:, 0:8], pl2[:], AF.Sigmoid))
                    mk[("pcb_sig", tc)] = s_act.mark
                on("scalar", act_l2)

                def dve_l2(v, tc=tc):
                    Wdve(v, mk[("pcb_sig", tc)])
                    if tc >= 1:
                        Wdve(v, mk[("pcb_tr", tc - 1)])
                    s_dve.inc(v.reduce_max(sc2[:, 8:9], sc2[:, 0:8], axis=AX.X))
                    v.wait_ge(s_dve.h, s_dve.n)
                    s_dve.inc(v.tensor_scalar(sc2[:, 16:24], sc2[:, 0:8], sc2[:, 8:9], None, ALU.is_equal))
                    v.wait_ge(s_dve.h, s_dve.n)
                    s_dve.inc(v.scalar_tensor_tensor(sc2[:, 24:32], sc2[:, 16:24], -1e9,
                                                     sc2[:, 0:8], ALU.mult, ALU.add))
                    v.wait_ge(s_dve.h, s_dve.n)
                    s_dve.inc(v.reduce_max(sc2[:, 9:10], sc2[:, 24:32], axis=AX.X))
                    v.wait_ge(s_dve.h, s_dve.n)
                    s_dve.inc(v.tensor_scalar(sc2[:, 24:32], sc2[:, 24:32], sc2[:, 9:10], None, ALU.is_equal))
                    v.wait_ge(s_dve.h, s_dve.n)
                    s_dve.inc(v.tensor_add(sc2[:, 16:24], sc2[:, 16:24], sc2[:, 24:32]))
                    v.wait_ge(s_dve.h, s_dve.n)
                    s_dve.inc(v.tensor_mul(sc2[:, 32:40], sc2[:, 0:8], sc2[:, 16:24]))
                    mk[("pcb_ws", tc)] = s_dve.mark
                on("vector", dve_l2)

                def pe_t2(t, tc=tc):
                    Wpe(t, mk[("pcb_ws", tc)])
                    if tc >= 1:
                        Wpe(t, mk[("pcb_ev", tc - 1)])
                    s_pe.inc(t.transpose(pt2[:], sc2[:, 32:40], id32[:]))
                    mk[("pcb_tr", tc)] = s_pe.mark
                on("tensor", pe_t2)

                def act_t2(s, tc=tc):
                    Wact(s, mk[("pcb_tr", tc)])
                    if tc >= 1:
                        Wact(s, mk[("pcb_st", tc - 1)])
                    s_act.inc(s.copy(wrow[:], pt2[:]))
                    mk[("pcb_ev", tc)] = s_act.mark
                on("scalar", act_t2)

                def sp_ws(sp, tc=tc):
                    Wsp(sp, mk[("pcb_ev", tc)])
                    swse.inc(sp.dma_start(out=wsel_d[:, tc * 128:(tc + 1) * 128], in_=wrow[:]), 16)
                    mk[("pcb_st", tc)] = swse.mark
                on("sync", sp_ws)
            wsel_done = swse.mark

        if PHASES < 6:
            def sp_end4(sp):
                for s in ALL_SEMS[3:]:
                    Wsp(sp, s.mark)
            on("sync", sp_end4)
            return nc

        phase_barrier()
        with ExitStack() as pc2:
            sw1 = pc2.enter_context(nc.sbuf_tensor("sw1", [128, 16, HID], BF16))
            sw3 = pc2.enter_context(nc.sbuf_tensor("sw3", [128, 16, HID], BF16))
            sw2 = pc2.enter_context(nc.sbuf_tensor("sw2", [128, 8, D], BF16))
            xw = pc2.enter_context(nc.sbuf_tensor("xw", [128, 16, 1024], BF16))
            acc = pc2.enter_context(nc.sbuf_tensor("acc", [128, 8, D], BF16))
            gat = pc2.enter_context(nc.sbuf_tensor("gat", [128, 8, 512], BF16))
            crow = pc2.enter_context(nc.sbuf_tensor("crow", [1, 1024], F32))
            cbc = pc2.enter_context(nc.sbuf_tensor("cbc", [128, 1024], F32))
            ca_t = pc2.enter_context(nc.sbuf_tensor("ca_t", [128, 512], F32))
            cb_t = pc2.enter_context(nc.sbuf_tensor("cb_t", [128, 512], F32))
            sg_t = pc2.enter_context(nc.sbuf_tensor("sg_t", [128, 512], F32))
            t1_t = pc2.enter_context(nc.sbuf_tensor("t1_t", [128, 512], F32))
            xfin = pc2.enter_context(nc.sbuf_tensor("xfin0", [128, D], F32))
            ofin = pc2.enter_context(nc.sbuf_tensor("ofin0", [128, D], F32))
            pab = [pc2.enter_context(nc.psum_tensor(f"pab{i}", [128, 512], F32)) for i in range(2)]
            pw2 = [pc2.enter_context(nc.psum_tensor(f"pw2{i}", [128, 512], F32)) for i in range(2)]
            pcb2 = pc2.enter_context(nc.psum_tensor("pcb2", [128, 512], F32))

            NW = 2
            EXPS = list(range(NEXP)) + ["sh"]
            prev_ab = [None]
            prev_w2 = [None]
            prev_cab = [None]
            prev_cbcmm = [None]

            for wv_i in range(NW):
                def sp_xw(sp, wv_i=wv_i):
                    Wsp(sp, mk[("pc1xst", 15)])
                    Wsp(sp, mk[("pc1xst", 14)])
                    if wv_i >= 1:
                        Wsp(sp, prev_ab[0])
                    sxw.inc(sp.dma_start(out=xw[:],
                                         in_=x1lnT_d[:, :, wv_i * 1024:(wv_i + 1) * 1024]), 16)
                    mk[("ffn_xw", wv_i)] = sxw.mark
                on("sync", sp_xw)

                for ei, e in enumerate(EXPS):
                    def pl_w(g, wv_i=wv_i, e=e):
                        Wpl(g, prev_ab[0])
                        if e == "sh":
                            s_w13.inc(g.dma_start(out=sw1[:], in_=ws1T_ap.rearrange("(dc p) h -> p dc h", p=128)), 16)
                            s_w13.inc(g.dma_start(out=sw3[:], in_=ws3T_ap.rearrange("(dc p) h -> p dc h", p=128)), 16)
                        else:
                            s_w13.inc(g.dma_start(out=sw1[:], in_=w1T_ap[e].rearrange("(dc p) h -> p dc h", p=128)), 16)
                            s_w13.inc(g.dma_start(out=sw3[:], in_=w3T_ap[e].rearrange("(dc p) h -> p dc h", p=128)), 16)
                        mk[("ffn_w13", wv_i, e)] = s_w13.mark
                        Wpl(g, prev_w2[0])
                        if e == "sh":
                            s_w2f.inc(g.dma_start(out=sw2[:], in_=ws2T_ap.rearrange("(hc p) n -> p hc n", p=128)), 16)
                        else:
                            s_w2f.inc(g.dma_start(out=sw2[:], in_=w2T_ap[e].rearrange("(hc p) n -> p hc n", p=128)), 16)
                        mk[("ffn_w2", wv_i, e)] = s_w2f.mark
                    on("gpsimd", pl_w)

                    if e != "sh":
                        def sp_c(sp, wv_i=wv_i, e=e):
                            Wsp(sp, wsel_done)
                            Wsp(sp, prev_cbcmm[0])
                            scro.inc(sp.dma_start(out=crow[:],
                                                  in_=wsel_d[e:e + 1, wv_i * 1024:(wv_i + 1) * 1024]), 16)
                            mk[("ffn_crow", wv_i, e)] = scro.mark
                        on("sync", sp_c)

                        for cb_i in range(2):
                            def pe_c(t, wv_i=wv_i, e=e, cb_i=cb_i):
                                Wpe(t, mk[("ffn_crow", wv_i, e)])
                                Wpe(t, pre_mark)
                                if cb_i == 0:
                                    Wpe(t, prev_cab[0])
                                else:
                                    Wpe(t, mk[("ffn_cbcev", wv_i, e, 0)])
                                s_pe.inc(t.matmul(pcb2[:], oner[:],
                                                  crow[:, cb_i * 512:(cb_i + 1) * 512],
                                                  start=True, stop=True))
                                mk[("ffn_cbcmm", wv_i, e, cb_i)] = s_pe.mark
                            on("tensor", pe_c)

                            def act_c(s, wv_i=wv_i, e=e, cb_i=cb_i):
                                Wact(s, mk[("ffn_cbcmm", wv_i, e, cb_i)])
                                s_act.inc(s.copy(cbc[:, cb_i * 512:(cb_i + 1) * 512], pcb2[:]))
                                mk[("ffn_cbcev", wv_i, e, cb_i)] = s_act.mark
                            on("scalar", act_c)
                        prev_cbcmm[0] = mk[("ffn_cbcmm", wv_i, e, 1)]

                    for tg in range(2):
                        for hb in range(8):
                            def pe_ab(t, wv_i=wv_i, e=e, tg=tg, hb=hb):
                                Wpe(t, mk[("ffn_w13", wv_i, e)])
                                Wpe(t, mk[("ffn_xw", wv_i)])
                                prevcab = mk.get(("ffn_cab", wv_i, e, tg, hb - 1),
                                                 mk.get(("ffn_cab", wv_i, e, tg - 1, 7),
                                                        prev_cab[0]))
                                Wpe(t, prevcab)
                                for dc in range(16):
                                    t.matmul(pab[0][:], sw1[:, dc, hb * 128:(hb + 1) * 128],
                                             xw[:, dc, tg * 512:(tg + 1) * 512],
                                             start=(dc == 0), stop=(dc == 15))
                                for dc in range(16):
                                    inst = t.matmul(pab[1][:], sw3[:, dc, hb * 128:(hb + 1) * 128],
                                                    xw[:, dc, tg * 512:(tg + 1) * 512],
                                                    start=(dc == 0), stop=(dc == 15))
                                s_pe.inc(inst)
                                mk[("ffn_ab", wv_i, e, tg, hb)] = s_pe.mark
                            on("tensor", pe_ab)

                            def dve_g1(v, wv_i=wv_i, e=e, tg=tg, hb=hb):
                                Wdve(v, mk[("ffn_ab", wv_i, e, tg, hb)])
                                if e != "sh":
                                    Wdve(v, mk[("ffn_cbcev", wv_i, e, 1)])
                                    v.tensor_mul(ca_t[:], pab[0][:], cbc[:, tg * 512:(tg + 1) * 512])
                                    inst = v.tensor_mul(cb_t[:], pab[1][:], cbc[:, tg * 512:(tg + 1) * 512])
                                else:
                                    v.tensor_copy(ca_t[:], pab[0][:])
                                    inst = v.tensor_copy(cb_t[:], pab[1][:])
                                s_dve.inc(inst)
                                mk[("ffn_cab", wv_i, e, tg, hb)] = s_dve.mark
                            on("vector", dve_g1)

                            def act_sg(s, wv_i=wv_i, e=e, tg=tg, hb=hb):
                                Wact(s, mk[("ffn_cab", wv_i, e, tg, hb)])
                                s_act.inc(s.activation(sg_t[:], ca_t[:], AF.Sigmoid))
                                mk[("ffn_sg", wv_i, e, tg, hb)] = s_act.mark
                            on("scalar", act_sg)

                            def dve_g2(v, wv_i=wv_i, e=e, tg=tg, hb=hb):
                                Wdve(v, mk[("ffn_sg", wv_i, e, tg, hb)])
                                if tg >= 1:
                                    Wdve(v, mk[("ffn_w2mm", wv_i, e, 15)])
                                s_dve.inc(v.tensor_mul(t1_t[:], ca_t[:], cb_t[:]))
                                v.wait_ge(s_dve.h, s_dve.n)
                                inst = v.tensor_mul(gat[:, hb, :], t1_t[:], sg_t[:])
                                s_dve.inc(inst)
                                mk[("ffn_gat", wv_i, e, tg, hb)] = s_dve.mark
                            on("vector", dve_g2)

                        for ts4 in range(4):
                            tci = tg * 4 + ts4
                            for dg in range(4):
                                gi = tci * 4 + dg

                                def pe_w2(t, wv_i=wv_i, e=e, tg=tg, ts4=ts4, gi=gi, dg=dg):
                                    Wpe(t, mk[("ffn_w2", wv_i, e)])
                                    Wpe(t, mk[("ffn_gat", wv_i, e, tg, 7)])
                                    if gi >= 2:
                                        Wpe(t, mk[("ffn_acc", wv_i, e, gi - 2)])
                                    p = pw2[gi % 2]
                                    for hc in range(8):
                                        inst = t.matmul(p[:], gat[:, hc, ts4 * 128:(ts4 + 1) * 128],
                                                        sw2[:, hc, dg * 512:(dg + 1) * 512],
                                                        start=(hc == 0), stop=(hc == 7))
                                    s_pe.inc(inst)
                                    mk[("ffn_w2mm", wv_i, e, gi)] = s_pe.mark
                                on("tensor", pe_w2)

                                def dve_acc(v, wv_i=wv_i, ei=ei, e=e, tci=tci, gi=gi, dg=dg):
                                    Wdve(v, mk[("ffn_w2mm", wv_i, e, gi)])
                                    p = pw2[gi % 2]
                                    dst = acc[:, tci, dg * 512:(dg + 1) * 512]
                                    if ei == 0:
                                        inst = v.tensor_copy(dst, p[:])
                                    else:
                                        inst = v.tensor_add(dst, dst, p[:])
                                    s_dve.inc(inst)
                                    mk[("ffn_acc", wv_i, e, gi)] = s_dve.mark
                                on("vector", dve_acc)

                    prev_ab[0] = mk[("ffn_ab", wv_i, e, 1, 7)]
                    prev_w2[0] = mk[("ffn_w2mm", wv_i, e, 31)]
                    prev_cab[0] = mk[("ffn_cab", wv_i, e, 1, 7)]

                for tci in range(8):
                    tc = wv_i * 8 + tci

                    def sp_xf(sp, tc=tc):
                        if tc >= 1:
                            Wsp(sp, mk[("fin", tc - 1)])
                        sxf.inc(sp.dma_start(out=xfin[:],
                                             in_=x1_d[tc * 128:(tc + 1) * 128, :]), 16)
                        mk[("fin_ld", tc)] = sxf.mark
                    on("sync", sp_xf)

                    def dve_fin(v, tc=tc, tci=tci, wv_i=wv_i):
                        Wdve(v, mk[("fin_ld", tc)])
                        if tc >= 1:
                            Wdve(v, mk[("fin_st", tc - 1)])
                        inst = v.tensor_add(ofin[:], xfin[:], acc[:, tci, :])
                        s_dve.inc(inst)
                        mk[("fin", tc)] = s_dve.mark
                    on("vector", dve_fin)

                    def sp_of(sp, tc=tc):
                        Wsp(sp, mk[("fin", tc)])
                        sout.inc(sp.dma_start(out=out_ap[tc * 128:(tc + 1) * 128, :],
                                              in_=ofin[:]), 16)
                        mk[("fin_st", tc)] = sout.mark
                    on("sync", sp_of)

            def sp_end(sp):
                for s in ALL_SEMS[3:]:
                    Wsp(sp, s.mark)
            on("sync", sp_end)

    return nc


def _host_inputs(inputs, core):
    b, hg = core // 2, core % 2
    f32 = np.float32
    ar = lambda v: np.ascontiguousarray(v)
    x = ar(np.asarray(inputs["L0_attn_hidden_states"][b], f32))
    return {
        "x": x,
        "x_own": ar(x[hg * TOWN:(hg + 1) * TOWN]),
        "wq": ar(np.asarray(inputs["L0_attn_Wq"], f32)[:, hg * 512:(hg + 1) * 512]),
        "wk": ar(np.asarray(inputs["L0_attn_Wk"], f32)[:, hg * 512:(hg + 1) * 512]),
        "ws": ar(np.asarray(inputs["L0_attn_Ws"], f32)[:, hg * 512:(hg + 1) * 512]),
        "wv": ar(np.asarray(inputs["L0_attn_Wv"], f32)[:, hg * 1024:(hg + 1) * 1024]),
        "wg": ar(np.asarray(inputs["L0_attn_Wg"], f32)[:, hg * 1024:(hg + 1) * 1024]),
        "wo": ar(np.asarray(inputs["L0_attn_Wo"], f32)),
        "gnw": ar(np.asarray(inputs["L0_attn_g_norm_weight"], f32).reshape(128, 1)),
        "gatewT": ar(np.asarray(inputs["L0_ffn_gate_w"], f32).T),
        "w1T": ar(np.asarray(inputs["L0_ffn_w1"], f32).transpose(0, 2, 1)),
        "w3T": ar(np.asarray(inputs["L0_ffn_w3"], f32).transpose(0, 2, 1)),
        "w2T": ar(np.asarray(inputs["L0_ffn_w2"], f32).transpose(0, 2, 1)),
        "ws1T": ar(np.asarray(inputs["L0_ffn_ws1"], f32).T),
        "ws3T": ar(np.asarray(inputs["L0_ffn_ws3"], f32).T),
        "ws2T": ar(np.asarray(inputs["L0_ffn_ws2"], f32).T),
        "mask_ut": np.triu(np.ones((128, 128), ml_dtypes.bfloat16)),
        "ident": np.eye(128, dtype=ml_dtypes.bfloat16),
        "ident32": np.eye(128, dtype=np.float32),
        "ones_col": np.ones((128, 1), ml_dtypes.bfloat16),
        "ones_row": np.ones((1, 128), np.float32),
        "constv": np.tile(np.array([[EPS, 1.0]], np.float32), (128, 1)),
        "hgv": np.array([[hg]], np.int32),
    }


def kernel(**inputs):
    nc = build_nc()
    in_maps = [_host_inputs(inputs, c) for c in range(8)]
    res = run_bass_kernel_spmd(nc, in_maps, core_ids=list(range(8)))
    out = np.empty((4, T, D), np.float32)
    for c in range(8):
        b, hg = c // 2, c % 2
        out[b, hg * TOWN:(hg + 1) * TOWN, :] = res.results[c]["out_half"]
    return out


if __name__ == "__main__":
    import time
    t0 = time.time()
    nc = build_nc()
    print("built", time.time() - t0, "n_instructions:", nc.n_instructions())


# version 10 (final)
# speedup vs baseline: 1.0600x; 1.0600x over previous
import numpy as np
import ml_dtypes
from contextlib import ExitStack

import concourse.bass as bass
import concourse.mybir as mybir
from concourse.bass_utils import run_bass_kernel_spmd

F32 = mybir.dt.float32
BF16 = mybir.dt.bfloat16
I32 = mybir.dt.int32
AF = mybir.ActivationFunctionType
ALU = mybir.AluOpType
AX = mybir.AxisListType

T, D = 4096, 2048
H8, DK, DV, M = 8, 64, 128, 64
C = 128
CH = 512
NCH = T // CH
NA = CH // C
TOWN = 2048
EPS = 1e-5
NEXP = 8
HID = 1024
PHASES = 6
DBG_DUMP = False
P0_SERIAL = False


class Sem:
    def __init__(self, ctx, nc, name):
        self.h = ctx.enter_context(nc.semaphore(name))
        self.n = 0

    def inc(self, inst, k=1):
        inst.then_inc(self.h, k)
        self.n += k
        return self.n

    @property
    def mark(self):
        return (self, self.n)


class Waiter:

    def __init__(self):
        self.seen = {}

    def __call__(self, eng, mark):
        if mark is None:
            return
        sem, val = mark
        if val <= 0:
            return
        if self.seen.get(id(sem), -1) >= val:
            return
        eng.wait_ge(sem.h, val)
        self.seen[id(sem)] = val


def build_nc():
    nc = bass.Bass("TRN2", target_bir_lowering=False, debug=False,
                   detect_race_conditions=False)

    def din(name, shape, dtype=F32):
        return nc.dram_tensor(name, list(shape), dtype, kind="ExternalInput").ap()

    x_ap = din("x", (T, D))
    xown_ap = din("x_own", (TOWN, D))
    wq_ap = din("wq", (D, 512))
    wk_ap = din("wk", (D, 512))
    ws_ap = din("ws", (D, 512))
    wv_ap = din("wv", (D, 1024))
    wg_ap = din("wg", (D, 1024))
    wo_ap = din("wo", (D, D))
    gnw_ap = din("gnw", (128, 1))
    gwT_ap = din("gatewT", (D, 8))
    w1T_ap = din("w1T", (NEXP, D, HID))
    w3T_ap = din("w3T", (NEXP, D, HID))
    w2T_ap = din("w2T", (NEXP, HID, D))
    ws1T_ap = din("ws1T", (D, HID))
    ws3T_ap = din("ws3T", (D, HID))
    ws2T_ap = din("ws2T", (HID, D))
    mut_ap = din("mask_ut", (128, 128), BF16)
    ident_ap = din("ident", (128, 128), BF16)
    id32_ap = din("ident32", (128, 128))
    onec_ap = din("ones_col", (128, 1), BF16)
    oner_ap = din("ones_row", (1, 128))
    cst_ap = din("constv", (128, 2))
    hgv_ap = din("hgv", (1, 1), I32)

    out_ap = nc.dram_tensor("out_half", [TOWN, D], F32, kind="ExternalOutput").ap()

    dbk = "ExternalOutput" if DBG_DUMP else "Internal"
    xlnT_d = nc.dram_tensor("xlnT_d", [128, 16, T], BF16, kind=dbk).ap()
    sgg_d = nc.dram_tensor("sgg_d", [T, 1024], BF16, kind=dbk).ap()
    o3s_d = nc.dram_tensor("o3s_d", [4, 1024, 1024], BF16, kind="Internal").ap()
    o3m_d = nc.dram_tensor("o3m_d", [4, 2048, 1024], BF16, kind="Internal").ap()
    x1_d = nc.dram_tensor("x1_d", [TOWN, D], F32, kind=dbk).ap()
    x1lnT_d = nc.dram_tensor("x1lnT_d", [128, 16, TOWN], BF16, kind=dbk).ap()
    wsel_d = nc.dram_tensor("wsel_d", [8, TOWN], F32, kind=dbk).ap()

    ctx = ExitStack()
    with ctx:
        s_pe = Sem(ctx, nc, "s_pe")
        s_act = Sem(ctx, nc, "s_act")
        s_dve = Sem(ctx, nc, "s_dve")
        s_pre = Sem(ctx, nc, "s_pre")
        sx = [Sem(ctx, nc, f"sx{i}") for i in range(2)]
        sxt = [Sem(ctx, nc, f"sxt{i}") for i in range(2)]
        sxl1 = [Sem(ctx, nc, f"sxl1_{i}") for i in range(2)]
        ssg = [Sem(ctx, nc, f"ssg{i}") for i in range(2)]
        sxl2 = [Sem(ctx, nc, f"sxl2_{i}") for i in range(2)]
        ssgi = Sem(ctx, nc, "ssgi")
        so3a = [Sem(ctx, nc, f"so3a{i}") for i in range(NA)]
        scc = [Sem(ctx, nc, f"scc{i}") for i in range(4)]
        so3f = [Sem(ctx, nc, f"so3f{i}") for i in range(2)]
        sxo = [Sem(ctx, nc, f"sxo{i}") for i in range(2)]
        sx1 = [Sem(ctx, nc, f"sx1_{i}") for i in range(2)]
        sxt1 = [Sem(ctx, nc, f"sxt1_{i}") for i in range(2)]
        sxlb = [Sem(ctx, nc, f"sxlb{i}") for i in range(2)]
        swse = Sem(ctx, nc, "swse")
        sxw = Sem(ctx, nc, "sxw")
        scro = Sem(ctx, nc, "scro")
        sxf = Sem(ctx, nc, "sxf")
        sout = Sem(ctx, nc, "sout")
        s_wat = Sem(ctx, nc, "s_wat")
        s_wg = Sem(ctx, nc, "s_wg")
        s_wo = Sem(ctx, nc, "s_wo")
        s_wgt = Sem(ctx, nc, "s_wgt")
        s_w13 = Sem(ctx, nc, "s_w13")
        s_w2f = Sem(ctx, nc, "s_w2f")

        ALL_SEMS = [s_pe, s_act, s_dve, s_pre, *sx, *sxt, *sxl1, *ssg, *sxl2,
                    ssgi, *so3a, *scc, *so3f, *sxo, *sx1, *sxt1, *sxlb, swse,
                    sxw, scro, sxf, sout, s_wat, s_wg, s_wo, s_wgt, s_w13, s_w2f]

        Wpe, Wact, Wdve, Wsp, Wpl = Waiter(), Waiter(), Waiter(), Waiter(), Waiter()

        mut = ctx.enter_context(nc.sbuf_tensor("mut_s", [128, 128], BF16))
        ident = ctx.enter_context(nc.sbuf_tensor("ident_s", [128, 128], BF16))
        id32 = ctx.enter_context(nc.sbuf_tensor("id32_s", [128, 128], F32))
        onec = ctx.enter_context(nc.sbuf_tensor("onec_s", [128, 1], BF16))
        oner = ctx.enter_context(nc.sbuf_tensor("oner_s", [1, 128], F32))
        gnw = ctx.enter_context(nc.sbuf_tensor("gnw_s", [128, 1], F32))
        cst = ctx.enter_context(nc.sbuf_tensor("cst_s", [128, 2], F32))
        hgv = ctx.enter_context(nc.sbuf_tensor("hgv_s", [1, 1], I32))

        block = ctx.enter_context(nc.Block())

        def on(engname, fn):
            getattr(block, engname)(fn)

        mk = {}

        def phase_barrier():
            cur = [s.mark for s in ALL_SEMS]

            def mkf(wtr):
                def f(eng):
                    for m in cur:
                        wtr(eng, m)
                return f
            on("tensor", mkf(Wpe))
            on("scalar", mkf(Wact))
            on("vector", mkf(Wdve))
            on("sync", mkf(Wsp))
            on("gpsimd", mkf(Wpl))

        def _pre(sp):
            for dst, src in ((mut, mut_ap), (ident, ident_ap), (id32, id32_ap),
                             (onec, onec_ap), (oner, oner_ap), (gnw, gnw_ap),
                             (cst, cst_ap), (hgv, hgv_ap)):
                s_pre.inc(sp.dma_start(out=dst[:], in_=src), 16)
        on("sync", _pre)
        pre_mark = s_pre.mark

        for _en, _W in (("tensor", Wpe), ("scalar", Wact), ("vector", Wdve),
                        ("gpsimd", Wpl)):
            def _pw(eng, _W=_W):
                _W(eng, pre_mark)
            on(_en, _pw)

        aw = ExitStack()
        wqs = aw.enter_context(nc.sbuf_tensor("wqs", [128, 16, 512], BF16))
        wks = aw.enter_context(nc.sbuf_tensor("wks", [128, 16, 512], BF16))
        wss = aw.enter_context(nc.sbuf_tensor("wss", [128, 16, 512], BF16))
        wvs = aw.enter_context(nc.sbuf_tensor("wvs", [128, 16, 1024], BF16))

        def p2_w(g):
            s_wat.inc(g.dma_start(out=wqs[:], in_=wq_ap.rearrange("(dc p) n -> p dc n", p=128)), 16)
            s_wat.inc(g.dma_start(out=wks[:], in_=wk_ap.rearrange("(dc p) n -> p dc n", p=128)), 16)
            s_wat.inc(g.dma_start(out=wss[:], in_=ws_ap.rearrange("(dc p) n -> p dc n", p=128)), 16)
            s_wat.inc(g.dma_start(out=wvs[:], in_=wv_ap.rearrange("(dc p) n -> p dc n", p=128)), 16)
        on("gpsimd", p2_w)
        wat_mark = s_wat.mark

        with ExitStack() as p0:
            xb = [p0.enter_context(nc.sbuf_tensor(f"p0x{i}", [128, NA, D], F32)) for i in range(2)]
            xl = [p0.enter_context(nc.sbuf_tensor("p0l0", [128, NA, D], BF16))] * 2
            xtT = [p0.enter_context(nc.sbuf_tensor(f"p0t{i}", [128, 16, CH], BF16)) for i in range(2)]
            st = p0.enter_context(nc.sbuf_tensor("p0st", [128, 8], F32))
            sqs = p0.enter_context(nc.sbuf_tensor("p0sq", [128, D], F32))
            pt = [p0.enter_context(nc.psum_tensor(f"p0p{i}", [128, 512], BF16)) for i in range(2)]

            for ch in range(NCH):
                if P0_SERIAL and ch >= 1:
                    cur = [s.mark for s in (s_pe, s_act, s_dve)]

                    def bar(eng, cur=cur, W=None):
                        for m in cur:
                            W(eng, m)
                    on("sync", lambda sp, cur=cur: [Wsp(sp, m) for m in cur])
                    on("vector", lambda v, cur=cur: [Wdve(v, m) for m in cur])
                    on("scalar", lambda s, cur=cur: [Wact(s, m) for m in cur])
                    on("tensor", lambda t, cur=cur: [Wpe(t, m) for m in cur])

                def sp_load(sp, ch=ch):
                    Wsp(sp, pre_mark)
                    if ch >= 2:
                        Wsp(sp, mk[("p0c", (ch - 2) * NA + NA - 1)])
                    s = sx[ch % 2]
                    s.inc(sp.dma_start(
                        out=xb[ch % 2][:],
                        in_=x_ap[ch * CH:(ch + 1) * CH, :].rearrange("(a p) d -> p a d", p=128)), 16)
                    mk[("p0ld", ch)] = s.mark
                on("sync", sp_load)

                for a in range(NA):
                    i = ch * NA + a

                    def dve_s1(v, ch=ch, a=a, i=i):
                        if i == 0:
                            Wdve(v, pre_mark)
                        Wdve(v, mk[("p0ld", ch)])
                        if ch >= 1:
                            Wdve(v, mk[("p0tr", (ch - 1) * NA + NA - 1, 3)])
                        xa = xb[ch % 2][:, a, :]
                        s_dve.inc(v.reduce_sum(st[:, 0:1], xa, axis=AX.X))
                        v.wait_ge(s_dve.h, s_dve.n)
                        s_dve.inc(v.tensor_scalar_mul(st[:, 1:2], st[:, 0:1], 1.0 / D))
                        v.wait_ge(s_dve.h, s_dve.n)
                        s_dve.inc(v.tensor_mul(st[:, 2:3], st[:, 1:2], st[:, 1:2]))
                        mk[("p0a", i)] = s_dve.mark
                    on("vector", dve_s1)

                    def dve_s2(v, ch=ch, a=a, i=i):
                        s_dve.inc(v.tensor_mul(sqs[:], xb[ch % 2][:, a, :], xb[ch % 2][:, a, :]))
                        v.wait_ge(s_dve.h, s_dve.n)
                        s_dve.inc(v.reduce_sum(st[:, 0:1], sqs[:], axis=AX.X))
                        v.wait_ge(s_dve.h, s_dve.n)
                        s_dve.inc(v.scalar_tensor_tensor(st[:, 3:4], st[:, 0:1], 1.0 / D,
                                                         st[:, 2:3], ALU.mult, ALU.subtract))
                        mk[("p0b", i)] = s_dve.mark
                    on("vector", dve_s2)

                    def act_s2(s, i=i):
                        Wact(s, mk[("p0b", i)])
                        s_act.inc(s.activation(st[:, 5:6], st[:, 3:4], AF.Ln, bias=cst[:, 0:1]))
                        s.wait_ge(s_act.h, s_act.n)
                        s_act.inc(s.activation(st[:, 6:7], st[:, 5:6], AF.Exp, scale=-0.5))
                        mk[("p0r", i)] = s_act.mark
                    on("scalar", act_s2)

                    def dve_s3(v, ch=ch, a=a, i=i):
                        Wdve(v, mk[("p0r", i)])
                        s_dve.inc(v.tensor_scalar(xl[ch % 2][:, a, :], xb[ch % 2][:, a, :],
                                                  st[:, 1:2], st[:, 6:7],
                                                  ALU.subtract, ALU.mult))
                        mk[("p0c", i)] = s_dve.mark
                    on("vector", dve_s3)

                    for k in range(4):
                        gidx = i * 4 + k

                        def pe_tr(t, ch=ch, a=a, i=i, k=k, gidx=gidx):
                            if gidx == 0:
                                Wpe(t, pre_mark)
                            Wpe(t, mk[("p0c", i)])
                            if gidx >= 2:
                                Wpe(t, mk[("p0ev", gidx - 2)])
                            p = pt[gidx % 2]
                            for j in range(4):
                                dc = k * 4 + j
                                inst = t.transpose(p[:, j * 128:(j + 1) * 128],
                                                   xl[ch % 2][:, a, dc * 128:(dc + 1) * 128],
                                                   ident[:])
                            s_pe.inc(inst)
                            mk[("p0tr", i, k)] = s_pe.mark
                        on("tensor", pe_tr)

                        def act_ev(s, ch=ch, a=a, i=i, k=k, gidx=gidx):
                            Wact(s, mk[("p0tr", i, k)])
                            if ch >= 2:
                                Wact(s, mk[("p0st", ch - 2)])
                            s_act.inc(s.copy(
                                xtT[ch % 2][:, k * 4:(k + 1) * 4, a * 128:(a + 1) * 128],
                                pt[gidx % 2][:].rearrange("p (j t) -> p j t", j=4)))
                            mk[("p0ev", gidx)] = s_act.mark
                        on("scalar", act_ev)

                def sp_store(sp, ch=ch):
                    Wsp(sp, mk[("p0ev", (ch * NA + NA - 1) * 4 + 3)])
                    s = sxt[ch % 2]
                    s.inc(sp.dma_start(
                        out=xlnT_d[:, :, ch * CH:(ch + 1) * CH], in_=xtT[ch % 2][:]), 16)
                    mk[("p0st", ch)] = s.mark
                on("sync", sp_store)

        if PHASES < 2:
            def sp_end0(sp):
                for s in ALL_SEMS[3:]:
                    Wsp(sp, s.mark)
            on("sync", sp_end0)
            aw.close()
            return nc

        phase_barrier()
        with ExitStack() as p1:
            wgt = p1.enter_context(nc.sbuf_tensor("p1wg", [128, 16, 1024], BF16))
            xlt = [p1.enter_context(nc.sbuf_tensor(f"p1x{i}", [128, 16, CH], BF16)) for i in range(2)]
            sgo = [p1.enter_context(nc.sbuf_tensor(f"p1o{i}", [128, NA, 1024], BF16)) for i in range(2)]
            sgt = p1.enter_context(nc.sbuf_tensor("p1s", [128, 512], F32))
            pg = [p1.enter_context(nc.psum_tensor(f"p1p{i}", [128, 512], F32)) for i in range(2)]

            def p1_w(g):
                s_wg.inc(g.dma_start(out=wgt[:],
                                     in_=wg_ap.rearrange("(dc p) n -> p dc n", p=128)), 16)
            on("gpsimd", p1_w)
            wg_mark = s_wg.mark

            for ch in range(NCH):
                def sp_l(sp, ch=ch):
                    Wsp(sp, mk[("p0st", ch)])
                    if ch >= 2:
                        Wsp(sp, mk[("p1mm", (ch - 2) * NA * 2 + NA * 2 - 1)])
                    s = sxl1[ch % 2]
                    s.inc(sp.dma_start(out=xlt[ch % 2][:],
                                       in_=xlnT_d[:, :, ch * CH:(ch + 1) * CH]), 16)
                    mk[("p1ld", ch)] = s.mark
                on("sync", sp_l)

                for a in range(NA):
                    for vb in range(2):
                        gi = (ch * NA + a) * 2 + vb

                        def pe_g(t, ch=ch, a=a, vb=vb, gi=gi):
                            Wpe(t, mk[("p1ld", ch)])
                            Wpe(t, wg_mark)
                            if gi >= 2:
                                Wpe(t, mk[("p1ev", gi - 2)])
                            p = pg[gi % 2]
                            for dc in range(16):
                                inst = t.matmul(p[:],
                                                xlt[ch % 2][:, dc, a * 128:(a + 1) * 128],
                                                wgt[:, dc, vb * 512:(vb + 1) * 512],
                                                start=(dc == 0), stop=(dc == 15))
                            s_pe.inc(inst)
                            mk[("p1mm", gi)] = s_pe.mark
                        on("tensor", pe_g)

                        def act_g(s, gi=gi):
                            Wact(s, mk[("p1mm", gi)])
                            if gi >= 1:
                                Wact(s, mk[("p1ev", gi - 1)])
                            s_act.inc(s.activation(sgt[:], pg[gi % 2][:], AF.Sigmoid))
                            mk[("p1sg", gi)] = s_act.mark
                        on("scalar", act_g)

                        def dve_g(v, ch=ch, a=a, vb=vb, gi=gi):
                            Wdve(v, mk[("p1sg", gi)])
                            if ch >= 2:
                                Wdve(v, mk[("p1st", ch - 2)])
                            s_dve.inc(v.tensor_mul(
                                sgo[ch % 2][:, a, vb * 512:(vb + 1) * 512],
                                pg[gi % 2][:], sgt[:]))
                            mk[("p1ev", gi)] = s_dve.mark
                        on("vector", dve_g)

                def sp_s(sp, ch=ch):
                    Wsp(sp, mk[("p1ev", (ch * NA + NA - 1) * 2 + 1)])
                    s = ssg[ch % 2]
                    s.inc(sp.dma_start(
                        out=sgg_d[ch * CH:(ch + 1) * CH, :].rearrange("(a p) n -> p a n", p=128),
                        in_=sgo[ch % 2][:]), 16)
                    mk[("p1st", ch)] = s.mark
                on("sync", sp_s)

        if PHASES < 3:
            def sp_end1(sp):
                for s in ALL_SEMS[3:]:
                    Wsp(sp, s.mark)
            on("sync", sp_end1)
            aw.close()
            return nc

        phase_barrier()
        with ExitStack() as p2:
            xlt2 = [p2.enter_context(nc.sbuf_tensor(f"p2x{i}", [128, 16, CH], BF16)) for i in range(2)]
            sgi = [p2.enter_context(nc.sbuf_tensor("p2g0", [128, NA, 1024], BF16))] * 2
            qsT = [p2.enter_context(nc.sbuf_tensor(f"qsT{h}", [64, CH], BF16)) for h in range(H8)]
            kT = [p2.enter_context(nc.sbuf_tensor(f"kT{h}", [64, CH], BF16)) for h in range(H8)]
            sT = [p2.enter_context(nc.sbuf_tensor(f"sT{h}", [64, CH], BF16)) for h in range(H8)]
            k_t = p2.enter_context(nc.sbuf_tensor("k_t", [128, NA, 512], BF16))
            v_t = p2.enter_context(nc.sbuf_tensor("v_t", [128, NA, 1024], BF16))
            hkT = [p2.enter_context(nc.sbuf_tensor(f"hkT{h}", [64, 64], F32)) for h in range(H8)]
            hv = [p2.enter_context(nc.sbuf_tensor(f"hv{h}", [64, 128], F32)) for h in range(H8)]
            nz = p2.enter_context(nc.sbuf_tensor("nz", [64, H8], F32))
            cs = p2.enter_context(nc.sbuf_tensor("cs", [64, H8], F32))
            lnt = p2.enter_context(nc.sbuf_tensor("lnt", [64, 2 * H8], F32))
            hkTb = [[p2.enter_context(nc.sbuf_tensor(f"hkTb{h}_{p}", [64, 64], BF16)) for p in range(2)] for h in range(H8)]
            hvb = [[p2.enter_context(nc.sbuf_tensor(f"hvb{h}_{p}", [64, 128], BF16)) for p in range(2)] for h in range(H8)]
            hkb = [p2.enter_context(nc.sbuf_tensor(f"hkb{h}", [64, 64], BF16)) for h in range(H8)]
            Emt = [p2.enter_context(nc.sbuf_tensor(f"Emt{h}", [64, C], BF16)) for h in range(H8)]
            Etm = [p2.enter_context(nc.sbuf_tensor(f"Etm{h}", [C, 64], BF16)) for h in range(H8)]
            ATm = [p2.enter_context(nc.sbuf_tensor(f"ATm{h}", [C, C], BF16)) for h in range(H8)]
            BTm = [p2.enter_context(nc.sbuf_tensor(f"BTm{h}", [C, C], BF16)) for h in range(H8)]
            rD1 = [p2.enter_context(nc.sbuf_tensor(f"rD1{h}", [C, 64], F32)) for h in range(H8)]
            okb = [p2.enter_context(nc.sbuf_tensor(f"ok{h}", [C, 64], F32)) for h in range(H8)]
            eob = [p2.enter_context(nc.sbuf_tensor(f"eo{h}", [C, 64], F32)) for h in range(H8)]
            pqb = [p2.enter_context(nc.sbuf_tensor(f"pq{h}", [C, 64], BF16)) for h in range(H8)]
            pqT = [p2.enter_context(nc.sbuf_tensor(f"pqT{h}", [64, C], BF16)) for h in range(H8)]
            ovs = [p2.enter_context(nc.sbuf_tensor(f"ovs{h}", [C, 128], F32)) for h in range(H8)]
            sq2 = [p2.enter_context(nc.sbuf_tensor(f"sq2{h}", [C, 128], F32)) for h in range(H8)]
            scl = p2.enter_context(nc.sbuf_tensor("scl", [128, 4 * H8], F32))
            o3T = [p2.enter_context(nc.sbuf_tensor(f"o3T{i}", [128, H8, C], BF16)) for i in range(NA)]
            bP = [p2.enter_context(nc.psum_tensor(f"bP{i}", [128, 512], F32)) for i in range(2)]
            bA = [p2.enter_context(nc.psum_tensor(f"bA{i}", [128, 512], F32)) for i in range(2)]
            bU = p2.enter_context(nc.psum_tensor("bU", [128, 512], F32))
            bV = p2.enter_context(nc.psum_tensor("bV", [128, 512], F32))
            bT1 = p2.enter_context(nc.psum_tensor("bT1", [128, 1024], BF16))
            bT2 = p2.enter_context(nc.psum_tensor("bT2", [128, 1024], BF16))

            prj = [None] * NCH
            prjv = [None] * NCH

            def emit_p2_sp(ch):
                def f(sp, ch=ch):
                    Wsp(sp, mk[("p0st", ch)])
                    Wsp(sp, mk[("p1st", ch)])
                    if ch >= 1:
                        Wsp(sp, mk[((ch - 1) * NA + NA - 1, "dve7")])
                    if ch >= 2:
                        Wsp(sp, prj[ch - 2])
                    s = sxl2[ch % 2]
                    s.inc(sp.dma_start(out=xlt2[ch % 2][:],
                                       in_=xlnT_d[:, :, ch * CH:(ch + 1) * CH]), 16)
                    mk[("p2ld", ch)] = s.mark
                    ssgi.inc(sp.dma_start(
                        out=sgi[ch % 2][:],
                        in_=sgg_d[ch * CH:(ch + 1) * CH, :].rearrange("(a p) n -> p a n", p=128)), 16)
                    mk[("p2sg", ch)] = ssgi.mark
                on("sync", f)

            def emit_proj(ch):
                blocks = ([("q", bi) for bi in range(4)] + [("k", bi) for bi in range(4)]
                          + [("s", bi) for bi in range(4)] + [("kt", a) for a in range(NA)]
                          + [("vt", (a, vb)) for a in range(NA) for vb in range(2)])

                for i, (kind, arg) in enumerate(blocks):
                    def pe_f(t, ch=ch, i=i, kind=kind, arg=arg):
                        if i == 0:
                            Wpe(t, mk[("p2ld", ch)])
                            Wpe(t, wat_mark)
                            if ch >= 1:
                                Wpe(t, mk[((ch - 1) * NA + NA - 1, "act1")])
                                Wpe(t, mk[((ch - 1) * NA + NA - 1, "dve7")])
                        p = bP[i % 2]
                        if i >= 2:
                            Wpe(t, mk[("prjev", ch, i - 2)])
                        if kind in ("q", "k", "s"):
                            wsb = {"q": wqs, "k": wks, "s": wss}[kind]
                            for dc in range(16):
                                inst = t.matmul(p[:], wsb[:, dc, arg * 128:(arg + 1) * 128],
                                                xlt2[ch % 2][:, dc, :],
                                                start=(dc == 0), stop=(dc == 15))
                        elif kind == "kt":
                            for dc in range(16):
                                inst = t.matmul(p[:], xlt2[ch % 2][:, dc, arg * 128:(arg + 1) * 128],
                                                wks[:, dc, :], start=(dc == 0), stop=(dc == 15))
                        else:
                            a, vb = arg
                            for dc in range(16):
                                inst = t.matmul(p[:], xlt2[ch % 2][:, dc, a * 128:(a + 1) * 128],
                                                wvs[:, dc, vb * 512:(vb + 1) * 512],
                                                start=(dc == 0), stop=(dc == 15))
                        s_pe.inc(inst)
                        mk[("prjmm", ch, i)] = s_pe.mark
                        if i == len(blocks) - 1:
                            prj[ch] = s_pe.mark
                    on("tensor", pe_f)

                    if kind in ("q", "k"):
                        def act_f(s, ch=ch, i=i, kind=kind, arg=arg):
                            Wact(s, mk[("prjmm", ch, i)])
                            p = bP[i % 2]
                            dsts = qsT if kind == "q" else kT
                            sc = (DK ** -0.5) if kind == "q" else 1.0
                            h0 = arg * 2
                            s.activation(dsts[h0][:], p[0:64, :], AF.Copy, scale=sc)
                            s_act.inc(s.activation(dsts[h0 + 1][:], p[64:128, :], AF.Copy, scale=sc))
                            mk[("prjev", ch, i)] = s_act.mark
                        on("scalar", act_f)
                    else:
                        def dve_f(v, ch=ch, i=i, kind=kind, arg=arg):
                            Wdve(v, mk[("prjmm", ch, i)])
                            p = bP[i % 2]
                            if kind == "s":
                                h0 = arg * 2
                                v.tensor_scalar(sT[h0][:], p[0:64, :], -32.0, 32.0, ALU.max, ALU.min)
                                inst = v.tensor_scalar(sT[h0 + 1][:], p[64:128, :], -32.0, 32.0,
                                                       ALU.max, ALU.min)
                            elif kind == "kt":
                                inst = v.tensor_copy(k_t[:, arg, :], p[:])
                            else:
                                a, vb = arg
                                inst = v.tensor_copy(v_t[:, a, vb * 512:(vb + 1) * 512], p[:])
                            s_dve.inc(inst)
                            mk[("prjev", ch, i)] = s_dve.mark
                            if i == len(blocks) - 1:
                                prjv[ch] = s_dve.mark
                        on("vector", dve_f)

            def scan_chunk(ch, a):
                g = ch * NA + a
                par = g % 2
                first = (g == 0)
                tsl = slice(a * C, (a + 1) * C)

                def act1(s):
                    Wact(s, prjv[ch])
                    Wact(s, mk.get((g - 1, "dve2B")))
                    Wact(s, mk.get((g - 1, "pe5")))
                    for h in range(H8):
                        if first:
                            inst = s.activation(Emt[h][:], sT[h][:, tsl], AF.Exp)
                        else:
                            inst = s.activation(Emt[h][:], sT[h][:, tsl], AF.Exp,
                                                bias=nz[:, h:h + 1])
                        s_act.inc(inst)
                    mk[(g, "act1")] = s_act.mark
                on("scalar", act1)

                def pe1(t):
                    Wpe(t, mk[("prjev", ch, 7)])
                    Wpe(t, mk.get((g - 1, "act6")))
                    for h in range(H8):
                        p = bA[h // 4]
                        inst = t.matmul(p[:, (h % 4) * 128:(h % 4) * 128 + 128],
                                        kT[h][:, tsl], qsT[h][:, tsl], start=True, stop=True)
                        s_pe.inc(inst)
                    mk[(g, "pe1a")] = s_pe.mark
                    Wpe(t, mk[(g, "act1")])
                    Wpe(t, mk.get((g - 1, "act2")))
                    if not first:
                        Wpe(t, mk.get((g - 1, "dve2B")))
                    for h in range(H8):
                        inst = t.transpose(bT1[:, h * 64:(h + 1) * 64], Emt[h][:],
                                           ident[0:64, 0:64])
                        if not first:
                            inst = t.transpose(bT1[0:64, 512 + h * 64:512 + (h + 1) * 64],
                                               hkTb[h][(g - 1) % 2][:], ident[0:64, 0:64])
                        s_pe.inc(inst)
                    mk[(g, "pe1b")] = s_pe.mark
                on("tensor", pe1)

                def act2(s):
                    Wact(s, mk[(g, "pe1b")])
                    Wact(s, mk.get((g - 1, "pe3")))
                    for h in range(H8):
                        inst = s.copy(Etm[h][:], bT1[:, h * 64:(h + 1) * 64])
                        if not first:
                            inst = s.copy(hkb[h][:], bT1[0:64, 512 + h * 64:512 + (h + 1) * 64])
                        s_act.inc(inst)
                    mk[(g, "act2")] = s_act.mark
                on("scalar", act2)

                def dve_cs(v):
                    Wdve(v, mk[(g, "act1")])
                    for h in range(H8):
                        inst = v.reduce_sum(cs[:, h:h + 1], Emt[h][:], axis=AX.X)
                        s_dve.inc(inst)
                    mk[(g, "dvecs")] = s_dve.mark
                on("vector", dve_cs)

                def dve1(v):
                    Wdve(v, mk[(g, "pe1a")])
                    for h in range(H8):
                        p = bA[h // 4]
                        inst = v.tensor_mul(ATm[h][:], p[:, (h % 4) * 128:(h % 4) * 128 + 128],
                                            mut[:])
                        s_dve.inc(inst)
                    mk[(g, "dve1")] = s_dve.mark
                on("vector", dve1)

                for half in range(2):
                    hs = range(half * 4, half * 4 + 4)
                    tag = "A" if half == 0 else "B"

                    def pe2(t, hs=hs, half=half, tag=tag):
                        Wpe(t, mk[(g, "act2")])
                        if half == 0:
                            Wpe(t, mk.get((g - 1, "dve3")))
                            Wpe(t, mk.get((g - 1, "dve2B")))
                        else:
                            Wpe(t, mk[(g, "dve2A")])
                            Wpe(t, mk[(g, "act3A")])
                        for h in hs:
                            r = h % 4
                            t.matmul(bU[0:64, r * 64:(r + 1) * 64], Etm[h][:],
                                     k_t[:, a, h * 64:(h + 1) * 64], start=True, stop=True)
                            t.matmul(bV[0:64, r * 128:(r + 1) * 128], Etm[h][:],
                                     v_t[:, a, h * 128:(h + 1) * 128], start=True, stop=True)
                            inst = t.matmul(bU[:, 256 + r * 64:256 + (r + 1) * 64], mut[:],
                                            Etm[h][:], start=True, stop=True)
                            s_pe.inc(inst)
                        mk[(g, "pe2" + tag)] = s_pe.mark
                    on("tensor", pe2)

                    def act3(s, hs=hs, tag=tag):
                        Wact(s, mk[(g, "pe2" + tag)])
                        Wact(s, mk[(g, "dvecs")])
                        Wact(s, mk.get((g - 1, "dve4")))
                        for h in hs:
                            r = h % 4
                            if first:
                                s.activation(rD1[h][:], bU[:, 256 + r * 64:256 + (r + 1) * 64],
                                             AF.Ln)
                            else:
                                s.activation(rD1[h][:], bU[:, 256 + r * 64:256 + (r + 1) * 64],
                                             AF.Ln, bias=cst[:, 1:2])
                        for h in hs:
                            if first:
                                inst = s.activation(lnt[:, 2 * h:2 * h + 1], cs[:, h:h + 1], AF.Ln)
                            else:
                                inst = s.activation(lnt[:, 2 * h:2 * h + 1], cs[:, h:h + 1], AF.Ln,
                                                    bias=cst[0:64, 1:2])
                        s_act.inc(inst)
                        s.wait_ge(s_act.h, s_act.n)
                        for h in hs:
                            s.activation(rD1[h][:], rD1[h][:], AF.Exp, scale=-1.0)
                        for h in hs:
                            inst = s.activation(lnt[:, 2 * h + 1:2 * h + 2],
                                                lnt[:, 2 * h:2 * h + 1], AF.Exp, scale=-1.0)
                        s_act.inc(inst)
                        mk[(g, "act3" + tag)] = s_act.mark
                    on("scalar", act3)

                    def dve2(v, hs=hs, tag=tag, first=first):
                        Wdve(v, mk[(g, "act3" + tag)])
                        if first:
                            for h in hs:
                                r = h % 4
                                gam = lnt[:, 2 * h + 1:2 * h + 2]
                                v.tensor_scalar_mul(nz[:, h:h + 1], lnt[:, 2 * h:2 * h + 1], -1.0)
                                v.tensor_scalar_mul(hkT[h][:], bU[0:64, r * 64:(r + 1) * 64], gam)
                                inst = v.tensor_scalar_mul(hv[h][:], bV[0:64, r * 128:(r + 1) * 128], gam)
                            s_dve.inc(inst)
                        else:
                            for h in hs:
                                r = h % 4
                                v.tensor_sub(nz[:, h:h + 1], nz[:, h:h + 1], lnt[:, 2 * h:2 * h + 1])
                                v.tensor_add(hkT[h][:], hkT[h][:], bU[0:64, r * 64:(r + 1) * 64])
                                inst = v.tensor_add(hv[h][:], hv[h][:], bV[0:64, r * 128:(r + 1) * 128])
                            s_dve.inc(inst)
                            v.wait_ge(s_dve.h, s_dve.n)
                            for h in hs:
                                gam = lnt[:, 2 * h + 1:2 * h + 2]
                                v.tensor_scalar_mul(hkT[h][:], hkT[h][:], gam)
                                inst = v.tensor_scalar_mul(hv[h][:], hv[h][:], gam)
                            s_dve.inc(inst)
                        v.wait_ge(s_dve.h, s_dve.n)
                        for h in hs:
                            v.tensor_copy(hkTb[h][par][:], hkT[h][:])
                            inst = v.tensor_copy(hvb[h][par][:], hv[h][:])
                        s_dve.inc(inst)
                        mk[(g, "dve2" + tag)] = s_dve.mark
                    on("vector", dve2)

                def pe3(t):
                    Wpe(t, mk[(g, "dve1")])
                    Wpe(t, mk[(g, "dve2B")])
                    Wpe(t, mk[(g, "act3B")])
                    for h in range(H8):
                        if first:
                            inst = t.matmul(bU[:, h * 64:(h + 1) * 64], ATm[h][:], Etm[h][:],
                                            start=True, stop=True)
                        else:
                            t.matmul(bU[:, h * 64:(h + 1) * 64], qsT[h][:, tsl], hkb[h][:],
                                     start=True, stop=False)
                            inst = t.matmul(bU[:, h * 64:(h + 1) * 64], ATm[h][:], Etm[h][:],
                                            start=False, stop=True)
                        s_pe.inc(inst)
                    mk[(g, "pe3")] = s_pe.mark
                on("tensor", pe3)

                def dve3(v):
                    Wdve(v, mk[(g, "pe3")])
                    Wdve(v, mk[(g, "act3B")])
                    for h in range(H8):
                        inst = v.tensor_mul(okb[h][:], bU[:, h * 64:(h + 1) * 64], rD1[h][:])
                        s_dve.inc(inst)
                    mk[(g, "dve3")] = s_dve.mark
                on("vector", dve3)

                def act4(s):
                    Wact(s, mk[(g, "dve3")])
                    for h in range(H8):
                        s_act.inc(s.activation(eob[h][:], okb[h][:], AF.Exp))
                    mk[(g, "act4")] = s_act.mark
                on("scalar", act4)

                def dve4(v):
                    Wdve(v, mk[(g, "act4")])
                    for h in range(H8):
                        s_dve.inc(v.reduce_sum(scl[:, 4 * h:4 * h + 1], eob[h][:], axis=AX.X))
                        v.wait_ge(s_dve.h, s_dve.n)
                        s_dve.inc(v.reciprocal(scl[:, 4 * h + 1:4 * h + 2], scl[:, 4 * h:4 * h + 1]))
                        v.wait_ge(s_dve.h, s_dve.n)
                        inst = v.scalar_tensor_tensor(pqb[h][:], eob[h][:],
                                                      scl[:, 4 * h + 1:4 * h + 2],
                                                      rD1[h][:], ALU.mult, ALU.mult)
                        s_dve.inc(inst)
                    mk[(g, "dve4")] = s_dve.mark
                on("vector", dve4)

                def pe4(t):
                    Wpe(t, mk[(g, "dve4")])
                    Wpe(t, mk.get((g - 1, "act8")))
                    for h in range(H8):
                        inst = t.transpose(bT2[0:64, h * 128:(h + 1) * 128], pqb[h][:], ident[:])
                        s_pe.inc(inst)
                    mk[(g, "pe4")] = s_pe.mark
                on("tensor", pe4)

                def act5(s):
                    Wact(s, mk[(g, "pe4")])
                    for h in range(H8):
                        s_act.inc(s.copy(pqT[h][:], bT2[0:64, h * 128:(h + 1) * 128]))
                    mk[(g, "act5")] = s_act.mark
                on("scalar", act5)

                def pe5(t):
                    Wpe(t, mk[(g, "act5")])
                    Wpe(t, mk[(g, "dve1")])
                    for h in range(H8):
                        p = bA[h // 4]
                        inst = t.matmul(p[:, (h % 4) * 128:(h % 4) * 128 + 128],
                                        Emt[h][:], pqT[h][:], start=True, stop=True)
                        s_pe.inc(inst)
                    mk[(g, "pe5")] = s_pe.mark
                on("tensor", pe5)

                def dve5(v):
                    Wdve(v, mk[(g, "pe5")])
                    for h in range(H8):
                        p = bA[h // 4]
                        inst = v.tensor_mul(BTm[h][:], p[:, (h % 4) * 128:(h % 4) * 128 + 128],
                                            mut[:])
                        s_dve.inc(inst)
                    mk[(g, "dve5")] = s_dve.mark
                on("vector", dve5)

                def pe6(t):
                    Wpe(t, mk[(g, "dve5")])
                    Wpe(t, mk[(g, "dve2B")])
                    for h in range(H8):
                        p = bA[h // 4]
                        reg = p[:, (h % 4) * 128:(h % 4) * 128 + 128]
                        if first:
                            inst = t.matmul(reg, BTm[h][:], v_t[:, a, h * 128:(h + 1) * 128],
                                            start=True, stop=True)
                        else:
                            t.matmul(reg, BTm[h][:], v_t[:, a, h * 128:(h + 1) * 128],
                                     start=True, stop=False)
                            inst = t.matmul(reg, pqT[h][:], hvb[h][(g - 1) % 2][:],
                                            start=False, stop=True)
                        s_pe.inc(inst)
                    mk[(g, "pe6")] = s_pe.mark
                on("tensor", pe6)

                def act6(s):
                    Wact(s, mk[(g, "pe6")])
                    for h in range(H8):
                        p = bA[h // 4]
                        s_act.inc(s.copy(ovs[h][:], p[:, (h % 4) * 128:(h % 4) * 128 + 128]))
                    mk[(g, "act6")] = s_act.mark
                on("scalar", act6)

                def dve6(v):
                    Wdve(v, mk[(g, "act6")])
                    for h in range(H8):
                        inst = v.tensor_mul(sq2[h][:], ovs[h][:], ovs[h][:])
                    s_dve.inc(inst)
                    v.wait_ge(s_dve.h, s_dve.n)
                    for h in range(H8):
                        inst = v.reduce_sum(scl[:, 4 * h + 2:4 * h + 3], sq2[h][:], axis=AX.X)
                    s_dve.inc(inst)
                    mk[(g, "dve6")] = s_dve.mark
                on("vector", dve6)

                def act7(s):
                    Wact(s, mk[(g, "dve6")])
                    for h in range(H8):
                        inst = s.activation(scl[:, 4 * h + 3:4 * h + 4], scl[:, 4 * h + 2:4 * h + 3],
                                            AF.Ln, bias=cst[:, 0:1], scale=1.0 / DV)
                    s_act.inc(inst)
                    s.wait_ge(s_act.h, s_act.n)
                    for h in range(H8):
                        inst = s.activation(scl[:, 4 * h + 3:4 * h + 4],
                                            scl[:, 4 * h + 3:4 * h + 4], AF.Exp, scale=-0.5)
                    s_act.inc(inst)
                    mk[(g, "act7")] = s_act.mark
                on("scalar", act7)

                def dve7(v):
                    Wdve(v, mk[(g, "act7")])
                    Wdve(v, mk[("p2sg", ch)])
                    for h in range(H8):
                        inst = v.scalar_tensor_tensor(sq2[h][:], ovs[h][:],
                                                      scl[:, 4 * h + 3:4 * h + 4],
                                                      sgi[ch % 2][:, a, h * 128:(h + 1) * 128],
                                                      ALU.mult, ALU.mult)
                    s_dve.inc(inst)
                    v.wait_ge(s_dve.h, s_dve.n)
                    for h in range(H8):
                        inst = v.tensor_copy(BTm[h][:], sq2[h][:])
                    s_dve.inc(inst)
                    mk[(g, "dve7")] = s_dve.mark
                on("vector", dve7)

                def pe7(t):
                    Wpe(t, mk[(g, "dve7")])
                    Wpe(t, mk[(g, "act5")])
                    for h in range(H8):
                        inst = t.transpose(bT2[:, h * 128:(h + 1) * 128], BTm[h][:], ident[:])
                        s_pe.inc(inst)
                    mk[(g, "pe7")] = s_pe.mark
                on("tensor", pe7)

                def act8(s):
                    Wact(s, mk[(g, "pe7")])
                    if g >= 4:
                        Wact(s, mk[("p2o3", g - 4)])
                    for h in range(H8):
                        s_act.inc(s.copy(o3T[a][:, h, :], bT2[:, h * 128:(h + 1) * 128]))
                    mk[(g, "act8")] = s_act.mark
                on("scalar", act8)

            for ch in range(NCH):
                emit_p2_sp(ch)
                emit_proj(ch)
                for a in range(NA):
                    scan_chunk(ch, a)

                def sp_o3(sp, ch=ch):
                    half, j = ch // 4, ch % 4
                    for a in range(NA):
                        Wsp(sp, mk[(ch * NA + a, "act8")])
                        s = so3a[a]
                        s.inc(sp.dma_start(
                            out=o3s_d[j, :, half * 512 + a * 128:half * 512 + (a + 1) * 128]
                                .rearrange("(h p) t -> p h t", p=128),
                            in_=o3T[a][:]), 16)
                        mk[("p2o3", ch * NA + a)] = s.mark
                on("sync", sp_o3)

                if ch >= 4:
                    def pl_cc(g_, ch=ch):
                        j = ch % 4
                        for a in range(NA):
                            Wpl(g_, mk[("p2o3", ch * NA + a)])
                            Wpl(g_, mk[("p2o3", j * NA + a)])
                        s = scc[j]
                        s.inc(g_.collective_compute(
                            "AllGather", ALU.bypass,
                            replica_groups=[[0, 1], [2, 3], [4, 5], [6, 7]],
                            ins=[o3s_d[j].opt()],
                            outs=[o3m_d[j].opt()]))
                        mk[("a2a", j)] = s.mark
                    on("gpsimd", pl_cc)

        if PHASES < 4:
            def sp_end2(sp):
                for s in ALL_SEMS[3:]:
                    Wsp(sp, s.mark)
            on("sync", sp_end2)
            aw.close()
            return nc

        aw.close()
        phase_barrier()
        with ExitStack() as pc1:
            wos = pc1.enter_context(nc.sbuf_tensor("wos", [128, 16, 2048], BF16))
            o3f = [pc1.enter_context(nc.sbuf_tensor(f"o3f{i}", [128, 16, 512], BF16)) for i in range(2)]
            xo = [pc1.enter_context(nc.sbuf_tensor(f"xo{i}", [128, D], F32)) for i in range(2)]
            x1b = [pc1.enter_context(nc.sbuf_tensor(f"x1b{i}", [128, D], F32)) for i in range(2)]
            x1l = [pc1.enter_context(nc.sbuf_tensor(f"x1l{i}", [128, D], BF16)) for i in range(2)]
            xtT1 = [pc1.enter_context(nc.sbuf_tensor(f"xtT1{i}", [128, 16, 128], BF16)) for i in range(2)]
            st2 = pc1.enter_context(nc.sbuf_tensor("pc1st", [128, 8], F32))
            sq1 = pc1.enter_context(nc.sbuf_tensor("pc1sq", [128, D], F32))
            pw = [pc1.enter_context(nc.psum_tensor(f"pw{i}", [128, 512], F32)) for i in range(4)]
            ptr = [pc1.enter_context(nc.psum_tensor(f"ptr{i}", [128, 512], BF16)) for i in range(2)]

            def pc1_w(g):
                s_wo.inc(g.dma_start(out=wos[:], in_=wo_ap.rearrange("(hc p) n -> p hc n", p=128)), 16)
            on("gpsimd", pc1_w)
            wo_mark = s_wo.mark

            def pc1_gnw(v):
                Wdve(v, wo_mark)
                Wdve(v, pre_mark)
                for hc in range(16):
                    s_dve.inc(v.tensor_scalar_mul(wos[:, hc, :], wos[:, hc, :], gnw[:]))
            on("vector", pc1_gnw)
            wognw_mark = s_dve.mark

            for j in range(4):
                def sp_o3f(sp, j=j):
                    Wsp(sp, mk[("a2a", j)])
                    Wsp(sp, pre_mark)
                    if j >= 2:
                        Wsp(sp, mk[("pc1mm", (j - 2) * 4 + 3, 3)])
                    with sp.register(f"hgo{j}") as hgr:
                        sp.reg_load(hgr, hgv[0:1, 0:1])
                        off = sp.snap(hgr, min_val=0, max_val=1)
                        s = so3f[j % 2]
                        s.inc(sp.dma_start(
                            out=o3f[j % 2][:],
                            in_=o3m_d[j].rearrange("(hc p) t -> p hc t", p=128)[:, :, bass.ts(off, 512)]), 16)
                    mk[("pc1o3", j)] = so3f[j % 2].mark
                on("sync", sp_o3f)

                for tc4 in range(4):
                    tc = j * 4 + tc4

                    def sp_x(sp, tc=tc):
                        if tc >= 2:
                            Wsp(sp, mk[("pc1x1", tc - 2)])
                        s = sxo[tc % 2]
                        s.inc(sp.dma_start(out=xo[tc % 2][:],
                                           in_=xown_ap[tc * 128:(tc + 1) * 128, :]), 16)
                        mk[("pc1ldx", tc)] = s.mark
                    on("sync", sp_x)

                    def pe_wo(t, tc=tc, j=j, tc4=tc4):
                        Wpe(t, mk[("pc1o3", j)])
                        Wpe(t, wognw_mark)
                        for dg in range(4):
                            p = pw[dg]
                            if tc >= 1:
                                Wpe(t, mk[("pc1ev", tc - 1, dg)])
                            for hc in range(16):
                                inst = t.matmul(p[:], o3f[j % 2][:, hc, tc4 * 128:(tc4 + 1) * 128],
                                                wos[:, hc, dg * 512:(dg + 1) * 512],
                                                start=(hc == 0), stop=(hc == 15))
                            s_pe.inc(inst)
                            mk[("pc1mm", tc, dg)] = s_pe.mark
                    on("tensor", pe_wo)

                    def dve_x1(v, tc=tc):
                        Wdve(v, mk[("pc1ldx", tc)])
                        if tc >= 2:
                            Wdve(v, mk[("pc1x1st", tc - 2)])
                        for dg in range(4):
                            Wdve(v, mk[("pc1mm", tc, dg)])
                            inst = v.tensor_add(x1b[tc % 2][:, dg * 512:(dg + 1) * 512],
                                                xo[tc % 2][:, dg * 512:(dg + 1) * 512],
                                                pw[dg][:])
                            s_dve.inc(inst)
                            mk[("pc1ev", tc, dg)] = s_dve.mark
                        mk[("pc1x1", tc)] = s_dve.mark
                    on("vector", dve_x1)

                    def sp_x1(sp, tc=tc):
                        Wsp(sp, mk[("pc1x1", tc)])
                        s = sx1[tc % 2]
                        s.inc(sp.dma_start(out=x1_d[tc * 128:(tc + 1) * 128, :],
                                           in_=x1b[tc % 2][:]), 16)
                        mk[("pc1x1st", tc)] = s.mark
                    on("sync", sp_x1)

                    def dve_lna(v, tc=tc):
                        s_dve.inc(v.reduce_sum(st2[:, 0:1], x1b[tc % 2][:], axis=AX.X))
                        v.wait_ge(s_dve.h, s_dve.n)
                        s_dve.inc(v.tensor_scalar_mul(st2[:, 1:2], st2[:, 0:1], 1.0 / D))
                        v.wait_ge(s_dve.h, s_dve.n)
                        s_dve.inc(v.tensor_mul(st2[:, 2:3], st2[:, 1:2], st2[:, 1:2]))
                        mk[("pc1lna", tc)] = s_dve.mark
                    on("vector", dve_lna)

                    def dve_lnb(v, tc=tc):
                        s_dve.inc(v.tensor_mul(sq1[:], x1b[tc % 2][:], x1b[tc % 2][:]))
                        v.wait_ge(s_dve.h, s_dve.n)
                        s_dve.inc(v.reduce_sum(st2[:, 0:1], sq1[:], axis=AX.X))
                        v.wait_ge(s_dve.h, s_dve.n)
                        s_dve.inc(v.scalar_tensor_tensor(st2[:, 3:4], st2[:, 0:1], 1.0 / D,
                                                         st2[:, 2:3], ALU.mult, ALU.subtract))
                        mk[("pc1lnb", tc)] = s_dve.mark
                    on("vector", dve_lnb)

                    def act_lnb(s, tc=tc):
                        Wact(s, mk[("pc1lnb", tc)])
                        s_act.inc(s.activation(st2[:, 5:6], st2[:, 3:4], AF.Ln, bias=cst[:, 0:1]))
                        s.wait_ge(s_act.h, s_act.n)
                        s_act.inc(s.activation(st2[:, 6:7], st2[:, 5:6], AF.Exp, scale=-0.5))
                        mk[("pc1rstd", tc)] = s_act.mark
                    on("scalar", act_lnb)

                    def dve_lnc(v, tc=tc):
                        Wdve(v, mk[("pc1rstd", tc)])
                        if tc >= 2:
                            Wdve(v, mk[("pc1tr", tc - 2, 3)])
                        s_dve.inc(v.tensor_scalar(x1l[tc % 2][:], x1b[tc % 2][:],
                                                  st2[:, 1:2], st2[:, 6:7],
                                                  ALU.subtract, ALU.mult))
                        mk[("pc1ln", tc)] = s_dve.mark
                    on("vector", dve_lnc)

                    for k in range(4):
                        def pe_t1(t, tc=tc, k=k):
                            Wpe(t, mk[("pc1ln", tc)])
                            gi = tc * 4 + k
                            if gi >= 2:
                                Wpe(t, mk[("pc1te", gi - 2)])
                            p = ptr[gi % 2]
                            for jj in range(4):
                                dc = k * 4 + jj
                                inst = t.transpose(p[:, jj * 128:(jj + 1) * 128],
                                                   x1l[tc % 2][:, dc * 128:(dc + 1) * 128],
                                                   ident[:])
                            s_pe.inc(inst)
                            mk[("pc1tr", tc, k)] = s_pe.mark
                        on("tensor", pe_t1)

                        def act_e1(s, tc=tc, k=k):
                            gi = tc * 4 + k
                            Wact(s, mk[("pc1tr", tc, k)])
                            if tc >= 2:
                                Wact(s, mk[("pc1xst", tc - 2)])
                            s_act.inc(s.copy(xtT1[tc % 2][:, k * 4:(k + 1) * 4, :],
                                             ptr[gi % 2][:].rearrange("p (j t) -> p j t", j=4)))
                            mk[("pc1te", gi)] = s_act.mark
                        on("scalar", act_e1)

                    def sp_xt1(sp, tc=tc):
                        Wsp(sp, mk[("pc1te", tc * 4 + 3)])
                        s = sxt1[tc % 2]
                        s.inc(sp.dma_start(out=x1lnT_d[:, :, tc * 128:(tc + 1) * 128],
                                           in_=xtT1[tc % 2][:]), 16)
                        mk[("pc1xst", tc)] = s.mark
                    on("sync", sp_xt1)

        if PHASES < 5:
            def sp_end3(sp):
                for s in ALL_SEMS[3:]:
                    Wsp(sp, s.mark)
            on("sync", sp_end3)
            return nc

        phase_barrier()
        with ExitStack() as pcb:
            gw = pcb.enter_context(nc.sbuf_tensor("gw", [128, 16, 8], BF16))
            xl2 = [pcb.enter_context(nc.sbuf_tensor(f"xl2{i}", [128, 16, 128], BF16)) for i in range(2)]
            sc2 = pcb.enter_context(nc.sbuf_tensor("sc2", [128, 40], F32))
            wrow = pcb.enter_context(nc.sbuf_tensor("wrow", [8, 128], F32))
            pl2 = pcb.enter_context(nc.psum_tensor("pl2", [128, 8], F32))
            pt2 = pcb.enter_context(nc.psum_tensor("pt2", [8, 128], F32))

            def pcb_w(g):
                s_wgt.inc(g.dma_start(out=gw[:], in_=gwT_ap.rearrange("(dc p) e -> p dc e", p=128)), 16)
            on("gpsimd", pcb_w)
            gw_mark = s_wgt.mark

            for tc in range(16):
                def sp_l2(sp, tc=tc):
                    Wsp(sp, mk[("pc1xst", tc)])
                    if tc >= 2:
                        Wsp(sp, mk[("pcb_mm", tc - 2)])
                    s = sxlb[tc % 2]
                    s.inc(sp.dma_start(out=xl2[tc % 2][:],
                                       in_=x1lnT_d[:, :, tc * 128:(tc + 1) * 128]), 16)
                    mk[("pcb_ld", tc)] = s.mark
                on("sync", sp_l2)

                def pe_l2(t, tc=tc):
                    Wpe(t, mk[("pcb_ld", tc)])
                    Wpe(t, gw_mark)
                    if tc >= 1:
                        Wpe(t, mk[("pcb_sig", tc - 1)])
                    for dc in range(16):
                        inst = t.matmul(pl2[:], xl2[tc % 2][:, dc, :], gw[:, dc, :],
                                        start=(dc == 0), stop=(dc == 15))
                    s_pe.inc(inst)
                    mk[("pcb_mm", tc)] = s_pe.mark
                on("tensor", pe_l2)

                def act_l2(s, tc=tc):
                    Wact(s, mk[("pcb_mm", tc)])
                    if tc >= 1:
                        Wact(s, mk[("pcb_ws", tc - 1)])
                    s_act.inc(s.activation(sc2[:, 0:8], pl2[:], AF.Sigmoid))
                    mk[("pcb_sig", tc)] = s_act.mark
                on("scalar", act_l2)

                def dve_l2(v, tc=tc):
                    Wdve(v, mk[("pcb_sig", tc)])
                    if tc >= 1:
                        Wdve(v, mk[("pcb_tr", tc - 1)])
                    s_dve.inc(v.reduce_max(sc2[:, 8:9], sc2[:, 0:8], axis=AX.X))
                    v.wait_ge(s_dve.h, s_dve.n)
                    s_dve.inc(v.tensor_scalar(sc2[:, 16:24], sc2[:, 0:8], sc2[:, 8:9], None, ALU.is_equal))
                    v.wait_ge(s_dve.h, s_dve.n)
                    s_dve.inc(v.scalar_tensor_tensor(sc2[:, 24:32], sc2[:, 16:24], -1e9,
                                                     sc2[:, 0:8], ALU.mult, ALU.add))
                    v.wait_ge(s_dve.h, s_dve.n)
                    s_dve.inc(v.reduce_max(sc2[:, 9:10], sc2[:, 24:32], axis=AX.X))
                    v.wait_ge(s_dve.h, s_dve.n)
                    s_dve.inc(v.tensor_scalar(sc2[:, 24:32], sc2[:, 24:32], sc2[:, 9:10], None, ALU.is_equal))
                    v.wait_ge(s_dve.h, s_dve.n)
                    s_dve.inc(v.tensor_add(sc2[:, 16:24], sc2[:, 16:24], sc2[:, 24:32]))
                    v.wait_ge(s_dve.h, s_dve.n)
                    s_dve.inc(v.tensor_mul(sc2[:, 32:40], sc2[:, 0:8], sc2[:, 16:24]))
                    mk[("pcb_ws", tc)] = s_dve.mark
                on("vector", dve_l2)

                def pe_t2(t, tc=tc):
                    Wpe(t, mk[("pcb_ws", tc)])
                    if tc >= 1:
                        Wpe(t, mk[("pcb_ev", tc - 1)])
                    s_pe.inc(t.transpose(pt2[:], sc2[:, 32:40], id32[:]))
                    mk[("pcb_tr", tc)] = s_pe.mark
                on("tensor", pe_t2)

                def act_t2(s, tc=tc):
                    Wact(s, mk[("pcb_tr", tc)])
                    if tc >= 1:
                        Wact(s, mk[("pcb_st", tc - 1)])
                    s_act.inc(s.copy(wrow[:], pt2[:]))
                    mk[("pcb_ev", tc)] = s_act.mark
                on("scalar", act_t2)

                def sp_ws(sp, tc=tc):
                    Wsp(sp, mk[("pcb_ev", tc)])
                    swse.inc(sp.dma_start(out=wsel_d[:, tc * 128:(tc + 1) * 128], in_=wrow[:]), 16)
                    mk[("pcb_st", tc)] = swse.mark
                on("sync", sp_ws)
            wsel_done = swse.mark

        if PHASES < 6:
            def sp_end4(sp):
                for s in ALL_SEMS[3:]:
                    Wsp(sp, s.mark)
            on("sync", sp_end4)
            return nc

        phase_barrier()
        with ExitStack() as pc2:
            sw1 = pc2.enter_context(nc.sbuf_tensor("sw1", [128, 16, HID], BF16))
            sw3 = pc2.enter_context(nc.sbuf_tensor("sw3", [128, 16, HID], BF16))
            sw2 = pc2.enter_context(nc.sbuf_tensor("sw2", [128, 8, D], BF16))
            xw = pc2.enter_context(nc.sbuf_tensor("xw", [128, 16, 1024], BF16))
            acc = pc2.enter_context(nc.sbuf_tensor("acc", [128, 8, D], BF16))
            gat = pc2.enter_context(nc.sbuf_tensor("gat", [128, 8, 512], BF16))
            crow = pc2.enter_context(nc.sbuf_tensor("crow", [1, 1024], F32))
            cbc = pc2.enter_context(nc.sbuf_tensor("cbc", [128, 1024], F32))
            ca_t = pc2.enter_context(nc.sbuf_tensor("ca_t", [128, 512], F32))
            cb_t = pc2.enter_context(nc.sbuf_tensor("cb_t", [128, 512], F32))
            sg_t = pc2.enter_context(nc.sbuf_tensor("sg_t", [128, 512], F32))
            t1_t = pc2.enter_context(nc.sbuf_tensor("t1_t", [128, 512], F32))
            xfin = pc2.enter_context(nc.sbuf_tensor("xfin0", [128, D], F32))
            ofin = pc2.enter_context(nc.sbuf_tensor("ofin0", [128, D], F32))
            pab = [pc2.enter_context(nc.psum_tensor(f"pab{i}", [128, 512], F32)) for i in range(4)]
            pw2 = [pc2.enter_context(nc.psum_tensor(f"pw2{i}", [128, 512], F32)) for i in range(2)]
            pcb2 = pc2.enter_context(nc.psum_tensor("pcb2", [128, 512], F32))

            NW = 2
            EXPS = list(range(NEXP)) + ["sh"]
            prev_ab = [None]
            prev_w2 = [None]
            prev_cab = [None]
            prev_cbcmm = [None]
            cab_hist = [None, None]
            unit_ctr = [0]

            for wv_i in range(NW):
                def sp_xw(sp, wv_i=wv_i):
                    Wsp(sp, mk[("pc1xst", 15)])
                    Wsp(sp, mk[("pc1xst", 14)])
                    if wv_i >= 1:
                        Wsp(sp, prev_ab[0])
                    sxw.inc(sp.dma_start(out=xw[:],
                                         in_=x1lnT_d[:, :, wv_i * 1024:(wv_i + 1) * 1024]), 16)
                    mk[("ffn_xw", wv_i)] = sxw.mark
                on("sync", sp_xw)

                for ei, e in enumerate(EXPS):
                    def pl_w(g, wv_i=wv_i, e=e):
                        Wpl(g, prev_ab[0])
                        if e == "sh":
                            s_w13.inc(g.dma_start(out=sw1[:], in_=ws1T_ap.rearrange("(dc p) h -> p dc h", p=128)), 16)
                            s_w13.inc(g.dma_start(out=sw3[:], in_=ws3T_ap.rearrange("(dc p) h -> p dc h", p=128)), 16)
                        else:
                            s_w13.inc(g.dma_start(out=sw1[:], in_=w1T_ap[e].rearrange("(dc p) h -> p dc h", p=128)), 16)
                            s_w13.inc(g.dma_start(out=sw3[:], in_=w3T_ap[e].rearrange("(dc p) h -> p dc h", p=128)), 16)
                        mk[("ffn_w13", wv_i, e)] = s_w13.mark
                        Wpl(g, prev_w2[0])
                        if e == "sh":
                            s_w2f.inc(g.dma_start(out=sw2[:], in_=ws2T_ap.rearrange("(hc p) n -> p hc n", p=128)), 16)
                        else:
                            s_w2f.inc(g.dma_start(out=sw2[:], in_=w2T_ap[e].rearrange("(hc p) n -> p hc n", p=128)), 16)
                        mk[("ffn_w2", wv_i, e)] = s_w2f.mark
                    on("gpsimd", pl_w)

                    if e != "sh":
                        def sp_c(sp, wv_i=wv_i, e=e):
                            Wsp(sp, wsel_done)
                            Wsp(sp, prev_cbcmm[0])
                            scro.inc(sp.dma_start(out=crow[:],
                                                  in_=wsel_d[e:e + 1, wv_i * 1024:(wv_i + 1) * 1024]), 16)
                            mk[("ffn_crow", wv_i, e)] = scro.mark
                        on("sync", sp_c)

                        for cb_i in range(2):
                            def pe_c(t, wv_i=wv_i, e=e, cb_i=cb_i):
                                Wpe(t, mk[("ffn_crow", wv_i, e)])
                                Wpe(t, pre_mark)
                                if cb_i == 0:
                                    Wpe(t, prev_cab[0])
                                else:
                                    Wpe(t, mk[("ffn_cbcev", wv_i, e, 0)])
                                s_pe.inc(t.matmul(pcb2[:], oner[:],
                                                  crow[:, cb_i * 512:(cb_i + 1) * 512],
                                                  start=True, stop=True))
                                mk[("ffn_cbcmm", wv_i, e, cb_i)] = s_pe.mark
                            on("tensor", pe_c)

                            def act_c(s, wv_i=wv_i, e=e, cb_i=cb_i):
                                Wact(s, mk[("ffn_cbcmm", wv_i, e, cb_i)])
                                s_act.inc(s.copy(cbc[:, cb_i * 512:(cb_i + 1) * 512], pcb2[:]))
                                mk[("ffn_cbcev", wv_i, e, cb_i)] = s_act.mark
                            on("scalar", act_c)
                        prev_cbcmm[0] = mk[("ffn_cbcmm", wv_i, e, 1)]

                    for tg in range(2):
                        for hb in range(8):
                            pr = (unit_ctr[0] % 2) * 2
                            prevcab2 = cab_hist[0]
                            unit_ctr[0] += 1

                            def pe_ab(t, wv_i=wv_i, e=e, tg=tg, hb=hb, pr=pr,
                                      prevcab2=prevcab2):
                                Wpe(t, mk[("ffn_w13", wv_i, e)])
                                Wpe(t, mk[("ffn_xw", wv_i)])
                                Wpe(t, prevcab2)
                                for dc in range(16):
                                    t.matmul(pab[pr][:], sw1[:, dc, hb * 128:(hb + 1) * 128],
                                             xw[:, dc, tg * 512:(tg + 1) * 512],
                                             start=(dc == 0), stop=(dc == 15))
                                for dc in range(16):
                                    inst = t.matmul(pab[pr + 1][:], sw3[:, dc, hb * 128:(hb + 1) * 128],
                                                    xw[:, dc, tg * 512:(tg + 1) * 512],
                                                    start=(dc == 0), stop=(dc == 15))
                                s_pe.inc(inst)
                                mk[("ffn_ab", wv_i, e, tg, hb)] = s_pe.mark
                            on("tensor", pe_ab)

                            def dve_g1(v, wv_i=wv_i, e=e, tg=tg, hb=hb, pr=pr):
                                Wdve(v, mk[("ffn_ab", wv_i, e, tg, hb)])
                                if e != "sh":
                                    Wdve(v, mk[("ffn_cbcev", wv_i, e, 1)])
                                    v.tensor_mul(ca_t[:], pab[pr][:], cbc[:, tg * 512:(tg + 1) * 512])
                                    inst = v.tensor_mul(cb_t[:], pab[pr + 1][:], cbc[:, tg * 512:(tg + 1) * 512])
                                else:
                                    v.tensor_copy(ca_t[:], pab[pr][:])
                                    inst = v.tensor_copy(cb_t[:], pab[pr + 1][:])
                                s_dve.inc(inst)
                                mk[("ffn_cab", wv_i, e, tg, hb)] = s_dve.mark
                            on("vector", dve_g1)
                            cab_hist[0], cab_hist[1] = cab_hist[1], mk[("ffn_cab", wv_i, e, tg, hb)]

                            def act_sg(s, wv_i=wv_i, e=e, tg=tg, hb=hb):
                                Wact(s, mk[("ffn_cab", wv_i, e, tg, hb)])
                                s_act.inc(s.activation(sg_t[:], ca_t[:], AF.Silu))
                                mk[("ffn_sg", wv_i, e, tg, hb)] = s_act.mark
                            on("scalar", act_sg)

                            def dve_g2(v, wv_i=wv_i, e=e, tg=tg, hb=hb):
                                Wdve(v, mk[("ffn_sg", wv_i, e, tg, hb)])
                                if tg >= 1:
                                    Wdve(v, mk[("ffn_w2mm", wv_i, e, 15)])
                                inst = v.tensor_mul(gat[:, hb, :], sg_t[:], cb_t[:])
                                s_dve.inc(inst)
                                mk[("ffn_gat", wv_i, e, tg, hb)] = s_dve.mark
                            on("vector", dve_g2)

                        for ts4 in range(4):
                            tci = tg * 4 + ts4
                            for dg in range(4):
                                gi = tci * 4 + dg

                                def pe_w2(t, wv_i=wv_i, e=e, tg=tg, ts4=ts4, gi=gi, dg=dg):
                                    Wpe(t, mk[("ffn_w2", wv_i, e)])
                                    Wpe(t, mk[("ffn_gat", wv_i, e, tg, 7)])
                                    if gi >= 2:
                                        Wpe(t, mk[("ffn_acc", wv_i, e, gi - 2)])
                                    p = pw2[gi % 2]
                                    for hc in range(8):
                                        inst = t.matmul(p[:], gat[:, hc, ts4 * 128:(ts4 + 1) * 128],
                                                        sw2[:, hc, dg * 512:(dg + 1) * 512],
                                                        start=(hc == 0), stop=(hc == 7))
                                    s_pe.inc(inst)
                                    mk[("ffn_w2mm", wv_i, e, gi)] = s_pe.mark
                                on("tensor", pe_w2)

                                def dve_acc(v, wv_i=wv_i, ei=ei, e=e, tci=tci, gi=gi, dg=dg):
                                    Wdve(v, mk[("ffn_w2mm", wv_i, e, gi)])
                                    p = pw2[gi % 2]
                                    dst = acc[:, tci, dg * 512:(dg + 1) * 512]
                                    if ei == 0:
                                        inst = v.tensor_copy(dst, p[:])
                                    else:
                                        inst = v.tensor_add(dst, dst, p[:])
                                    s_dve.inc(inst)
                                    mk[("ffn_acc", wv_i, e, gi)] = s_dve.mark
                                on("vector", dve_acc)

                    prev_ab[0] = mk[("ffn_ab", wv_i, e, 1, 7)]
                    prev_w2[0] = mk[("ffn_w2mm", wv_i, e, 31)]
                    prev_cab[0] = mk[("ffn_cab", wv_i, e, 1, 7)]

                for tci in range(8):
                    tc = wv_i * 8 + tci

                    def sp_xf(sp, tc=tc):
                        if tc >= 1:
                            Wsp(sp, mk[("fin", tc - 1)])
                        sxf.inc(sp.dma_start(out=xfin[:],
                                             in_=x1_d[tc * 128:(tc + 1) * 128, :]), 16)
                        mk[("fin_ld", tc)] = sxf.mark
                    on("sync", sp_xf)

                    def dve_fin(v, tc=tc, tci=tci, wv_i=wv_i):
                        Wdve(v, mk[("fin_ld", tc)])
                        if tc >= 1:
                            Wdve(v, mk[("fin_st", tc - 1)])
                        inst = v.tensor_add(ofin[:], xfin[:], acc[:, tci, :])
                        s_dve.inc(inst)
                        mk[("fin", tc)] = s_dve.mark
                    on("vector", dve_fin)

                    def sp_of(sp, tc=tc):
                        Wsp(sp, mk[("fin", tc)])
                        sout.inc(sp.dma_start(out=out_ap[tc * 128:(tc + 1) * 128, :],
                                              in_=ofin[:]), 16)
                        mk[("fin_st", tc)] = sout.mark
                    on("sync", sp_of)

            def sp_end(sp):
                for s in ALL_SEMS[3:]:
                    Wsp(sp, s.mark)
            on("sync", sp_end)

    return nc


def _host_inputs(inputs, core):
    b, hg = core // 2, core % 2
    f32 = np.float32
    ar = lambda v: np.ascontiguousarray(v)
    x = ar(np.asarray(inputs["L0_attn_hidden_states"][b], f32))
    return {
        "x": x,
        "x_own": ar(x[hg * TOWN:(hg + 1) * TOWN]),
        "wq": ar(np.asarray(inputs["L0_attn_Wq"], f32)[:, hg * 512:(hg + 1) * 512]),
        "wk": ar(np.asarray(inputs["L0_attn_Wk"], f32)[:, hg * 512:(hg + 1) * 512]),
        "ws": ar(np.asarray(inputs["L0_attn_Ws"], f32)[:, hg * 512:(hg + 1) * 512]),
        "wv": ar(np.asarray(inputs["L0_attn_Wv"], f32)[:, hg * 1024:(hg + 1) * 1024]),
        "wg": ar(np.asarray(inputs["L0_attn_Wg"], f32)[:, hg * 1024:(hg + 1) * 1024]),
        "wo": ar(np.asarray(inputs["L0_attn_Wo"], f32)),
        "gnw": ar(np.asarray(inputs["L0_attn_g_norm_weight"], f32).reshape(128, 1)),
        "gatewT": ar(np.asarray(inputs["L0_ffn_gate_w"], f32).T),
        "w1T": ar(np.asarray(inputs["L0_ffn_w1"], f32).transpose(0, 2, 1)),
        "w3T": ar(np.asarray(inputs["L0_ffn_w3"], f32).transpose(0, 2, 1)),
        "w2T": ar(np.asarray(inputs["L0_ffn_w2"], f32).transpose(0, 2, 1)),
        "ws1T": ar(np.asarray(inputs["L0_ffn_ws1"], f32).T),
        "ws3T": ar(np.asarray(inputs["L0_ffn_ws3"], f32).T),
        "ws2T": ar(np.asarray(inputs["L0_ffn_ws2"], f32).T),
        "mask_ut": np.triu(np.ones((128, 128), ml_dtypes.bfloat16)),
        "ident": np.eye(128, dtype=ml_dtypes.bfloat16),
        "ident32": np.eye(128, dtype=np.float32),
        "ones_col": np.ones((128, 1), ml_dtypes.bfloat16),
        "ones_row": np.ones((1, 128), np.float32),
        "constv": np.tile(np.array([[EPS, 1.0]], np.float32), (128, 1)),
        "hgv": np.array([[hg]], np.int32),
    }


def kernel(**inputs):
    nc = build_nc()
    in_maps = [_host_inputs(inputs, c) for c in range(8)]
    res = run_bass_kernel_spmd(nc, in_maps, core_ids=list(range(8)))
    out = np.empty((4, T, D), np.float32)
    for c in range(8):
        b, hg = c // 2, c % 2
        out[b, hg * TOWN:(hg + 1) * TOWN, :] = res.results[c]["out_half"]
    return out


if __name__ == "__main__":
    import time
    t0 = time.time()
    nc = build_nc()
    print("built", time.time() - t0, "n_instructions:", nc.n_instructions())


# version 11
# speedup vs baseline: 1.0929x; 1.0310x over previous
import numpy as np
import ml_dtypes
from contextlib import ExitStack

import concourse.bass as bass
import concourse.mybir as mybir
from concourse.bass_utils import run_bass_kernel_spmd

F32 = mybir.dt.float32
BF16 = mybir.dt.bfloat16
I32 = mybir.dt.int32
AF = mybir.ActivationFunctionType
ALU = mybir.AluOpType
AX = mybir.AxisListType

T, D = 4096, 2048
H8, DK, DV, M = 8, 64, 128, 64
C = 128
CH = 512
NCH = T // CH
NA = CH // C
TOWN = 2048
EPS = 1e-5
NEXP = 8
HID = 1024
PHASES = 6
DBG_DUMP = False
P0_SERIAL = False


class Sem:
    def __init__(self, ctx, nc, name):
        self.h = ctx.enter_context(nc.semaphore(name))
        self.n = 0

    def inc(self, inst, k=1):
        inst.then_inc(self.h, k)
        self.n += k
        return self.n

    @property
    def mark(self):
        return (self, self.n)


class Waiter:

    def __init__(self):
        self.seen = {}

    def __call__(self, eng, mark):
        if mark is None:
            return
        sem, val = mark
        if val <= 0:
            return
        if self.seen.get(id(sem), -1) >= val:
            return
        eng.wait_ge(sem.h, val)
        self.seen[id(sem)] = val


def build_nc():
    nc = bass.Bass("TRN2", target_bir_lowering=False, debug=False,
                   detect_race_conditions=False)

    def din(name, shape, dtype=F32):
        return nc.dram_tensor(name, list(shape), dtype, kind="ExternalInput").ap()

    x_ap = din("x", (T, D))
    xown_ap = din("x_own", (TOWN, D))
    wq_ap = din("wq", (D, 512))
    wk_ap = din("wk", (D, 512))
    ws_ap = din("ws", (D, 512))
    wv_ap = din("wv", (D, 1024))
    wg_ap = din("wg", (D, 1024))
    wo_ap = din("wo", (D, D))
    gnw_ap = din("gnw", (128, 1))
    gwT_ap = din("gatewT", (D, 8))
    w1S_ap = din("w1S", (NEXP, 128, 8, 16, 128))
    w3S_ap = din("w3S", (NEXP, 128, 8, 16, 128))
    w2T_ap = din("w2T", (NEXP, HID, D))
    ws1S_ap = din("ws1S", (128, 8, 16, 128))
    ws3S_ap = din("ws3S", (128, 8, 16, 128))
    ws2T_ap = din("ws2T", (HID, D))
    mut_ap = din("mask_ut", (128, 128), BF16)
    ident_ap = din("ident", (128, 128), BF16)
    id32_ap = din("ident32", (128, 128))
    onec_ap = din("ones_col", (128, 1), BF16)
    oner_ap = din("ones_row", (1, 128))
    cst_ap = din("constv", (128, 2))
    hgv_ap = din("hgv", (1, 1), I32)

    out_ap = nc.dram_tensor("out_half", [TOWN, D], F32, kind="ExternalOutput").ap()

    dbk = "ExternalOutput" if DBG_DUMP else "Internal"
    xlnT_d = nc.dram_tensor("xlnT_d", [128, 16, T], BF16, kind=dbk).ap()
    sgg_d = nc.dram_tensor("sgg_d", [T, 1024], BF16, kind=dbk).ap()
    o3s_d = nc.dram_tensor("o3s_d", [4, 1024, 1024], BF16, kind="Internal").ap()
    o3m_d = nc.dram_tensor("o3m_d", [4, 2048, 1024], BF16, kind="Internal").ap()
    x1_d = nc.dram_tensor("x1_d", [TOWN, D], F32, kind=dbk).ap()
    x1lnT_d = nc.dram_tensor("x1lnT_d", [128, 16, TOWN], BF16, kind=dbk).ap()
    wsel_d = nc.dram_tensor("wsel_d", [8, TOWN], F32, kind=dbk).ap()

    ctx = ExitStack()
    with ctx:
        s_pe = Sem(ctx, nc, "s_pe")
        s_act = Sem(ctx, nc, "s_act")
        s_dve = Sem(ctx, nc, "s_dve")
        s_pre = Sem(ctx, nc, "s_pre")
        sx = [Sem(ctx, nc, f"sx{i}") for i in range(2)]
        sxt = [Sem(ctx, nc, f"sxt{i}") for i in range(2)]
        sxl1 = [Sem(ctx, nc, f"sxl1_{i}") for i in range(2)]
        ssg = [Sem(ctx, nc, f"ssg{i}") for i in range(2)]
        sxl2 = [Sem(ctx, nc, f"sxl2_{i}") for i in range(2)]
        ssgi = Sem(ctx, nc, "ssgi")
        so3a = [Sem(ctx, nc, f"so3a{i}") for i in range(NA)]
        scc = [Sem(ctx, nc, f"scc{i}") for i in range(4)]
        so3f = [Sem(ctx, nc, f"so3f{i}") for i in range(2)]
        sxo = [Sem(ctx, nc, f"sxo{i}") for i in range(2)]
        sx1 = [Sem(ctx, nc, f"sx1_{i}") for i in range(2)]
        sxt1 = [Sem(ctx, nc, f"sxt1_{i}") for i in range(2)]
        sxlb = [Sem(ctx, nc, f"sxlb{i}") for i in range(2)]
        swse = Sem(ctx, nc, "swse")
        sxw = Sem(ctx, nc, "sxw")
        scro = Sem(ctx, nc, "scro")
        sxf = Sem(ctx, nc, "sxf")
        sout = Sem(ctx, nc, "sout")
        s_wat = Sem(ctx, nc, "s_wat")
        s_wg = Sem(ctx, nc, "s_wg")
        s_wo = Sem(ctx, nc, "s_wo")
        s_wgt = Sem(ctx, nc, "s_wgt")
        s_w13 = Sem(ctx, nc, "s_w13")
        s_w2f = Sem(ctx, nc, "s_w2f")

        ALL_SEMS = [s_pe, s_act, s_dve, s_pre, *sx, *sxt, *sxl1, *ssg, *sxl2,
                    ssgi, *so3a, *scc, *so3f, *sxo, *sx1, *sxt1, *sxlb, swse,
                    sxw, scro, sxf, sout, s_wat, s_wg, s_wo, s_wgt, s_w13, s_w2f]

        Wpe, Wact, Wdve, Wsp, Wpl = Waiter(), Waiter(), Waiter(), Waiter(), Waiter()

        mut = ctx.enter_context(nc.sbuf_tensor("mut_s", [128, 128], BF16))
        ident = ctx.enter_context(nc.sbuf_tensor("ident_s", [128, 128], BF16))
        id32 = ctx.enter_context(nc.sbuf_tensor("id32_s", [128, 128], F32))
        onec = ctx.enter_context(nc.sbuf_tensor("onec_s", [128, 1], BF16))
        oner = ctx.enter_context(nc.sbuf_tensor("oner_s", [1, 128], F32))
        gnw = ctx.enter_context(nc.sbuf_tensor("gnw_s", [128, 1], F32))
        cst = ctx.enter_context(nc.sbuf_tensor("cst_s", [128, 2], F32))
        hgv = ctx.enter_context(nc.sbuf_tensor("hgv_s", [1, 1], I32))

        block = ctx.enter_context(nc.Block())

        def on(engname, fn):
            getattr(block, engname)(fn)

        mk = {}

        def phase_barrier():
            cur = [s.mark for s in ALL_SEMS]

            def mkf(wtr):
                def f(eng):
                    for m in cur:
                        wtr(eng, m)
                return f
            on("tensor", mkf(Wpe))
            on("scalar", mkf(Wact))
            on("vector", mkf(Wdve))
            on("sync", mkf(Wsp))
            on("gpsimd", mkf(Wpl))

        def _pre(sp):
            for dst, src in ((mut, mut_ap), (ident, ident_ap), (id32, id32_ap),
                             (onec, onec_ap), (oner, oner_ap), (gnw, gnw_ap),
                             (cst, cst_ap), (hgv, hgv_ap)):
                s_pre.inc(sp.dma_start(out=dst[:], in_=src), 16)
        on("sync", _pre)
        pre_mark = s_pre.mark

        for _en, _W in (("tensor", Wpe), ("scalar", Wact), ("vector", Wdve),
                        ("gpsimd", Wpl)):
            def _pw(eng, _W=_W):
                _W(eng, pre_mark)
            on(_en, _pw)

        aw = ExitStack()
        wqs = aw.enter_context(nc.sbuf_tensor("wqs", [128, 16, 512], BF16))
        wks = aw.enter_context(nc.sbuf_tensor("wks", [128, 16, 512], BF16))
        wss = aw.enter_context(nc.sbuf_tensor("wss", [128, 16, 512], BF16))
        wvs = aw.enter_context(nc.sbuf_tensor("wvs", [128, 16, 1024], BF16))

        def p2_w(g):
            s_wat.inc(g.dma_start(out=wqs[:], in_=wq_ap.rearrange("(dc p) n -> p dc n", p=128)), 16)
            s_wat.inc(g.dma_start(out=wks[:], in_=wk_ap.rearrange("(dc p) n -> p dc n", p=128)), 16)
            s_wat.inc(g.dma_start(out=wss[:], in_=ws_ap.rearrange("(dc p) n -> p dc n", p=128)), 16)
            s_wat.inc(g.dma_start(out=wvs[:], in_=wv_ap.rearrange("(dc p) n -> p dc n", p=128)), 16)
        on("gpsimd", p2_w)
        wat_mark = s_wat.mark

        with ExitStack() as p0:
            xb = [p0.enter_context(nc.sbuf_tensor(f"p0x{i}", [128, NA, D], F32)) for i in range(2)]
            xl = [p0.enter_context(nc.sbuf_tensor("p0l0", [128, NA, D], BF16))] * 2
            xtT = [p0.enter_context(nc.sbuf_tensor(f"p0t{i}", [128, 16, CH], BF16)) for i in range(2)]
            st = p0.enter_context(nc.sbuf_tensor("p0st", [128, 8], F32))
            sqs = p0.enter_context(nc.sbuf_tensor("p0sq", [128, D], F32))
            pt = [p0.enter_context(nc.psum_tensor(f"p0p{i}", [128, 512], BF16)) for i in range(2)]

            for ch in range(NCH):
                if P0_SERIAL and ch >= 1:
                    cur = [s.mark for s in (s_pe, s_act, s_dve)]

                    def bar(eng, cur=cur, W=None):
                        for m in cur:
                            W(eng, m)
                    on("sync", lambda sp, cur=cur: [Wsp(sp, m) for m in cur])
                    on("vector", lambda v, cur=cur: [Wdve(v, m) for m in cur])
                    on("scalar", lambda s, cur=cur: [Wact(s, m) for m in cur])
                    on("tensor", lambda t, cur=cur: [Wpe(t, m) for m in cur])

                def sp_load(sp, ch=ch):
                    Wsp(sp, pre_mark)
                    if ch >= 2:
                        Wsp(sp, mk[("p0c", (ch - 2) * NA + NA - 1)])
                    s = sx[ch % 2]
                    s.inc(sp.dma_start(
                        out=xb[ch % 2][:],
                        in_=x_ap[ch * CH:(ch + 1) * CH, :].rearrange("(a p) d -> p a d", p=128)), 16)
                    mk[("p0ld", ch)] = s.mark
                on("sync", sp_load)

                for a in range(NA):
                    i = ch * NA + a

                    def dve_s1(v, ch=ch, a=a, i=i):
                        if i == 0:
                            Wdve(v, pre_mark)
                        Wdve(v, mk[("p0ld", ch)])
                        if ch >= 1:
                            Wdve(v, mk[("p0tr", (ch - 1) * NA + NA - 1, 3)])
                        xa = xb[ch % 2][:, a, :]
                        s_dve.inc(v.reduce_sum(st[:, 0:1], xa, axis=AX.X))
                        v.wait_ge(s_dve.h, s_dve.n)
                        s_dve.inc(v.tensor_scalar_mul(st[:, 1:2], st[:, 0:1], 1.0 / D))
                        v.wait_ge(s_dve.h, s_dve.n)
                        s_dve.inc(v.tensor_mul(st[:, 2:3], st[:, 1:2], st[:, 1:2]))
                        mk[("p0a", i)] = s_dve.mark
                    on("vector", dve_s1)

                    def dve_s2(v, ch=ch, a=a, i=i):
                        s_dve.inc(v.tensor_mul(sqs[:], xb[ch % 2][:, a, :], xb[ch % 2][:, a, :]))
                        v.wait_ge(s_dve.h, s_dve.n)
                        s_dve.inc(v.reduce_sum(st[:, 0:1], sqs[:], axis=AX.X))
                        v.wait_ge(s_dve.h, s_dve.n)
                        s_dve.inc(v.scalar_tensor_tensor(st[:, 3:4], st[:, 0:1], 1.0 / D,
                                                         st[:, 2:3], ALU.mult, ALU.subtract))
                        mk[("p0b", i)] = s_dve.mark
                    on("vector", dve_s2)

                    def act_s2(s, i=i):
                        Wact(s, mk[("p0b", i)])
                        s_act.inc(s.activation(st[:, 5:6], st[:, 3:4], AF.Ln, bias=cst[:, 0:1]))
                        s.wait_ge(s_act.h, s_act.n)
                        s_act.inc(s.activation(st[:, 6:7], st[:, 5:6], AF.Exp, scale=-0.5))
                        mk[("p0r", i)] = s_act.mark
                    on("scalar", act_s2)

                    def dve_s3(v, ch=ch, a=a, i=i):
                        Wdve(v, mk[("p0r", i)])
                        s_dve.inc(v.tensor_scalar(xl[ch % 2][:, a, :], xb[ch % 2][:, a, :],
                                                  st[:, 1:2], st[:, 6:7],
                                                  ALU.subtract, ALU.mult))
                        mk[("p0c", i)] = s_dve.mark
                    on("vector", dve_s3)

                    for k in range(4):
                        gidx = i * 4 + k

                        def pe_tr(t, ch=ch, a=a, i=i, k=k, gidx=gidx):
                            if gidx == 0:
                                Wpe(t, pre_mark)
                            Wpe(t, mk[("p0c", i)])
                            if gidx >= 2:
                                Wpe(t, mk[("p0ev", gidx - 2)])
                            p = pt[gidx % 2]
                            for j in range(4):
                                dc = k * 4 + j
                                inst = t.transpose(p[:, j * 128:(j + 1) * 128],
                                                   xl[ch % 2][:, a, dc * 128:(dc + 1) * 128],
                                                   ident[:])
                            s_pe.inc(inst)
                            mk[("p0tr", i, k)] = s_pe.mark
                        on("tensor", pe_tr)

                        def act_ev(s, ch=ch, a=a, i=i, k=k, gidx=gidx):
                            Wact(s, mk[("p0tr", i, k)])
                            if ch >= 2:
                                Wact(s, mk[("p0st", ch - 2)])
                            s_act.inc(s.copy(
                                xtT[ch % 2][:, k * 4:(k + 1) * 4, a * 128:(a + 1) * 128],
                                pt[gidx % 2][:].rearrange("p (j t) -> p j t", j=4)))
                            mk[("p0ev", gidx)] = s_act.mark
                        on("scalar", act_ev)

                def sp_store(sp, ch=ch):
                    Wsp(sp, mk[("p0ev", (ch * NA + NA - 1) * 4 + 3)])
                    s = sxt[ch % 2]
                    s.inc(sp.dma_start(
                        out=xlnT_d[:, :, ch * CH:(ch + 1) * CH], in_=xtT[ch % 2][:]), 16)
                    mk[("p0st", ch)] = s.mark
                on("sync", sp_store)

        if PHASES < 2:
            def sp_end0(sp):
                for s in ALL_SEMS[3:]:
                    Wsp(sp, s.mark)
            on("sync", sp_end0)
            aw.close()
            return nc

        phase_barrier()
        with ExitStack() as p1:
            wgt = p1.enter_context(nc.sbuf_tensor("p1wg", [128, 16, 1024], BF16))
            xlt = [p1.enter_context(nc.sbuf_tensor(f"p1x{i}", [128, 16, CH], BF16)) for i in range(2)]
            sgo = [p1.enter_context(nc.sbuf_tensor(f"p1o{i}", [128, NA, 1024], BF16)) for i in range(2)]
            sgt = p1.enter_context(nc.sbuf_tensor("p1s", [128, 512], F32))
            pg = [p1.enter_context(nc.psum_tensor(f"p1p{i}", [128, 512], F32)) for i in range(2)]

            def p1_w(g):
                s_wg.inc(g.dma_start(out=wgt[:],
                                     in_=wg_ap.rearrange("(dc p) n -> p dc n", p=128)), 16)
            on("gpsimd", p1_w)
            wg_mark = s_wg.mark

            for ch in range(NCH):
                def sp_l(sp, ch=ch):
                    Wsp(sp, mk[("p0st", ch)])
                    if ch >= 2:
                        Wsp(sp, mk[("p1mm", (ch - 2) * NA * 2 + NA * 2 - 1)])
                    s = sxl1[ch % 2]
                    s.inc(sp.dma_start(out=xlt[ch % 2][:],
                                       in_=xlnT_d[:, :, ch * CH:(ch + 1) * CH]), 16)
                    mk[("p1ld", ch)] = s.mark
                on("sync", sp_l)

                for a in range(NA):
                    for vb in range(2):
                        gi = (ch * NA + a) * 2 + vb

                        def pe_g(t, ch=ch, a=a, vb=vb, gi=gi):
                            Wpe(t, mk[("p1ld", ch)])
                            Wpe(t, wg_mark)
                            if gi >= 2:
                                Wpe(t, mk[("p1ev", gi - 2)])
                            p = pg[gi % 2]
                            for dc in range(16):
                                inst = t.matmul(p[:],
                                                xlt[ch % 2][:, dc, a * 128:(a + 1) * 128],
                                                wgt[:, dc, vb * 512:(vb + 1) * 512],
                                                start=(dc == 0), stop=(dc == 15))
                            s_pe.inc(inst)
                            mk[("p1mm", gi)] = s_pe.mark
                        on("tensor", pe_g)

                        def act_g(s, gi=gi):
                            Wact(s, mk[("p1mm", gi)])
                            if gi >= 1:
                                Wact(s, mk[("p1ev", gi - 1)])
                            s_act.inc(s.activation(sgt[:], pg[gi % 2][:], AF.Sigmoid))
                            mk[("p1sg", gi)] = s_act.mark
                        on("scalar", act_g)

                        def dve_g(v, ch=ch, a=a, vb=vb, gi=gi):
                            Wdve(v, mk[("p1sg", gi)])
                            if ch >= 2:
                                Wdve(v, mk[("p1st", ch - 2)])
                            s_dve.inc(v.tensor_mul(
                                sgo[ch % 2][:, a, vb * 512:(vb + 1) * 512],
                                pg[gi % 2][:], sgt[:]))
                            mk[("p1ev", gi)] = s_dve.mark
                        on("vector", dve_g)

                def sp_s(sp, ch=ch):
                    Wsp(sp, mk[("p1ev", (ch * NA + NA - 1) * 2 + 1)])
                    s = ssg[ch % 2]
                    s.inc(sp.dma_start(
                        out=sgg_d[ch * CH:(ch + 1) * CH, :].rearrange("(a p) n -> p a n", p=128),
                        in_=sgo[ch % 2][:]), 16)
                    mk[("p1st", ch)] = s.mark
                on("sync", sp_s)

        if PHASES < 3:
            def sp_end1(sp):
                for s in ALL_SEMS[3:]:
                    Wsp(sp, s.mark)
            on("sync", sp_end1)
            aw.close()
            return nc

        phase_barrier()
        with ExitStack() as p2:
            xlt2 = [p2.enter_context(nc.sbuf_tensor(f"p2x{i}", [128, 16, CH], BF16)) for i in range(2)]
            sgi = [p2.enter_context(nc.sbuf_tensor("p2g0", [128, NA, 1024], BF16))] * 2
            qsT = [p2.enter_context(nc.sbuf_tensor(f"qsT{h}", [64, CH], BF16)) for h in range(H8)]
            kT = [p2.enter_context(nc.sbuf_tensor(f"kT{h}", [64, CH], BF16)) for h in range(H8)]
            sT = [p2.enter_context(nc.sbuf_tensor(f"sT{h}", [64, CH], BF16)) for h in range(H8)]
            k_t = p2.enter_context(nc.sbuf_tensor("k_t", [128, NA, 512], BF16))
            v_t = p2.enter_context(nc.sbuf_tensor("v_t", [128, NA, 1024], BF16))
            hkT = [p2.enter_context(nc.sbuf_tensor(f"hkT{h}", [64, 64], F32)) for h in range(H8)]
            hv = [p2.enter_context(nc.sbuf_tensor(f"hv{h}", [64, 128], F32)) for h in range(H8)]
            nz = p2.enter_context(nc.sbuf_tensor("nz", [64, H8], F32))
            cs = p2.enter_context(nc.sbuf_tensor("cs", [64, H8], F32))
            lnt = p2.enter_context(nc.sbuf_tensor("lnt", [64, 2 * H8], F32))
            hkTb = [[p2.enter_context(nc.sbuf_tensor(f"hkTb{h}_{p}", [64, 64], BF16)) for p in range(2)] for h in range(H8)]
            hvb = [[p2.enter_context(nc.sbuf_tensor(f"hvb{h}_{p}", [64, 128], BF16)) for p in range(2)] for h in range(H8)]
            hkb = [p2.enter_context(nc.sbuf_tensor(f"hkb{h}", [64, 64], BF16)) for h in range(H8)]
            Emt = [p2.enter_context(nc.sbuf_tensor(f"Emt{h}", [64, C], BF16)) for h in range(H8)]
            Etm = [p2.enter_context(nc.sbuf_tensor(f"Etm{h}", [C, 64], BF16)) for h in range(H8)]
            ATm = [p2.enter_context(nc.sbuf_tensor(f"ATm{h}", [C, C], BF16)) for h in range(H8)]
            BTm = [p2.enter_context(nc.sbuf_tensor(f"BTm{h}", [C, C], BF16)) for h in range(H8)]
            rD1 = [p2.enter_context(nc.sbuf_tensor(f"rD1{h}", [C, 64], F32)) for h in range(H8)]
            okb = [p2.enter_context(nc.sbuf_tensor(f"ok{h}", [C, 64], F32)) for h in range(H8)]
            eob = [p2.enter_context(nc.sbuf_tensor(f"eo{h}", [C, 64], F32)) for h in range(H8)]
            pqb = [p2.enter_context(nc.sbuf_tensor(f"pq{h}", [C, 64], BF16)) for h in range(H8)]
            pqT = [p2.enter_context(nc.sbuf_tensor(f"pqT{h}", [64, C], BF16)) for h in range(H8)]
            ovs = [p2.enter_context(nc.sbuf_tensor(f"ovs{h}", [C, 128], F32)) for h in range(H8)]
            sq2 = [p2.enter_context(nc.sbuf_tensor(f"sq2{h}", [C, 128], F32)) for h in range(H8)]
            scl = p2.enter_context(nc.sbuf_tensor("scl", [128, 4 * H8], F32))
            o3T = [p2.enter_context(nc.sbuf_tensor(f"o3T{i}", [128, H8, C], BF16)) for i in range(NA)]
            bP = [p2.enter_context(nc.psum_tensor(f"bP{i}", [128, 512], F32)) for i in range(2)]
            bA = [p2.enter_context(nc.psum_tensor(f"bA{i}", [128, 512], F32)) for i in range(2)]
            bU = p2.enter_context(nc.psum_tensor("bU", [128, 512], F32))
            bV = p2.enter_context(nc.psum_tensor("bV", [128, 512], F32))
            bT1 = p2.enter_context(nc.psum_tensor("bT1", [128, 1024], BF16))
            bT2 = p2.enter_context(nc.psum_tensor("bT2", [128, 1024], BF16))

            prj = [None] * NCH
            prjv = [None] * NCH

            def emit_p2_sp(ch):
                def f(sp, ch=ch):
                    Wsp(sp, mk[("p0st", ch)])
                    Wsp(sp, mk[("p1st", ch)])
                    if ch >= 1:
                        Wsp(sp, mk[((ch - 1) * NA + NA - 1, "dve7")])
                    if ch >= 2:
                        Wsp(sp, prj[ch - 2])
                    s = sxl2[ch % 2]
                    s.inc(sp.dma_start(out=xlt2[ch % 2][:],
                                       in_=xlnT_d[:, :, ch * CH:(ch + 1) * CH]), 16)
                    mk[("p2ld", ch)] = s.mark
                    ssgi.inc(sp.dma_start(
                        out=sgi[ch % 2][:],
                        in_=sgg_d[ch * CH:(ch + 1) * CH, :].rearrange("(a p) n -> p a n", p=128)), 16)
                    mk[("p2sg", ch)] = ssgi.mark
                on("sync", f)

            def emit_proj(ch):
                blocks = ([("q", bi) for bi in range(4)] + [("k", bi) for bi in range(4)]
                          + [("s", bi) for bi in range(4)] + [("kt", a) for a in range(NA)]
                          + [("vt", (a, vb)) for a in range(NA) for vb in range(2)])

                for i, (kind, arg) in enumerate(blocks):
                    def pe_f(t, ch=ch, i=i, kind=kind, arg=arg):
                        if i == 0:
                            Wpe(t, mk[("p2ld", ch)])
                            Wpe(t, wat_mark)
                            if ch >= 1:
                                Wpe(t, mk[((ch - 1) * NA + NA - 1, "act1")])
                                Wpe(t, mk[((ch - 1) * NA + NA - 1, "dve7")])
                        p = bP[i % 2]
                        if i >= 2:
                            Wpe(t, mk[("prjev", ch, i - 2)])
                        if kind in ("q", "k", "s"):
                            wsb = {"q": wqs, "k": wks, "s": wss}[kind]
                            for dc in range(16):
                                inst = t.matmul(p[:], wsb[:, dc, arg * 128:(arg + 1) * 128],
                                                xlt2[ch % 2][:, dc, :],
                                                start=(dc == 0), stop=(dc == 15))
                        elif kind == "kt":
                            for dc in range(16):
                                inst = t.matmul(p[:], xlt2[ch % 2][:, dc, arg * 128:(arg + 1) * 128],
                                                wks[:, dc, :], start=(dc == 0), stop=(dc == 15))
                        else:
                            a, vb = arg
                            for dc in range(16):
                                inst = t.matmul(p[:], xlt2[ch % 2][:, dc, a * 128:(a + 1) * 128],
                                                wvs[:, dc, vb * 512:(vb + 1) * 512],
                                                start=(dc == 0), stop=(dc == 15))
                        s_pe.inc(inst)
                        mk[("prjmm", ch, i)] = s_pe.mark
                        if i == len(blocks) - 1:
                            prj[ch] = s_pe.mark
                    on("tensor", pe_f)

                    if kind in ("q", "k"):
                        def act_f(s, ch=ch, i=i, kind=kind, arg=arg):
                            Wact(s, mk[("prjmm", ch, i)])
                            p = bP[i % 2]
                            dsts = qsT if kind == "q" else kT
                            sc = (DK ** -0.5) if kind == "q" else 1.0
                            h0 = arg * 2
                            s.activation(dsts[h0][:], p[0:64, :], AF.Copy, scale=sc)
                            s_act.inc(s.activation(dsts[h0 + 1][:], p[64:128, :], AF.Copy, scale=sc))
                            mk[("prjev", ch, i)] = s_act.mark
                        on("scalar", act_f)
                    else:
                        def dve_f(v, ch=ch, i=i, kind=kind, arg=arg):
                            Wdve(v, mk[("prjmm", ch, i)])
                            p = bP[i % 2]
                            if kind == "s":
                                h0 = arg * 2
                                v.tensor_scalar(sT[h0][:], p[0:64, :], -32.0, 32.0, ALU.max, ALU.min)
                                inst = v.tensor_scalar(sT[h0 + 1][:], p[64:128, :], -32.0, 32.0,
                                                       ALU.max, ALU.min)
                            elif kind == "kt":
                                inst = v.tensor_copy(k_t[:, arg, :], p[:])
                            else:
                                a, vb = arg
                                inst = v.tensor_copy(v_t[:, a, vb * 512:(vb + 1) * 512], p[:])
                            s_dve.inc(inst)
                            mk[("prjev", ch, i)] = s_dve.mark
                            if i == len(blocks) - 1:
                                prjv[ch] = s_dve.mark
                        on("vector", dve_f)

            def scan_chunk(ch, a):
                g = ch * NA + a
                par = g % 2
                first = (g == 0)
                tsl = slice(a * C, (a + 1) * C)

                def act1(s):
                    Wact(s, prjv[ch])
                    Wact(s, mk.get((g - 1, "dve2B")))
                    Wact(s, mk.get((g - 1, "pe5")))
                    for h in range(H8):
                        if first:
                            inst = s.activation(Emt[h][:], sT[h][:, tsl], AF.Exp)
                        else:
                            inst = s.activation(Emt[h][:], sT[h][:, tsl], AF.Exp,
                                                bias=nz[:, h:h + 1])
                        s_act.inc(inst)
                    mk[(g, "act1")] = s_act.mark
                on("scalar", act1)

                def pe1(t):
                    Wpe(t, mk[("prjev", ch, 7)])
                    Wpe(t, mk.get((g - 1, "act6")))
                    for h in range(H8):
                        p = bA[h // 4]
                        inst = t.matmul(p[:, (h % 4) * 128:(h % 4) * 128 + 128],
                                        kT[h][:, tsl], qsT[h][:, tsl], start=True, stop=True)
                        s_pe.inc(inst)
                    mk[(g, "pe1a")] = s_pe.mark
                    Wpe(t, mk[(g, "act1")])
                    Wpe(t, mk.get((g - 1, "act2")))
                    if not first:
                        Wpe(t, mk.get((g - 1, "dve2B")))
                    for h in range(H8):
                        inst = t.transpose(bT1[:, h * 64:(h + 1) * 64], Emt[h][:],
                                           ident[0:64, 0:64])
                        if not first:
                            inst = t.transpose(bT1[0:64, 512 + h * 64:512 + (h + 1) * 64],
                                               hkTb[h][(g - 1) % 2][:], ident[0:64, 0:64])
                        s_pe.inc(inst)
                    mk[(g, "pe1b")] = s_pe.mark
                on("tensor", pe1)

                def act2(s):
                    Wact(s, mk[(g, "pe1b")])
                    Wact(s, mk.get((g - 1, "pe3")))
                    for h in range(H8):
                        inst = s.copy(Etm[h][:], bT1[:, h * 64:(h + 1) * 64])
                        if not first:
                            inst = s.copy(hkb[h][:], bT1[0:64, 512 + h * 64:512 + (h + 1) * 64])
                        s_act.inc(inst)
                    mk[(g, "act2")] = s_act.mark
                on("scalar", act2)

                def dve_cs(v):
                    Wdve(v, mk[(g, "act1")])
                    for h in range(H8):
                        inst = v.reduce_sum(cs[:, h:h + 1], Emt[h][:], axis=AX.X)
                        s_dve.inc(inst)
                    mk[(g, "dvecs")] = s_dve.mark
                on("vector", dve_cs)

                def dve1(v):
                    Wdve(v, mk[(g, "pe1a")])
                    for h in range(H8):
                        p = bA[h // 4]
                        inst = v.tensor_mul(ATm[h][:], p[:, (h % 4) * 128:(h % 4) * 128 + 128],
                                            mut[:])
                        s_dve.inc(inst)
                    mk[(g, "dve1")] = s_dve.mark
                on("vector", dve1)

                for half in range(2):
                    hs = range(half * 4, half * 4 + 4)
                    tag = "A" if half == 0 else "B"

                    def pe2(t, hs=hs, half=half, tag=tag):
                        Wpe(t, mk[(g, "act2")])
                        if half == 0:
                            Wpe(t, mk.get((g - 1, "dve3")))
                            Wpe(t, mk.get((g - 1, "dve2B")))
                        else:
                            Wpe(t, mk[(g, "dve2A")])
                            Wpe(t, mk[(g, "act3A")])
                        for h in hs:
                            r = h % 4
                            t.matmul(bU[0:64, r * 64:(r + 1) * 64], Etm[h][:],
                                     k_t[:, a, h * 64:(h + 1) * 64], start=True, stop=True)
                            t.matmul(bV[0:64, r * 128:(r + 1) * 128], Etm[h][:],
                                     v_t[:, a, h * 128:(h + 1) * 128], start=True, stop=True)
                            inst = t.matmul(bU[:, 256 + r * 64:256 + (r + 1) * 64], mut[:],
                                            Etm[h][:], start=True, stop=True)
                            s_pe.inc(inst)
                        mk[(g, "pe2" + tag)] = s_pe.mark
                    on("tensor", pe2)

                    def act3(s, hs=hs, tag=tag):
                        Wact(s, mk[(g, "pe2" + tag)])
                        Wact(s, mk[(g, "dvecs")])
                        Wact(s, mk.get((g - 1, "dve4")))
                        for h in hs:
                            r = h % 4
                            if first:
                                s.activation(rD1[h][:], bU[:, 256 + r * 64:256 + (r + 1) * 64],
                                             AF.Ln)
                            else:
                                s.activation(rD1[h][:], bU[:, 256 + r * 64:256 + (r + 1) * 64],
                                             AF.Ln, bias=cst[:, 1:2])
                        for h in hs:
                            if first:
                                inst = s.activation(lnt[:, 2 * h:2 * h + 1], cs[:, h:h + 1], AF.Ln)
                            else:
                                inst = s.activation(lnt[:, 2 * h:2 * h + 1], cs[:, h:h + 1], AF.Ln,
                                                    bias=cst[0:64, 1:2])
                        s_act.inc(inst)
                        s.wait_ge(s_act.h, s_act.n)
                        for h in hs:
                            s.activation(rD1[h][:], rD1[h][:], AF.Exp, scale=-1.0)
                        for h in hs:
                            inst = s.activation(lnt[:, 2 * h + 1:2 * h + 2],
                                                lnt[:, 2 * h:2 * h + 1], AF.Exp, scale=-1.0)
                        s_act.inc(inst)
                        mk[(g, "act3" + tag)] = s_act.mark
                    on("scalar", act3)

                    def dve2(v, hs=hs, tag=tag, first=first):
                        Wdve(v, mk[(g, "act3" + tag)])
                        if first:
                            for h in hs:
                                r = h % 4
                                gam = lnt[:, 2 * h + 1:2 * h + 2]
                                v.tensor_scalar_mul(nz[:, h:h + 1], lnt[:, 2 * h:2 * h + 1], -1.0)
                                v.tensor_scalar_mul(hkT[h][:], bU[0:64, r * 64:(r + 1) * 64], gam)
                                inst = v.tensor_scalar_mul(hv[h][:], bV[0:64, r * 128:(r + 1) * 128], gam)
                            s_dve.inc(inst)
                        else:
                            for h in hs:
                                r = h % 4
                                v.tensor_sub(nz[:, h:h + 1], nz[:, h:h + 1], lnt[:, 2 * h:2 * h + 1])
                                v.tensor_add(hkT[h][:], hkT[h][:], bU[0:64, r * 64:(r + 1) * 64])
                                inst = v.tensor_add(hv[h][:], hv[h][:], bV[0:64, r * 128:(r + 1) * 128])
                            s_dve.inc(inst)
                            v.wait_ge(s_dve.h, s_dve.n)
                            for h in hs:
                                gam = lnt[:, 2 * h + 1:2 * h + 2]
                                v.tensor_scalar_mul(hkT[h][:], hkT[h][:], gam)
                                inst = v.tensor_scalar_mul(hv[h][:], hv[h][:], gam)
                            s_dve.inc(inst)
                        v.wait_ge(s_dve.h, s_dve.n)
                        for h in hs:
                            v.tensor_copy(hkTb[h][par][:], hkT[h][:])
                            inst = v.tensor_copy(hvb[h][par][:], hv[h][:])
                        s_dve.inc(inst)
                        mk[(g, "dve2" + tag)] = s_dve.mark
                    on("vector", dve2)

                def pe3(t):
                    Wpe(t, mk[(g, "dve1")])
                    Wpe(t, mk[(g, "dve2B")])
                    Wpe(t, mk[(g, "act3B")])
                    for h in range(H8):
                        if first:
                            inst = t.matmul(bU[:, h * 64:(h + 1) * 64], ATm[h][:], Etm[h][:],
                                            start=True, stop=True)
                        else:
                            t.matmul(bU[:, h * 64:(h + 1) * 64], qsT[h][:, tsl], hkb[h][:],
                                     start=True, stop=False)
                            inst = t.matmul(bU[:, h * 64:(h + 1) * 64], ATm[h][:], Etm[h][:],
                                            start=False, stop=True)
                        s_pe.inc(inst)
                    mk[(g, "pe3")] = s_pe.mark
                on("tensor", pe3)

                def dve3(v):
                    Wdve(v, mk[(g, "pe3")])
                    Wdve(v, mk[(g, "act3B")])
                    for h in range(H8):
                        inst = v.tensor_mul(okb[h][:], bU[:, h * 64:(h + 1) * 64], rD1[h][:])
                        s_dve.inc(inst)
                    mk[(g, "dve3")] = s_dve.mark
                on("vector", dve3)

                def act4(s):
                    Wact(s, mk[(g, "dve3")])
                    for h in range(H8):
                        s_act.inc(s.activation(eob[h][:], okb[h][:], AF.Exp))
                    mk[(g, "act4")] = s_act.mark
                on("scalar", act4)

                def dve4(v):
                    Wdve(v, mk[(g, "act4")])
                    for h in range(H8):
                        s_dve.inc(v.reduce_sum(scl[:, 4 * h:4 * h + 1], eob[h][:], axis=AX.X))
                        v.wait_ge(s_dve.h, s_dve.n)
                        s_dve.inc(v.reciprocal(scl[:, 4 * h + 1:4 * h + 2], scl[:, 4 * h:4 * h + 1]))
                        v.wait_ge(s_dve.h, s_dve.n)
                        inst = v.scalar_tensor_tensor(pqb[h][:], eob[h][:],
                                                      scl[:, 4 * h + 1:4 * h + 2],
                                                      rD1[h][:], ALU.mult, ALU.mult)
                        s_dve.inc(inst)
                    mk[(g, "dve4")] = s_dve.mark
                on("vector", dve4)

                def pe4(t):
                    Wpe(t, mk[(g, "dve4")])
                    Wpe(t, mk.get((g - 1, "act8")))
                    for h in range(H8):
                        inst = t.transpose(bT2[0:64, h * 128:(h + 1) * 128], pqb[h][:], ident[:])
                        s_pe.inc(inst)
                    mk[(g, "pe4")] = s_pe.mark
                on("tensor", pe4)

                def act5(s):
                    Wact(s, mk[(g, "pe4")])
                    for h in range(H8):
                        s_act.inc(s.copy(pqT[h][:], bT2[0:64, h * 128:(h + 1) * 128]))
                    mk[(g, "act5")] = s_act.mark
                on("scalar", act5)

                def pe5(t):
                    Wpe(t, mk[(g, "act5")])
                    Wpe(t, mk[(g, "dve1")])
                    for h in range(H8):
                        p = bA[h // 4]
                        inst = t.matmul(p[:, (h % 4) * 128:(h % 4) * 128 + 128],
                                        Emt[h][:], pqT[h][:], start=True, stop=True)
                        s_pe.inc(inst)
                    mk[(g, "pe5")] = s_pe.mark
                on("tensor", pe5)

                def dve5(v):
                    Wdve(v, mk[(g, "pe5")])
                    for h in range(H8):
                        p = bA[h // 4]
                        inst = v.tensor_mul(BTm[h][:], p[:, (h % 4) * 128:(h % 4) * 128 + 128],
                                            mut[:])
                        s_dve.inc(inst)
                    mk[(g, "dve5")] = s_dve.mark
                on("vector", dve5)

                def pe6(t):
                    Wpe(t, mk[(g, "dve5")])
                    Wpe(t, mk[(g, "dve2B")])
                    for h in range(H8):
                        p = bA[h // 4]
                        reg = p[:, (h % 4) * 128:(h % 4) * 128 + 128]
                        if first:
                            inst = t.matmul(reg, BTm[h][:], v_t[:, a, h * 128:(h + 1) * 128],
                                            start=True, stop=True)
                        else:
                            t.matmul(reg, BTm[h][:], v_t[:, a, h * 128:(h + 1) * 128],
                                     start=True, stop=False)
                            inst = t.matmul(reg, pqT[h][:], hvb[h][(g - 1) % 2][:],
                                            start=False, stop=True)
                        s_pe.inc(inst)
                    mk[(g, "pe6")] = s_pe.mark
                on("tensor", pe6)

                def act6(s):
                    Wact(s, mk[(g, "pe6")])
                    for h in range(H8):
                        p = bA[h // 4]
                        s_act.inc(s.copy(ovs[h][:], p[:, (h % 4) * 128:(h % 4) * 128 + 128]))
                    mk[(g, "act6")] = s_act.mark
                on("scalar", act6)

                def dve6(v):
                    Wdve(v, mk[(g, "act6")])
                    for h in range(H8):
                        inst = v.tensor_mul(sq2[h][:], ovs[h][:], ovs[h][:])
                    s_dve.inc(inst)
                    v.wait_ge(s_dve.h, s_dve.n)
                    for h in range(H8):
                        inst = v.reduce_sum(scl[:, 4 * h + 2:4 * h + 3], sq2[h][:], axis=AX.X)
                    s_dve.inc(inst)
                    mk[(g, "dve6")] = s_dve.mark
                on("vector", dve6)

                def act7(s):
                    Wact(s, mk[(g, "dve6")])
                    for h in range(H8):
                        inst = s.activation(scl[:, 4 * h + 3:4 * h + 4], scl[:, 4 * h + 2:4 * h + 3],
                                            AF.Ln, bias=cst[:, 0:1], scale=1.0 / DV)
                    s_act.inc(inst)
                    s.wait_ge(s_act.h, s_act.n)
                    for h in range(H8):
                        inst = s.activation(scl[:, 4 * h + 3:4 * h + 4],
                                            scl[:, 4 * h + 3:4 * h + 4], AF.Exp, scale=-0.5)
                    s_act.inc(inst)
                    mk[(g, "act7")] = s_act.mark
                on("scalar", act7)

                def dve7(v):
                    Wdve(v, mk[(g, "act7")])
                    Wdve(v, mk[("p2sg", ch)])
                    for h in range(H8):
                        inst = v.scalar_tensor_tensor(sq2[h][:], ovs[h][:],
                                                      scl[:, 4 * h + 3:4 * h + 4],
                                                      sgi[ch % 2][:, a, h * 128:(h + 1) * 128],
                                                      ALU.mult, ALU.mult)
                    s_dve.inc(inst)
                    v.wait_ge(s_dve.h, s_dve.n)
                    for h in range(H8):
                        inst = v.tensor_copy(BTm[h][:], sq2[h][:])
                    s_dve.inc(inst)
                    mk[(g, "dve7")] = s_dve.mark
                on("vector", dve7)

                def pe7(t):
                    Wpe(t, mk[(g, "dve7")])
                    Wpe(t, mk[(g, "act5")])
                    for h in range(H8):
                        inst = t.transpose(bT2[:, h * 128:(h + 1) * 128], BTm[h][:], ident[:])
                        s_pe.inc(inst)
                    mk[(g, "pe7")] = s_pe.mark
                on("tensor", pe7)

                def act8(s):
                    Wact(s, mk[(g, "pe7")])
                    if g >= 4:
                        Wact(s, mk[("p2o3", g - 4)])
                    for h in range(H8):
                        s_act.inc(s.copy(o3T[a][:, h, :], bT2[:, h * 128:(h + 1) * 128]))
                    mk[(g, "act8")] = s_act.mark
                on("scalar", act8)

            for ch in range(NCH):
                emit_p2_sp(ch)
                emit_proj(ch)
                for a in range(NA):
                    scan_chunk(ch, a)

                def sp_o3(sp, ch=ch):
                    half, j = ch // 4, ch % 4
                    for a in range(NA):
                        Wsp(sp, mk[(ch * NA + a, "act8")])
                        s = so3a[a]
                        s.inc(sp.dma_start(
                            out=o3s_d[j, :, half * 512 + a * 128:half * 512 + (a + 1) * 128]
                                .rearrange("(h p) t -> p h t", p=128),
                            in_=o3T[a][:]), 16)
                        mk[("p2o3", ch * NA + a)] = s.mark
                on("sync", sp_o3)

                if ch >= 4:
                    def pl_cc(g_, ch=ch):
                        j = ch % 4
                        for a in range(NA):
                            Wpl(g_, mk[("p2o3", ch * NA + a)])
                            Wpl(g_, mk[("p2o3", j * NA + a)])
                        s = scc[j]
                        s.inc(g_.collective_compute(
                            "AllGather", ALU.bypass,
                            replica_groups=[[0, 1], [2, 3], [4, 5], [6, 7]],
                            ins=[o3s_d[j].opt()],
                            outs=[o3m_d[j].opt()]))
                        mk[("a2a", j)] = s.mark
                    on("gpsimd", pl_cc)

        if PHASES < 4:
            def sp_end2(sp):
                for s in ALL_SEMS[3:]:
                    Wsp(sp, s.mark)
            on("sync", sp_end2)
            aw.close()
            return nc

        aw.close()
        phase_barrier()
        with ExitStack() as pc1:
            wos = pc1.enter_context(nc.sbuf_tensor("wos", [128, 16, 2048], BF16))
            o3f = [pc1.enter_context(nc.sbuf_tensor(f"o3f{i}", [128, 16, 512], BF16)) for i in range(2)]
            xo = [pc1.enter_context(nc.sbuf_tensor(f"xo{i}", [128, D], F32)) for i in range(2)]
            x1b = [pc1.enter_context(nc.sbuf_tensor(f"x1b{i}", [128, D], F32)) for i in range(2)]
            x1l = [pc1.enter_context(nc.sbuf_tensor(f"x1l{i}", [128, D], BF16)) for i in range(2)]
            xtT1 = [pc1.enter_context(nc.sbuf_tensor(f"xtT1{i}", [128, 16, 128], BF16)) for i in range(2)]
            st2 = pc1.enter_context(nc.sbuf_tensor("pc1st", [128, 8], F32))
            sq1 = pc1.enter_context(nc.sbuf_tensor("pc1sq", [128, D], F32))
            pw = [pc1.enter_context(nc.psum_tensor(f"pw{i}", [128, 512], F32)) for i in range(4)]
            ptr = [pc1.enter_context(nc.psum_tensor(f"ptr{i}", [128, 512], BF16)) for i in range(2)]

            def pc1_w(g):
                s_wo.inc(g.dma_start(out=wos[:], in_=wo_ap.rearrange("(hc p) n -> p hc n", p=128)), 16)
            on("gpsimd", pc1_w)
            wo_mark = s_wo.mark

            def pc1_gnw(v):
                Wdve(v, wo_mark)
                Wdve(v, pre_mark)
                for hc in range(16):
                    s_dve.inc(v.tensor_scalar_mul(wos[:, hc, :], wos[:, hc, :], gnw[:]))
            on("vector", pc1_gnw)
            wognw_mark = s_dve.mark

            for j in range(4):
                def sp_o3f(sp, j=j):
                    Wsp(sp, mk[("a2a", j)])
                    Wsp(sp, pre_mark)
                    if j >= 2:
                        Wsp(sp, mk[("pc1mm", (j - 2) * 4 + 3, 3)])
                    with sp.register(f"hgo{j}") as hgr:
                        sp.reg_load(hgr, hgv[0:1, 0:1])
                        off = sp.snap(hgr, min_val=0, max_val=1)
                        s = so3f[j % 2]
                        s.inc(sp.dma_start(
                            out=o3f[j % 2][:],
                            in_=o3m_d[j].rearrange("(hc p) t -> p hc t", p=128)[:, :, bass.ts(off, 512)]), 16)
                    mk[("pc1o3", j)] = so3f[j % 2].mark
                on("sync", sp_o3f)

                for tc4 in range(4):
                    tc = j * 4 + tc4

                    def sp_x(sp, tc=tc):
                        if tc >= 2:
                            Wsp(sp, mk[("pc1x1", tc - 2)])
                        s = sxo[tc % 2]
                        s.inc(sp.dma_start(out=xo[tc % 2][:],
                                           in_=xown_ap[tc * 128:(tc + 1) * 128, :]), 16)
                        mk[("pc1ldx", tc)] = s.mark
                    on("sync", sp_x)

                    def pe_wo(t, tc=tc, j=j, tc4=tc4):
                        Wpe(t, mk[("pc1o3", j)])
                        Wpe(t, wognw_mark)
                        for dg in range(4):
                            p = pw[dg]
                            if tc >= 1:
                                Wpe(t, mk[("pc1ev", tc - 1, dg)])
                            for hc in range(16):
                                inst = t.matmul(p[:], o3f[j % 2][:, hc, tc4 * 128:(tc4 + 1) * 128],
                                                wos[:, hc, dg * 512:(dg + 1) * 512],
                                                start=(hc == 0), stop=(hc == 15))
                            s_pe.inc(inst)
                            mk[("pc1mm", tc, dg)] = s_pe.mark
                    on("tensor", pe_wo)

                    def dve_x1(v, tc=tc):
                        Wdve(v, mk[("pc1ldx", tc)])
                        if tc >= 2:
                            Wdve(v, mk[("pc1x1st", tc - 2)])
                        for dg in range(4):
                            Wdve(v, mk[("pc1mm", tc, dg)])
                            inst = v.tensor_add(x1b[tc % 2][:, dg * 512:(dg + 1) * 512],
                                                xo[tc % 2][:, dg * 512:(dg + 1) * 512],
                                                pw[dg][:])
                            s_dve.inc(inst)
                            mk[("pc1ev", tc, dg)] = s_dve.mark
                        mk[("pc1x1", tc)] = s_dve.mark
                    on("vector", dve_x1)

                    def sp_x1(sp, tc=tc):
                        Wsp(sp, mk[("pc1x1", tc)])
                        s = sx1[tc % 2]
                        s.inc(sp.dma_start(out=x1_d[tc * 128:(tc + 1) * 128, :],
                                           in_=x1b[tc % 2][:]), 16)
                        mk[("pc1x1st", tc)] = s.mark
                    on("sync", sp_x1)

                    def dve_lna(v, tc=tc):
                        s_dve.inc(v.reduce_sum(st2[:, 0:1], x1b[tc % 2][:], axis=AX.X))
                        v.wait_ge(s_dve.h, s_dve.n)
                        s_dve.inc(v.tensor_scalar_mul(st2[:, 1:2], st2[:, 0:1], 1.0 / D))
                        v.wait_ge(s_dve.h, s_dve.n)
                        s_dve.inc(v.tensor_mul(st2[:, 2:3], st2[:, 1:2], st2[:, 1:2]))
                        mk[("pc1lna", tc)] = s_dve.mark
                    on("vector", dve_lna)

                    def dve_lnb(v, tc=tc):
                        s_dve.inc(v.tensor_mul(sq1[:], x1b[tc % 2][:], x1b[tc % 2][:]))
                        v.wait_ge(s_dve.h, s_dve.n)
                        s_dve.inc(v.reduce_sum(st2[:, 0:1], sq1[:], axis=AX.X))
                        v.wait_ge(s_dve.h, s_dve.n)
                        s_dve.inc(v.scalar_tensor_tensor(st2[:, 3:4], st2[:, 0:1], 1.0 / D,
                                                         st2[:, 2:3], ALU.mult, ALU.subtract))
                        mk[("pc1lnb", tc)] = s_dve.mark
                    on("vector", dve_lnb)

                    def act_lnb(s, tc=tc):
                        Wact(s, mk[("pc1lnb", tc)])
                        s_act.inc(s.activation(st2[:, 5:6], st2[:, 3:4], AF.Ln, bias=cst[:, 0:1]))
                        s.wait_ge(s_act.h, s_act.n)
                        s_act.inc(s.activation(st2[:, 6:7], st2[:, 5:6], AF.Exp, scale=-0.5))
                        mk[("pc1rstd", tc)] = s_act.mark
                    on("scalar", act_lnb)

                    def dve_lnc(v, tc=tc):
                        Wdve(v, mk[("pc1rstd", tc)])
                        if tc >= 2:
                            Wdve(v, mk[("pc1tr", tc - 2, 3)])
                        s_dve.inc(v.tensor_scalar(x1l[tc % 2][:], x1b[tc % 2][:],
                                                  st2[:, 1:2], st2[:, 6:7],
                                                  ALU.subtract, ALU.mult))
                        mk[("pc1ln", tc)] = s_dve.mark
                    on("vector", dve_lnc)

                    for k in range(4):
                        def pe_t1(t, tc=tc, k=k):
                            Wpe(t, mk[("pc1ln", tc)])
                            gi = tc * 4 + k
                            if gi >= 2:
                                Wpe(t, mk[("pc1te", gi - 2)])
                            p = ptr[gi % 2]
                            for jj in range(4):
                                dc = k * 4 + jj
                                inst = t.transpose(p[:, jj * 128:(jj + 1) * 128],
                                                   x1l[tc % 2][:, dc * 128:(dc + 1) * 128],
                                                   ident[:])
                            s_pe.inc(inst)
                            mk[("pc1tr", tc, k)] = s_pe.mark
                        on("tensor", pe_t1)

                        def act_e1(s, tc=tc, k=k):
                            gi = tc * 4 + k
                            Wact(s, mk[("pc1tr", tc, k)])
                            if tc >= 2:
                                Wact(s, mk[("pc1xst", tc - 2)])
                            s_act.inc(s.copy(xtT1[tc % 2][:, k * 4:(k + 1) * 4, :],
                                             ptr[gi % 2][:].rearrange("p (j t) -> p j t", j=4)))
                            mk[("pc1te", gi)] = s_act.mark
                        on("scalar", act_e1)

                    def sp_xt1(sp, tc=tc):
                        Wsp(sp, mk[("pc1te", tc * 4 + 3)])
                        s = sxt1[tc % 2]
                        s.inc(sp.dma_start(out=x1lnT_d[:, :, tc * 128:(tc + 1) * 128],
                                           in_=xtT1[tc % 2][:]), 16)
                        mk[("pc1xst", tc)] = s.mark
                    on("sync", sp_xt1)

        if PHASES < 5:
            def sp_end3(sp):
                for s in ALL_SEMS[3:]:
                    Wsp(sp, s.mark)
            on("sync", sp_end3)
            return nc

        phase_barrier()
        with ExitStack() as pcb:
            gw = pcb.enter_context(nc.sbuf_tensor("gw", [128, 16, 8], BF16))
            xl2 = [pcb.enter_context(nc.sbuf_tensor(f"xl2{i}", [128, 16, 128], BF16)) for i in range(2)]
            sc2 = pcb.enter_context(nc.sbuf_tensor("sc2", [128, 40], F32))
            wrow = pcb.enter_context(nc.sbuf_tensor("wrow", [8, 128], F32))
            pl2 = pcb.enter_context(nc.psum_tensor("pl2", [128, 8], F32))
            pt2 = pcb.enter_context(nc.psum_tensor("pt2", [8, 128], F32))

            def pcb_w(g):
                s_wgt.inc(g.dma_start(out=gw[:], in_=gwT_ap.rearrange("(dc p) e -> p dc e", p=128)), 16)
            on("gpsimd", pcb_w)
            gw_mark = s_wgt.mark

            for tc in range(16):
                def sp_l2(sp, tc=tc):
                    Wsp(sp, mk[("pc1xst", tc)])
                    if tc >= 2:
                        Wsp(sp, mk[("pcb_mm", tc - 2)])
                    s = sxlb[tc % 2]
                    s.inc(sp.dma_start(out=xl2[tc % 2][:],
                                       in_=x1lnT_d[:, :, tc * 128:(tc + 1) * 128]), 16)
                    mk[("pcb_ld", tc)] = s.mark
                on("sync", sp_l2)

                def pe_l2(t, tc=tc):
                    Wpe(t, mk[("pcb_ld", tc)])
                    Wpe(t, gw_mark)
                    if tc >= 1:
                        Wpe(t, mk[("pcb_sig", tc - 1)])
                    for dc in range(16):
                        inst = t.matmul(pl2[:], xl2[tc % 2][:, dc, :], gw[:, dc, :],
                                        start=(dc == 0), stop=(dc == 15))
                    s_pe.inc(inst)
                    mk[("pcb_mm", tc)] = s_pe.mark
                on("tensor", pe_l2)

                def act_l2(s, tc=tc):
                    Wact(s, mk[("pcb_mm", tc)])
                    if tc >= 1:
                        Wact(s, mk[("pcb_ws", tc - 1)])
                    s_act.inc(s.activation(sc2[:, 0:8], pl2[:], AF.Sigmoid))
                    mk[("pcb_sig", tc)] = s_act.mark
                on("scalar", act_l2)

                def dve_l2(v, tc=tc):
                    Wdve(v, mk[("pcb_sig", tc)])
                    if tc >= 1:
                        Wdve(v, mk[("pcb_tr", tc - 1)])
                    s_dve.inc(v.reduce_max(sc2[:, 8:9], sc2[:, 0:8], axis=AX.X))
                    v.wait_ge(s_dve.h, s_dve.n)
                    s_dve.inc(v.tensor_scalar(sc2[:, 16:24], sc2[:, 0:8], sc2[:, 8:9], None, ALU.is_equal))
                    v.wait_ge(s_dve.h, s_dve.n)
                    s_dve.inc(v.scalar_tensor_tensor(sc2[:, 24:32], sc2[:, 16:24], -1e9,
                                                     sc2[:, 0:8], ALU.mult, ALU.add))
                    v.wait_ge(s_dve.h, s_dve.n)
                    s_dve.inc(v.reduce_max(sc2[:, 9:10], sc2[:, 24:32], axis=AX.X))
                    v.wait_ge(s_dve.h, s_dve.n)
                    s_dve.inc(v.tensor_scalar(sc2[:, 24:32], sc2[:, 24:32], sc2[:, 9:10], None, ALU.is_equal))
                    v.wait_ge(s_dve.h, s_dve.n)
                    s_dve.inc(v.tensor_add(sc2[:, 16:24], sc2[:, 16:24], sc2[:, 24:32]))
                    v.wait_ge(s_dve.h, s_dve.n)
                    s_dve.inc(v.tensor_mul(sc2[:, 32:40], sc2[:, 0:8], sc2[:, 16:24]))
                    mk[("pcb_ws", tc)] = s_dve.mark
                on("vector", dve_l2)

                def pe_t2(t, tc=tc):
                    Wpe(t, mk[("pcb_ws", tc)])
                    if tc >= 1:
                        Wpe(t, mk[("pcb_ev", tc - 1)])
                    s_pe.inc(t.transpose(pt2[:], sc2[:, 32:40], id32[:]))
                    mk[("pcb_tr", tc)] = s_pe.mark
                on("tensor", pe_t2)

                def act_t2(s, tc=tc):
                    Wact(s, mk[("pcb_tr", tc)])
                    if tc >= 1:
                        Wact(s, mk[("pcb_st", tc - 1)])
                    s_act.inc(s.copy(wrow[:], pt2[:]))
                    mk[("pcb_ev", tc)] = s_act.mark
                on("scalar", act_t2)

                def sp_ws(sp, tc=tc):
                    Wsp(sp, mk[("pcb_ev", tc)])
                    swse.inc(sp.dma_start(out=wsel_d[:, tc * 128:(tc + 1) * 128], in_=wrow[:]), 16)
                    mk[("pcb_st", tc)] = swse.mark
                on("sync", sp_ws)
            wsel_done = swse.mark

        if PHASES < 6:
            def sp_end4(sp):
                for s in ALL_SEMS[3:]:
                    Wsp(sp, s.mark)
            on("sync", sp_end4)
            return nc

        phase_barrier()
        with ExitStack() as pc2:
            sw1 = pc2.enter_context(nc.sbuf_tensor("sw1", [128, 8, 16, 128], BF16))
            sw3 = pc2.enter_context(nc.sbuf_tensor("sw3", [128, 8, 16, 128], BF16))
            sw2 = pc2.enter_context(nc.sbuf_tensor("sw2", [128, 8, D], BF16))
            xw = pc2.enter_context(nc.sbuf_tensor("xw", [128, 16, 1024], BF16))
            acc = pc2.enter_context(nc.sbuf_tensor("acc", [128, 8, D], BF16))
            gat = pc2.enter_context(nc.sbuf_tensor("gat", [128, 8, 512], BF16))
            crow = pc2.enter_context(nc.sbuf_tensor("crow", [1, 1024], F32))
            cbc = pc2.enter_context(nc.sbuf_tensor("cbc", [128, 1024], F32))
            ca_t = pc2.enter_context(nc.sbuf_tensor("ca_t", [128, 512], F32))
            cb_t = pc2.enter_context(nc.sbuf_tensor("cb_t", [128, 512], F32))
            sg_t = pc2.enter_context(nc.sbuf_tensor("sg_t", [128, 512], F32))
            t1_t = pc2.enter_context(nc.sbuf_tensor("t1_t", [128, 512], F32))
            xfin = pc2.enter_context(nc.sbuf_tensor("xfin0", [128, D], F32))
            ofin = pc2.enter_context(nc.sbuf_tensor("ofin0", [128, D], F32))
            pab = [pc2.enter_context(nc.psum_tensor(f"pab{i}", [128, 512], F32)) for i in range(4)]
            pw2 = [pc2.enter_context(nc.psum_tensor(f"pw2{i}", [128, 512], F32)) for i in range(2)]
            pcb2 = pc2.enter_context(nc.psum_tensor("pcb2", [128, 512], F32))

            NW = 2
            EXPS = list(range(NEXP)) + ["sh"]
            prev_ab = [None]
            prev_w2 = [None]
            prev_cab = [None]
            prev_cbcmm = [None]
            cab_hist = [None, None]
            unit_ctr = [0]

            for wv_i in range(NW):
                def sp_xw(sp, wv_i=wv_i):
                    Wsp(sp, mk[("pc1xst", 15)])
                    Wsp(sp, mk[("pc1xst", 14)])
                    if wv_i >= 1:
                        Wsp(sp, prev_ab[0])
                    sxw.inc(sp.dma_start(out=xw[:],
                                         in_=x1lnT_d[:, :, wv_i * 1024:(wv_i + 1) * 1024]), 16)
                    mk[("ffn_xw", wv_i)] = sxw.mark
                on("sync", sp_xw)

                for ei, e in enumerate(EXPS):
                    def pl_w(g, wv_i=wv_i, e=e):
                        Wpl(g, prev_ab[0])
                        if e == "sh":
                            s_w13.inc(g.dma_start(out=sw1[:], in_=ws1S_ap), 16)
                            s_w13.inc(g.dma_start(out=sw3[:], in_=ws3S_ap), 16)
                        else:
                            s_w13.inc(g.dma_start(out=sw1[:], in_=w1S_ap[e]), 16)
                            s_w13.inc(g.dma_start(out=sw3[:], in_=w3S_ap[e]), 16)
                        mk[("ffn_w13", wv_i, e)] = s_w13.mark
                        Wpl(g, prev_w2[0])
                        if e == "sh":
                            s_w2f.inc(g.dma_start(out=sw2[:], in_=ws2T_ap.rearrange("(hc p) n -> p hc n", p=128)), 16)
                        else:
                            s_w2f.inc(g.dma_start(out=sw2[:], in_=w2T_ap[e].rearrange("(hc p) n -> p hc n", p=128)), 16)
                        mk[("ffn_w2", wv_i, e)] = s_w2f.mark
                    on("gpsimd", pl_w)

                    if e != "sh":
                        def sp_c(sp, wv_i=wv_i, e=e):
                            Wsp(sp, wsel_done)
                            Wsp(sp, prev_cbcmm[0])
                            scro.inc(sp.dma_start(out=crow[:],
                                                  in_=wsel_d[e:e + 1, wv_i * 1024:(wv_i + 1) * 1024]), 16)
                            mk[("ffn_crow", wv_i, e)] = scro.mark
                        on("sync", sp_c)

                        for cb_i in range(2):
                            def pe_c(t, wv_i=wv_i, e=e, cb_i=cb_i):
                                Wpe(t, mk[("ffn_crow", wv_i, e)])
                                Wpe(t, pre_mark)
                                if cb_i == 0:
                                    Wpe(t, prev_cab[0])
                                else:
                                    Wpe(t, mk[("ffn_cbcev", wv_i, e, 0)])
                                s_pe.inc(t.matmul(pcb2[:], oner[:],
                                                  crow[:, cb_i * 512:(cb_i + 1) * 512],
                                                  start=True, stop=True))
                                mk[("ffn_cbcmm", wv_i, e, cb_i)] = s_pe.mark
                            on("tensor", pe_c)

                            def act_c(s, wv_i=wv_i, e=e, cb_i=cb_i):
                                Wact(s, mk[("ffn_cbcmm", wv_i, e, cb_i)])
                                s_act.inc(s.copy(cbc[:, cb_i * 512:(cb_i + 1) * 512], pcb2[:]))
                                mk[("ffn_cbcev", wv_i, e, cb_i)] = s_act.mark
                            on("scalar", act_c)
                        prev_cbcmm[0] = mk[("ffn_cbcmm", wv_i, e, 1)]

                    for tg in range(2):
                        for hb in range(8):
                            pr = (unit_ctr[0] % 2) * 2
                            prevcab2 = cab_hist[0]
                            unit_ctr[0] += 1

                            def pe_ab(t, wv_i=wv_i, e=e, tg=tg, hb=hb, pr=pr,
                                      prevcab2=prevcab2):
                                Wpe(t, mk[("ffn_w13", wv_i, e)])
                                Wpe(t, mk[("ffn_xw", wv_i)])
                                Wpe(t, prevcab2)
                                for dc in range(16):
                                    t.matmul(pab[pr][:], sw1[:, hb, dc, :],
                                             xw[:, dc, tg * 512:(tg + 1) * 512],
                                             start=(dc == 0), stop=(dc == 15))
                                for dc in range(16):
                                    inst = t.matmul(pab[pr + 1][:], sw3[:, hb, dc, :],
                                                    xw[:, dc, tg * 512:(tg + 1) * 512],
                                                    start=(dc == 0), stop=(dc == 15))
                                s_pe.inc(inst)
                                mk[("ffn_ab", wv_i, e, tg, hb)] = s_pe.mark
                            on("tensor", pe_ab)

                            def dve_g1(v, wv_i=wv_i, e=e, tg=tg, hb=hb, pr=pr):
                                Wdve(v, mk[("ffn_ab", wv_i, e, tg, hb)])
                                if e != "sh":
                                    Wdve(v, mk[("ffn_cbcev", wv_i, e, 1)])
                                    v.tensor_mul(ca_t[:], pab[pr][:], cbc[:, tg * 512:(tg + 1) * 512])
                                    inst = v.tensor_mul(cb_t[:], pab[pr + 1][:], cbc[:, tg * 512:(tg + 1) * 512])
                                else:
                                    v.tensor_copy(ca_t[:], pab[pr][:])
                                    inst = v.tensor_copy(cb_t[:], pab[pr + 1][:])
                                s_dve.inc(inst)
                                mk[("ffn_cab", wv_i, e, tg, hb)] = s_dve.mark
                            on("vector", dve_g1)
                            cab_hist[0], cab_hist[1] = cab_hist[1], mk[("ffn_cab", wv_i, e, tg, hb)]

                            def act_sg(s, wv_i=wv_i, e=e, tg=tg, hb=hb):
                                Wact(s, mk[("ffn_cab", wv_i, e, tg, hb)])
                                s_act.inc(s.activation(sg_t[:], ca_t[:], AF.Silu))
                                mk[("ffn_sg", wv_i, e, tg, hb)] = s_act.mark
                            on("scalar", act_sg)

                            def dve_g2(v, wv_i=wv_i, e=e, tg=tg, hb=hb):
                                Wdve(v, mk[("ffn_sg", wv_i, e, tg, hb)])
                                if tg >= 1:
                                    Wdve(v, mk[("ffn_w2mm", wv_i, e, 15)])
                                inst = v.tensor_mul(gat[:, hb, :], sg_t[:], cb_t[:])
                                s_dve.inc(inst)
                                mk[("ffn_gat", wv_i, e, tg, hb)] = s_dve.mark
                            on("vector", dve_g2)

                        for ts4 in range(4):
                            tci = tg * 4 + ts4
                            for dg in range(4):
                                gi = tci * 4 + dg

                                def pe_w2(t, wv_i=wv_i, e=e, tg=tg, ts4=ts4, gi=gi, dg=dg):
                                    Wpe(t, mk[("ffn_w2", wv_i, e)])
                                    Wpe(t, mk[("ffn_gat", wv_i, e, tg, 7)])
                                    if gi >= 2:
                                        Wpe(t, mk[("ffn_acc", wv_i, e, gi - 2)])
                                    p = pw2[gi % 2]
                                    for hc in range(8):
                                        inst = t.matmul(p[:], gat[:, hc, ts4 * 128:(ts4 + 1) * 128],
                                                        sw2[:, hc, dg * 512:(dg + 1) * 512],
                                                        start=(hc == 0), stop=(hc == 7))
                                    s_pe.inc(inst)
                                    mk[("ffn_w2mm", wv_i, e, gi)] = s_pe.mark
                                on("tensor", pe_w2)

                                def dve_acc(v, wv_i=wv_i, ei=ei, e=e, tci=tci, gi=gi, dg=dg):
                                    Wdve(v, mk[("ffn_w2mm", wv_i, e, gi)])
                                    p = pw2[gi % 2]
                                    dst = acc[:, tci, dg * 512:(dg + 1) * 512]
                                    if ei == 0:
                                        inst = v.tensor_copy(dst, p[:])
                                    else:
                                        inst = v.tensor_add(dst, dst, p[:])
                                    s_dve.inc(inst)
                                    mk[("ffn_acc", wv_i, e, gi)] = s_dve.mark
                                on("vector", dve_acc)

                    prev_ab[0] = mk[("ffn_ab", wv_i, e, 1, 7)]
                    prev_w2[0] = mk[("ffn_w2mm", wv_i, e, 31)]
                    prev_cab[0] = mk[("ffn_cab", wv_i, e, 1, 7)]

                for tci in range(8):
                    tc = wv_i * 8 + tci

                    def sp_xf(sp, tc=tc):
                        if tc >= 1:
                            Wsp(sp, mk[("fin", tc - 1)])
                        sxf.inc(sp.dma_start(out=xfin[:],
                                             in_=x1_d[tc * 128:(tc + 1) * 128, :]), 16)
                        mk[("fin_ld", tc)] = sxf.mark
                    on("sync", sp_xf)

                    def dve_fin(v, tc=tc, tci=tci, wv_i=wv_i):
                        Wdve(v, mk[("fin_ld", tc)])
                        if tc >= 1:
                            Wdve(v, mk[("fin_st", tc - 1)])
                        inst = v.tensor_add(ofin[:], xfin[:], acc[:, tci, :])
                        s_dve.inc(inst)
                        mk[("fin", tc)] = s_dve.mark
                    on("vector", dve_fin)

                    def sp_of(sp, tc=tc):
                        Wsp(sp, mk[("fin", tc)])
                        sout.inc(sp.dma_start(out=out_ap[tc * 128:(tc + 1) * 128, :],
                                              in_=ofin[:]), 16)
                        mk[("fin_st", tc)] = sout.mark
                    on("sync", sp_of)

            def sp_end(sp):
                for s in ALL_SEMS[3:]:
                    Wsp(sp, s.mark)
            on("sync", sp_end)

    return nc


def _host_inputs(inputs, core):
    b, hg = core // 2, core % 2
    f32 = np.float32
    ar = lambda v: np.ascontiguousarray(v)
    x = ar(np.asarray(inputs["L0_attn_hidden_states"][b], f32))
    return {
        "x": x,
        "x_own": ar(x[hg * TOWN:(hg + 1) * TOWN]),
        "wq": ar(np.asarray(inputs["L0_attn_Wq"], f32)[:, hg * 512:(hg + 1) * 512]),
        "wk": ar(np.asarray(inputs["L0_attn_Wk"], f32)[:, hg * 512:(hg + 1) * 512]),
        "ws": ar(np.asarray(inputs["L0_attn_Ws"], f32)[:, hg * 512:(hg + 1) * 512]),
        "wv": ar(np.asarray(inputs["L0_attn_Wv"], f32)[:, hg * 1024:(hg + 1) * 1024]),
        "wg": ar(np.asarray(inputs["L0_attn_Wg"], f32)[:, hg * 1024:(hg + 1) * 1024]),
        "wo": ar(np.asarray(inputs["L0_attn_Wo"], f32)),
        "gnw": ar(np.asarray(inputs["L0_attn_g_norm_weight"], f32).reshape(128, 1)),
        "gatewT": ar(np.asarray(inputs["L0_ffn_gate_w"], f32).T),
        "w1S": ar(np.asarray(inputs["L0_ffn_w1"], f32).transpose(0, 2, 1)
                  .reshape(NEXP, 16, 128, 8, 128).transpose(0, 2, 3, 1, 4)),
        "w3S": ar(np.asarray(inputs["L0_ffn_w3"], f32).transpose(0, 2, 1)
                  .reshape(NEXP, 16, 128, 8, 128).transpose(0, 2, 3, 1, 4)),
        "w2T": ar(np.asarray(inputs["L0_ffn_w2"], f32).transpose(0, 2, 1)),
        "ws1S": ar(np.asarray(inputs["L0_ffn_ws1"], f32).T
                   .reshape(16, 128, 8, 128).transpose(1, 2, 0, 3)),
        "ws3S": ar(np.asarray(inputs["L0_ffn_ws3"], f32).T
                   .reshape(16, 128, 8, 128).transpose(1, 2, 0, 3)),
        "ws2T": ar(np.asarray(inputs["L0_ffn_ws2"], f32).T),
        "mask_ut": np.triu(np.ones((128, 128), ml_dtypes.bfloat16)),
        "ident": np.eye(128, dtype=ml_dtypes.bfloat16),
        "ident32": np.eye(128, dtype=np.float32),
        "ones_col": np.ones((128, 1), ml_dtypes.bfloat16),
        "ones_row": np.ones((1, 128), np.float32),
        "constv": np.tile(np.array([[EPS, 1.0]], np.float32), (128, 1)),
        "hgv": np.array([[hg]], np.int32),
    }


def kernel(**inputs):
    nc = build_nc()
    in_maps = [_host_inputs(inputs, c) for c in range(8)]
    res = run_bass_kernel_spmd(nc, in_maps, core_ids=list(range(8)))
    out = np.empty((4, T, D), np.float32)
    for c in range(8):
        b, hg = c // 2, c % 2
        out[b, hg * TOWN:(hg + 1) * TOWN, :] = res.results[c]["out_half"]
    return out


if __name__ == "__main__":
    import time
    t0 = time.time()
    nc = build_nc()
    print("built", time.time() - t0, "n_instructions:", nc.n_instructions())
